# version 20
# baseline (speedup 1.0000x reference)
import contextlib
import numpy as np
import concourse.bass as bass
import concourse.mybir as mybir
from concourse.bass_utils import run_bass_kernel_spmd

F32 = mybir.dt.float32
BF16 = mybir.dt.bfloat16
AF = mybir.ActivationFunctionType
ALU = mybir.AluOpType
AX = mybir.AxisListType.X

T = 2048
D = 1024
KT = D // 128
NQ = 4
QC = T // NQ
MAGIC = float(np.float32(1.5 * 2 ** 23))
PI = float(np.pi)

DBG = None


class Sem:
    def __init__(self, nc, name):
        self.h = nc.semaphore(name).__enter__()
        self.n = 0

    def inc(self, inst, k=1):
        inst.then_inc(self.h, k)
        self.n += k
        return inst

    def wait(self, eng, target=None):
        eng.wait_ge(self.h, self.n if target is None else target)


class B:
    def __init__(self, nc, block):
        self.nc = nc
        self.block = block
        self.sT = Sem(nc, "sT")
        self.sV = Sem(nc, "sV")
        self.sS = Sem(nc, "sS")
        self.sG = Sem(nc, "sG")
        self.dW = Sem(nc, "dW")
        self.dX = Sem(nc, "dX")
        self.dPA = Sem(nc, "dPA")
        self.dPB = Sem(nc, "dPB")
        self.dPC = Sem(nc, "dPC")
        self.dPD = Sem(nc, "dPD")
        self.dZ = Sem(nc, "dZ")
        self.dYA = Sem(nc, "dYA")
        self.dYB = Sem(nc, "dYB")
        self.dYC = Sem(nc, "dYC")
        self.dYD = Sem(nc, "dYD")
        self.all_sems = [self.sT, self.sV, self.sS, self.sG, self.dW, self.dX,
                         self.dPA, self.dPB, self.dPC, self.dPD, self.dZ,
                         self.dYA, self.dYB, self.dYC, self.dYD]

    def t(self, f): self.block.tensor(f)
    def v(self, f): self.block.vector(f)
    def s(self, f): self.block.scalar(f)
    def g(self, f): self.block.gpsimd(f)
    def y(self, f): self.block.sync(f)

    def engsem(self, name):
        return {"t": self.sT, "v": self.sV, "s": self.sS, "g": self.sG}[name]

    def eng_emit(self, name, f):
        getattr(self, name)(f)

    def snap(self):
        return [(sm, sm.n) for sm in self.all_sems]

    def wait_all(self, eng, snap):
        for sm, n in snap:
            if n:
                sm.wait(eng, n)


def hbm_ap(w, offset, dims):
    return bass.AP(w.tensor if hasattr(w, "tensor") else w, offset, dims)


def build_program(dbg=None):
    dbg = list(dbg or [])
    nc = bass.Bass("TRN2", target_bir_lowering=False)
    x_in = nc.dram_tensor("x_in", [T, D], F32, kind="ExternalInput")
    eye_f = nc.dram_tensor("eye_f", [128, 128], F32, kind="ExternalInput")
    eye_b = nc.dram_tensor("eye_b", [128, 128], BF16, kind="ExternalInput")
    W = {}
    wspecs = {
        "L0_attn_Wz": [D, D], "L0_attn_bz": [D], "L0_attn_ln_g": [D], "L0_attn_ln_b": [D],
        "L0_attn_Wq": [D, D], "L0_attn_bq": [D], "L0_attn_Wr": [D, D], "L0_attn_br": [D],
        "L0_attn_Wh": [D, D], "L0_attn_bh": [D], "L0_attn_Wphi": [64, 64], "L0_attn_bphi": [64],
        "L0_attn_Wo": [D, D], "L0_attn_bo": [D],
        "L0_ffn_sw1": [D, 2048], "L0_ffn_sb1": [2048], "L0_ffn_sw3": [D, 2048], "L0_ffn_sb3": [2048],
        "L0_ffn_sw2": [2048, D], "L0_ffn_sb2": [D],
        "L0_ffn_w1": [8, D, D], "L0_ffn_b1": [8, D], "L0_ffn_w2": [8, D, D], "L0_ffn_b2": [8, D],
        "L1_attn_ln_g": [D], "L1_attn_ln_b": [D],
        "L1_ffn_rhs": [D, D],
    }
    for tag in ("m1", "m2"):
        wspecs.update({
            f"L1_attn_{tag}_Win": [D, 2048], f"L1_attn_{tag}_Wconv": [D, 4],
            f"L1_attn_{tag}_bconv": [D], f"L1_attn_{tag}_Wx": [D, 64],
            f"L1_attn_{tag}_Wdt": [32, D], f"L1_attn_{tag}_bdt": [D],
            f"L1_attn_{tag}_Alog": [D, 16], f"L1_attn_{tag}_Dp": [D],
            f"L1_attn_{tag}_Wout": [D, 512],
        })
    for name, shp in wspecs.items():
        W[name] = nc.dram_tensor(name, shp, F32, kind="ExternalInput")
    out_hbm = nc.dram_tensor("out", [T, D], F32, kind="ExternalOutput")
    WSC = [nc.dram_tensor(f"wmsc{i}", [2621440], BF16) for i in range(2)]
    FSC = nc.dram_tensor("wfsc", [23068672], BF16)
    FSC_S1, FSC_S3, FSC_S2 = 0, 2097152, 4194304
    FSC_W1, FSC_W2 = 6291456, 14680064
    dbg_hbm = {}

    est = contextlib.ExitStack()
    with est:
        X = est.enter_context(nc.sbuf_tensor("X", [128, KT * T], F32))
        EYEF = est.enter_context(nc.sbuf_tensor("EYEF", [128, 128], F32))
        EYEB = est.enter_context(nc.sbuf_tensor("EYEB", [128, 128], BF16))
        ONESF = est.enter_context(nc.sbuf_tensor("ONESF", [1, 128], F32))
        ONECOL = est.enter_context(nc.sbuf_tensor("ONECOL", [128, 1], BF16))
        CONST = est.enter_context(nc.sbuf_tensor("CONST", [128, 8], F32))
        C_EPS, C_MAGIC, C_HALFPI, C_TINY = (CONST[:, i:i + 1] for i in range(4))
        C_NEGPI = CONST[:, 5:6]
        SROW = est.enter_context(nc.sbuf_tensor("SROW", [1, 2 * T], F32))
        REP = est.enter_context(nc.sbuf_tensor("REP", [128, 2 * T], F32))
        ROW_R0 = REP[0:1, 0:T]
        ROW_R1 = REP[0:1, T:2 * T]
        ROW_A = SROW[0:1, 0:T]
        ROW_B = SROW[0:1, T:2 * T]

        with nc.Block() as block:
            b = B(nc, block)

            def tap(name, ap_list, total_elems, dtype=F32):
                if name not in dbg:
                    return
                h = nc.dram_tensor("dbg_" + name, [total_elems], dtype, kind="ExternalOutput")
                dbg_hbm[name] = h
                snap = b.snap()
                state = {"off": 0}

                def _tap(y):
                    b.wait_all(y, snap)
                    for ap in ap_list:
                        npart = ap.shape[0]
                        nfree = int(np.prod(ap.shape[1:]))
                        y.dma_start(out=hbm_ap(h, state["off"], [[nfree, npart], [1, nfree]]),
                                    in_=ap).then_inc(b.dX.h, 16)
                        b.dX.n += 16
                        state["off"] += npart * nfree
                b.y(_tap)

            def load(dst_ap, src_ap, slow=False, pre=None):
                def _(g):
                    if pre is not None:
                        b.wait_all(g, pre)
                    g.dma_start(out=dst_ap, in_=src_ap,
                                allow_slow_non_contiguous=slow).then_inc(b.dW.h, 16)
                    b.dW.n += 16
                b.g(_)
                return b.dW.n

            ysems = [b.dYA, b.dYB, b.dYC, b.dYD]
            yrot = [0]

            def ydma(g, out, in_):
                sm = ysems[yrot[0] % 4]
                yrot[0] += 1
                sm.wait(g, sm.n)
                g.dma_start(out=out, in_=in_).then_inc(sm.h, 16)
                sm.n += 16
                return (sm, sm.n)

            def wait_marks(eng, marks):
                for sm, n in marks:
                    sm.wait(eng, n)

            def wload_ap(name, Kdim, M, base_off=0):
                w = W[name]
                if Kdim >= 128:
                    return hbm_ap(w, base_off, [[M, 128], [128 * M, Kdim // 128], [1, M]])
                return hbm_ap(w, base_off, [[M, Kdim], [M * Kdim, 1], [1, M]])

            def col_ap(name, n, base_off=0):
                w = W[name]
                if n >= 128:
                    return hbm_ap(w, base_off, [[1, 128], [128, n // 128]])
                return hbm_ap(w, base_off, [[1, n], [n, 1]])

            def mm_group(t, psum_ap, pairs):
                n = len(pairs)
                for i, (st, mv) in enumerate(pairs):
                    inst = t.matmul(psum_ap, st, mv, start=(i == 0), stop=(i == n - 1))
                return inst

            def mm_evac(n, mm_fn, evac_fn, banks, evac_eng="s", evac_incs=1,
                        pre_engines=("v", "s"), pre_marks=None):
                es = b.engsem(evac_eng)
                nb = len(banks)
                tb, eb = b.sT.n, es.n
                pre = (pre_marks if pre_marks is not None
                       else [(b.engsem(e), b.engsem(e).n) for e in pre_engines])

                def _t(t):
                    for i in range(n):
                        if i < nb:
                            b.wait_all(t, pre)
                        else:
                            es.wait(t, eb + (i - nb + 1) * evac_incs)
                        b.sT.inc(mm_fn(t, i, banks[i % nb]))

                def _e(eng):
                    for i in range(n):
                        b.sT.wait(eng, tb + i + 1)
                        es.inc(evac_fn(eng, i, banks[i % nb]))
                b.t(_t)
                b.eng_emit(evac_eng, _e)
                return tb, eb

            def Xt(dt):
                return X[:, dt * T:(dt + 1) * T]

            def _c_v(v):
                v.memset(ONESF[:], 1.0)
                v.memset(ONECOL[:], 1.0)
                v.memset(CONST[:, 0:1], 1e-5)
                v.memset(CONST[:, 1:2], MAGIC)
                v.memset(CONST[:, 2:3], PI / 2)
                v.memset(CONST[:, 5:6], -PI)
                b.sV.inc(v.memset(CONST[:, 3:4], 1e-12))
            b.v(_c_v)

            p0 = contextlib.ExitStack()
            XTMF = p0.enter_context(nc.sbuf_tensor("XTMF", [128, 8 * D], F32))
            PT = [p0.enter_context(nc.psum_tensor(f"PT{i}", [128, 128], F32)) for i in range(2)]
            tb0, sb0 = b.sT.n, b.sS.n

            def _p0_y(y):
                y.dma_start(out=EYEF[:], in_=eye_f[:]).then_inc(b.dX.h, 16)
                y.dma_start(out=EYEB[:], in_=eye_b[:]).then_inc(b.dX.h, 16)
                b.dX.n += 32
                for half in range(2):
                    if half == 1:
                        b.sT.wait(y, tb0 + 64)
                    y.dma_start(out=XTMF[:],
                                in_=hbm_ap(x_in, half * 8 * 128 * D,
                                           [[D, 128], [128 * D, 8], [1, D]])).then_inc(b.dX.h, 16)
                    b.dX.n += 16
            b.y(_p0_y)

            def _p0_t(t):
                for half in range(2):
                    b.dX.wait(t, 32 + (half + 1) * 16)
                    for i in range(64):
                        tt, dt = divmod(i, 8)
                        tt += half * 8
                        gi = half * 64 + i
                        if gi >= 2:
                            b.sS.wait(t, sb0 + gi - 1)
                        b.sT.inc(t.transpose(PT[gi % 2][:],
                                             XTMF[:, (tt % 8) * D + dt * 128: (tt % 8) * D + (dt + 1) * 128],
                                             EYEF[:]))
            b.t(_p0_t)

            def _p0_s(s):
                for gi in range(128):
                    tt, dt = divmod(gi, 8)
                    b.sT.wait(s, tb0 + gi + 1)
                    b.sS.inc(s.copy(X[:, dt * T + tt * 128: dt * T + (tt + 1) * 128], PT[gi % 2][:]))
            b.s(_p0_s)
            p0.close()

            def ln_stats_from_chunks(chunk_fn, psA, psB, stb):
                sqm, xbm, evm = {}, {}, {}
                tb = b.sT.n
                lsnap = b.snap()

                def _s(s):
                    b.wait_all(s, lsnap)
                    for q in range(NQ):
                        for dt in range(KT):
                            j = q * KT + dt
                            src, waits = chunk_fn(j)
                            for sm, cnt in waits:
                                sm.wait(s, cnt)
                            if j >= 2:
                                b.sT.wait(s, tb + 2 * j - 2)
                            b.sS.inc(s.activation(stb[2 * (j % 2)][:], src, AF.Square))
                            sqm[j] = b.sS.n
                        b.sT.wait(s, tb + 16 * (q + 1))
                        b.sS.inc(s.copy(ROW_R0[0:1, q * QC:(q + 1) * QC], psA[0:1, :]))
                        b.sS.inc(s.copy(ROW_R1[0:1, q * QC:(q + 1) * QC], psB[0:1, :]))
                        evm[q] = b.sS.n
                b.s(_s)

                def _v(v):
                    b.wait_all(v, lsnap)
                    for q in range(NQ):
                        for dt in range(KT):
                            j = q * KT + dt
                            src, waits = chunk_fn(j)
                            for sm, cnt in waits:
                                sm.wait(v, cnt)
                            if j >= 2:
                                b.sT.wait(v, tb + 2 * j - 2)
                            b.sV.inc(v.tensor_copy(stb[2 * (j % 2) + 1][:], src))
                            xbm[j] = b.sV.n
                b.v(_v)

                def _t(t):
                    for q in range(NQ):
                        if q >= 1:
                            b.sS.wait(t, evm[q - 1])
                        for dt in range(KT):
                            j = q * KT + dt
                            b.sS.wait(t, sqm[j])
                            b.sV.wait(t, xbm[j])
                            b.sT.inc(t.matmul(psA[0:1, :], ONECOL[:], stb[2 * (j % 2) + 1][:],
                                              start=(dt == 0), stop=(dt == KT - 1)))
                            b.sT.inc(t.matmul(psB[0:1, :], ONECOL[:], stb[2 * (j % 2)][:],
                                              start=(dt == 0), stop=(dt == KT - 1)))
                b.t(_t)

            def ln_statmath_and_reps(psR, denom=D):
                def _sm_s(s):
                    b.sS.wait(s)
                    b.sS.inc(s.activation(ROW_A, ROW_R0, AF.Identity, scale=1.0 / denom))
                    b.sS.wait(s)
                    b.sS.inc(s.activation(ROW_B, ROW_A, AF.Square))
                b.s(_sm_s)

                def _sm_v(v):
                    b.sS.wait(v)
                    b.sV.inc(v.tensor_scalar(ROW_R1, ROW_R1, 1.0 / denom, None, ALU.mult))
                    b.sV.wait(v)
                    b.sV.inc(v.tensor_sub(ROW_B, ROW_R1, ROW_B))
                b.v(_sm_v)

                def _sm_s2(s):
                    b.sV.wait(s)
                    b.sS.inc(s.activation(ROW_R1, ROW_B, AF.Sqrt, bias=C_EPS[0:1, :]))
                b.s(_sm_s2)

                def _sm_v2(v):
                    b.sS.wait(v)
                    b.sV.inc(v.reciprocal(ROW_R0, ROW_R1))
                    b.sV.wait(v)
                    b.sV.inc(v.scalar_tensor_tensor(ROW_R1, ROW_A, -1.0, ROW_R0,
                                                    ALU.mult, ALU.mult))
                b.v(_sm_v2)

                def rep_mm(t, i, bank):
                    if i == 0:
                        b.sV.wait(t)
                    k, q = divmod(i, NQ)
                    row = ROW_R0 if k == 0 else ROW_R1
                    return mm_group(t, bank[:, :],
                                    [(ONESF[:], row[0:1, q * QC:(q + 1) * QC])])

                def rep_ev(s, i, bank):
                    k, q = divmod(i, NQ)
                    return s.copy(REP[:, k * T + q * QC: k * T + (q + 1) * QC], bank[:, :])
                mm_evac(2 * NQ, rep_mm, rep_ev, [psR], "s")

            def ln_stats_resident(src_tile, psA, psB, stb):
                def chunk_fn(j):
                    q, dt = divmod(j, KT)
                    return src_tile(dt)[:, q * QC:(q + 1) * QC], []
                ln_stats_from_chunks(chunk_fn, psA, psB, stb)

            def ln_normalize(src_tile, dst_tile, scratch, gcol=None, bcol=None):
                snap = b.snap()
                vb = b.sV.n
                afb = b.sS.n

                def _v(v):
                    b.wait_all(v, snap)
                    for dt in range(KT):
                        if gcol is None:
                            if dt >= 1:
                                b.sV.wait(v)
                            b.sV.inc(v.tensor_mul(scratch[:], src_tile(dt), REP[:, 0:T]))
                            b.sV.wait(v)
                            b.sV.inc(v.tensor_add(dst_tile(dt), scratch[:], REP[:, T:2 * T]))
                        else:
                            if dt >= 1:
                                b.sS.wait(v, afb + dt)
                            b.sV.inc(v.tensor_mul(scratch[:], src_tile(dt), REP[:, 0:T]))
                            b.sV.wait(v)
                            b.sV.inc(v.tensor_add(scratch[:], scratch[:], REP[:, T:2 * T]))
                b.v(_v)
                if gcol is not None:
                    def _s(s):
                        for dt in range(KT):
                            b.sV.wait(s, vb + 2 * (dt + 1))
                            b.sS.inc(s.activation(dst_tile(dt), scratch[:], AF.Identity,
                                                  bias=bcol(dt), scale=gcol(dt)))
                    b.s(_s)

            p1 = contextlib.ExitStack()
            PSU = [p1.enter_context(nc.psum_tensor(f"PSU{i}", [128, 512], F32)) for i in range(4)]
            PSBK = [p1.enter_context(nc.psum_tensor(f"PSBK{i}", [128, 1024], BF16)) for i in range(2)]
            BIAS = p1.enter_context(nc.sbuf_tensor("BIAS", [128, 32], F32))
            BIAS2 = p1.enter_context(nc.sbuf_tensor("BIAS2", [128, 26], F32))
            WPHI2 = p1.enter_context(nc.sbuf_tensor("WPHI2", [128, 128], BF16))
            ZLN = p1.enter_context(nc.sbuf_tensor("ZLN", [128, KT * T], BF16))

            def ZLNt(dt):
                return ZLN[:, dt * T:(dt + 1) * T]

            load(BIAS[:, 0:8], col_ap("L0_attn_bz", D), slow=True, pre=b.snap())
            load(BIAS[:, 8:16], col_ap("L0_attn_bo", D), slow=True)
            load(BIAS[:, 16:24], col_ap("L0_attn_ln_g", D), slow=True)
            load(BIAS[:, 24:32], col_ap("L0_attn_ln_b", D), slow=True)
            load(BIAS2[:, 0:8], hbm_ap(W["L0_attn_bq"], 0, [[1, 128], [128, 8]]), slow=True)
            load(BIAS2[:, 8:16], hbm_ap(W["L0_attn_br"], 0, [[1, 128], [128, 8]]), slow=True)
            load(BIAS2[:, 16:24], hbm_ap(W["L0_attn_bh"], 0, [[1, 128], [128, 8]]), slow=True)
            load(BIAS2[0:64, 24:25], hbm_ap(W["L0_attn_bphi"], 0, [[1, 64], [64, 1]]), slow=True)
            load(BIAS2[64:128, 24:25], hbm_ap(W["L0_attn_bphi"], 0, [[1, 64], [64, 1]]), slow=True)

            def _wp0_v(v):
                b.sT.wait(v)
                b.sV.inc(v.memset(WPHI2[:], 0.0))
            b.v(_wp0_v)
            load(WPHI2[0:64, 0:64],
                 hbm_ap(W["L0_attn_Wphi"], 0, [[64, 64], [64 * 64, 1], [1, 64]]), pre=b.snap())
            load(WPHI2[64:128, 64:128],
                 hbm_ap(W["L0_attn_Wphi"], 0, [[64, 64], [64 * 64, 1], [1, 64]]))
            wld_b2 = b.dW.n

            pz = contextlib.ExitStack()
            WZ = pz.enter_context(nc.sbuf_tensor("WZ", [128, KT * 1024], BF16))
            XLN = pz.enter_context(nc.sbuf_tensor("XLN", [128, KT * T], BF16))
            SCR = pz.enter_context(nc.sbuf_tensor("SCR", [128, T], F32))
            STB = [pz.enter_context(nc.sbuf_tensor(f"STB{i}", [128, QC], BF16)) for i in range(4)]
            ZCH = [pz.enter_context(nc.sbuf_tensor(f"ZCH{i}", [128, QC], F32)) for i in range(2)]

            def XLNt(dt):
                return XLN[:, dt * T:(dt + 1) * T]

            load(WZ[:], wload_ap("L0_attn_Wz", D, 1024))
            wld_wz = b.dW.n

            def _bphi_s(s):
                b.dW.wait(s, wld_wz)
                b.sS.inc(s.mul(BIAS2[:, 24:25], BIAS2[:, 24:25], 1.0 / (2 * PI)))
            b.s(_bphi_s)

            ln_stats_resident(Xt, PSU[0], PSU[1], STB)
            ln_statmath_and_reps(PSU[2])
            ln_normalize(Xt, XLNt, SCR)
            tap("xln1", [XLNt(dt) for dt in range(KT)], KT * 128 * T, BF16)

            NJ = NQ * KT
            zsnap = b.snap()
            z_tb, z_sb, z_vb = b.sT.n, b.sS.n, b.sV.n
            zmark, statmark = {}, {}

            def s_zevac_cnt(j):
                return z_sb + 18 * (j // KT) + 2 * (j % KT) + 1

            def s_sq_cnt(j):
                return z_sb + 18 * (j // KT) + 2 * (j % KT) + 2

            def s_q_cnt(q):
                return z_sb + 18 * (q + 1)

            def _za_t(t):
                b.wait_all(t, zsnap)
                b.dW.wait(t, wld_wz)

                def statmm(j):
                    q, et = divmod(j, KT)
                    if et == 0 and q >= 1:
                        b.sS.wait(t, s_q_cnt(q - 1))
                    b.sS.wait(t, s_sq_cnt(j))
                    b.sV.wait(t, z_vb + j + 1)
                    qj, ej = divmod(j, KT)
                    b.sT.inc(t.matmul(PSU[0][0:1, :], ONECOL[:],
                                      ZLN[:, ej * T + qj * QC: ej * T + (qj + 1) * QC],
                                      start=(et == 0), stop=(et == KT - 1)))
                    b.sT.inc(t.matmul(PSU[1][0:1, :], ONECOL[:], STB[2 * (j % 2)][:],
                                      start=(et == 0), stop=(et == KT - 1)))
                    statmark[j] = b.sT.n

                for j in range(NJ):
                    q, et = divmod(j, KT)
                    if j >= 2:
                        b.sS.wait(t, s_zevac_cnt(j - 2))
                    b.sT.inc(mm_group(t, PSU[2 + j % 2][:, :],
                                      [(WZ[:, kt * 1024 + et * 128: kt * 1024 + (et + 1) * 128],
                                        XLNt(kt)[:, q * QC:(q + 1) * QC]) for kt in range(KT)]))
                    zmark[j] = b.sT.n
                    if j >= 1:
                        statmm(j - 1)
                statmm(NJ - 1)
            b.t(_za_t)

            def _za_s(s):
                for j in range(NJ):
                    q, et = divmod(j, KT)
                    b.sT.wait(s, zmark[j])
                    if j >= 2:
                        b.sV.wait(s, z_vb + j - 1)
                    b.sS.inc(s.activation(ZCH[j % 2][:], PSU[2 + j % 2][:, :], AF.Identity,
                                          bias=BIAS[:, et:et + 1]))
                    b.sS.wait(s)
                    b.sS.inc(s.activation(STB[2 * (j % 2)][:], ZCH[j % 2][:], AF.Square))
                    if et == KT - 1:
                        b.sT.wait(s, statmark[j])
                        b.sS.inc(s.copy(ROW_R0[0:1, q * QC:(q + 1) * QC], PSU[0][0:1, :]))
                        b.sS.inc(s.copy(ROW_R1[0:1, q * QC:(q + 1) * QC], PSU[1][0:1, :]))
            b.s(_za_s)

            def _za_v(v):
                for j in range(NJ):
                    q, et = divmod(j, KT)
                    b.sS.wait(v, s_zevac_cnt(j))
                    b.sV.inc(v.tensor_copy(ZLNt(et)[:, q * QC:(q + 1) * QC], ZCH[j % 2][:]))
            b.v(_za_v)

            ln_statmath_and_reps(PSU[2], denom=D)

            z2sb, z2vb = b.sS.n, b.sV.n
            zb_vm = {j: z2vb + 2 * (j + 1) for j in range(NJ)}
            zb_sm = {j: z2sb + j + 1 for j in range(NJ)}

            def _zb_v(v):
                for j in range(NJ):
                    q, et = divmod(j, KT)
                    if j == 0:
                        b.sT.wait(v)
                        b.sS.wait(v)
                    if j >= 2:
                        b.sS.wait(v, zb_sm[j - 2])
                    b.sV.inc(v.tensor_mul(ZCH[j % 2][:],
                                          ZLNt(et)[:, q * QC:(q + 1) * QC],
                                          REP[:, q * QC:(q + 1) * QC]))
                    b.sV.wait(v)
                    b.sV.inc(v.tensor_add(ZCH[j % 2][:], ZCH[j % 2][:],
                                          REP[:, T + q * QC: T + (q + 1) * QC]))
            b.v(_zb_v)

            def _zb_s(s):
                for j in range(NJ):
                    q, et = divmod(j, KT)
                    b.sV.wait(s, zb_vm[j])
                    b.sS.inc(s.activation(ZLNt(et)[:, q * QC:(q + 1) * QC], ZCH[j % 2][:],
                                          AF.Identity, bias=BIAS[:, 24 + et:25 + et],
                                          scale=BIAS[:, 16 + et:17 + et]))
            b.s(_zb_s)
            tap("zln", [ZLNt(dt) for dt in range(KT)], KT * 128 * T, BF16)

            pz.close()

            g_atto = nc.sbuf_tensor("ATTO", [128, KT * T], BF16)
            ATTO = g_atto.__enter__()

            def ATTOt(dt):
                return ATTO[:, dt * T:(dt + 1) * T]

            NH, HD = 16, 64
            ph = contextlib.ExitStack()
            WHD = ph.enter_context(nc.sbuf_tensor("WHD", [128, 3 * KT * 128], BF16))
            QRH = ph.enter_context(nc.sbuf_tensor("QRH", [128, T], BF16))
            HV = ph.enter_context(nc.sbuf_tensor("HV", [128, 16 * 128], BF16))
            HC = T // 2
            PH = ph.enter_context(nc.sbuf_tensor("PH", [128, HC], F32))
            SSIN = ph.enter_context(nc.sbuf_tensor("SSIN", [128, HC], BF16))
            SCOS = ph.enter_context(nc.sbuf_tensor("SCOS", [128, HC], BF16))
            QCB = [ph.enter_context(nc.sbuf_tensor(f"QCB{i}", [128, T], BF16)) for i in range(2)]
            RCB = [ph.enter_context(nc.sbuf_tensor(f"RCB{i}", [128, T], BF16)) for i in range(2)]
            RCTM = ph.enter_context(nc.sbuf_tensor("RCTM", [128, 16 * 128], BF16))
            RSC = RCTM[:, :].bitcast(F32)
            GG = ph.enter_context(nc.sbuf_tensor("GG", [128, 128], BF16))
            SCBW = [ph.enter_context(nc.sbuf_tensor(f"SCBW{i}", [128, QC], BF16)) for i in range(2)]
            QG = REP[:, 0:T]
            RNREP = REP[:, T:2 * T]

            wsc_jobs = []
            for ht in range(16):
                wsc_jobs.append((
                    hbm_ap(FSC, FSC_S1 + ht * 131072, [[1024, 128], [128, 8], [1, 128]]),
                    hbm_ap(W["L0_ffn_sw1"], ht * 128, [[2048, 128], [128 * 2048, 8], [1, 128]])))
                wsc_jobs.append((
                    hbm_ap(FSC, FSC_S3 + ht * 131072, [[1024, 128], [128, 8], [1, 128]]),
                    hbm_ap(W["L0_ffn_sw3"], ht * 128, [[2048, 128], [128 * 2048, 8], [1, 128]])))
            for co in range(KT):
                wsc_jobs.append((
                    hbm_ap(FSC, FSC_S2 + co * 262144, [[2048, 128], [128, 16], [1, 128]]),
                    hbm_ap(W["L0_ffn_sw2"], co * 128, [[1024, 128], [128 * 1024, 16], [1, 128]])))
            for e in range(8):
                wsc_jobs.append((
                    hbm_ap(FSC, FSC_W1 + e * 1048576, [[8192, 128], [1024, 8], [1, 1024]]),
                    hbm_ap(W["L0_ffn_w1"], e * D * D, [[1024, 128], [131072, 8], [1, 1024]])))
                wsc_jobs.append((
                    hbm_ap(FSC, FSC_W2 + e * 1048576, [[8192, 128], [1024, 8], [1, 1024]]),
                    hbm_ap(W["L0_ffn_w2"], e * D * D, [[1024, 128], [131072, 8], [1, 1024]])))
            wsc_ffn_jobs = len(wsc_jobs)
            for dri, mtg in ((0, "m1"), (1, "m2")):
                for mdt in range(KT):
                    wsc_jobs.append((
                        hbm_ap(WSC[dri], mdt * 131072, [[1024, 128], [128, 8], [1, 128]]),
                        hbm_ap(W[f"L1_attn_{mtg}_Win"], mdt * 128,
                               [[2048, 128], [128 * 2048, 8], [1, 128]])))
                    wsc_jobs.append((
                        hbm_ap(WSC[dri], 1048576 + mdt * 131072,
                               [[1024, 128], [128, 8], [1, 128]]),
                        hbm_ap(W[f"L1_attn_{mtg}_Win"], 1024 + mdt * 128,
                               [[2048, 128], [128 * 2048, 8], [1, 128]])))
                wsc_jobs.append((
                    hbm_ap(WSC[dri], 2097152, [[512, 1024], [1, 512]]),
                    hbm_ap(W[f"L1_attn_{mtg}_Wout"], 0, [[512, 1024], [1, 512]])))

            wh_marks, hm_marks = {}, {}
            pair_free = {}
            head_snap = b.snap()
            for hp in range(8):
                bq_c = BIAS2[:, 0 + hp:1 + hp]
                br_c = BIAS2[:, 8 + hp:9 + hp]
                bh_c = BIAS2[:, 16 + hp:17 + hp]
                hsem = b.dPA if hp % 2 == 0 else b.dPB

                def _wg(g, hp=hp, hsem=hsem):
                    if hp == 0:
                        b.wait_all(g, head_snap)
                    else:
                        b.sT.wait(g, hm_marks[hp - 1])
                    for wi, nm in enumerate(("L0_attn_Wq", "L0_attn_Wr", "L0_attn_Wh")):
                        g.dma_start(out=WHD[:, wi * KT * 128:(wi + 1) * KT * 128],
                                    in_=hbm_ap(W[nm], hp * 128,
                                               [[D, 128], [128 * D, KT], [1, 128]])).then_inc(hsem.h, 16)
                        hsem.n += 16
                    wh_marks[hp] = (hsem, hsem.n)
                b.g(_wg)

                lo, hi = hp * 16, min((hp + 1) * 16, len(wsc_jobs))
                if lo < hi:
                    def _wsc_g(g, lo=lo, hi=hi):
                        for oa, ia in wsc_jobs[lo:hi]:
                            b.dZ.inc(g.dma_start(out=oa, in_=ia), 16)
                    b.g(_wsc_g)

                def stat_w(wi, kt):
                    return WHD[:, wi * KT * 128 + kt * 128: wi * KT * 128 + (kt + 1) * 128]

                for wi in range(2):
                    bias_c = bq_c if wi == 0 else br_c
                    wm = wh_marks[hp]

                    for half in range(2):
                        hs = slice(half * HC, (half + 1) * HC)

                        def p_mm(t, i, bank, wi=wi, wm=wm, half=half):
                            wm[0].wait(t, wm[1])
                            j = half * 2 + i
                            return mm_group(t, bank[:, :],
                                            [(stat_w(wi, kt), ZLNt(kt)[:, j * QC:(j + 1) * QC])
                                             for kt in range(KT)])

                        def p_ev(s, i, bank, bias_c=bias_c, half=half):
                            j = half * 2 + i
                            return s.activation(QRH[:, j * QC:(j + 1) * QC], bank[:, :],
                                                AF.Identity, bias=bias_c)
                        if wi == 0 and half == 0:
                            mm_evac(2, p_mm, p_ev, [PSU[0], PSU[1]], "s",
                                    pre_marks=pair_free.get(hp - 1))
                        elif wi == 1 and half == 0:
                            mm_evac(2, p_mm, p_ev, [PSU[0], PSU[1]], "s")
                        else:
                            mm_evac(2, p_mm, p_ev, [PSU[0], PSU[1]], "s",
                                    pre_engines=("s",))

                        def phi_mm(t, i, bank, half=half):
                            return mm_group(t, bank[:, :],
                                            [(WPHI2[:],
                                              QRH[:, half * HC + i * QC: half * HC + (i + 1) * QC])])

                        def phi_ev(s, i, bank):
                            return s.activation(PH[:, i * QC:(i + 1) * QC], bank[:, :],
                                                AF.Identity, scale=1.0 / (2 * PI),
                                                bias=BIAS2[:, 24:25])
                        if half == 0:
                            mm_evac(2, phi_mm, phi_ev, [PSU[0], PSU[1]], "s",
                                    pre_engines=("s",))
                        else:
                            mm_evac(2, phi_mm, phi_ev, [PSU[0], PSU[1]], "s")

                        vb0 = b.sV.n
                        sb_t = b.sS.n

                        def _trig_v(v, sb_t=sb_t):
                            b.sS.wait(v)
                            b.sV.inc(v.tensor_scalar(RSC, PH[:], C_MAGIC,
                                                     C_MAGIC, ALU.add, ALU.subtract))
                            b.sV.wait(v)
                            b.sV.inc(v.scalar_tensor_tensor(RSC, RSC, -1.0, PH[:],
                                                            ALU.mult, ALU.add))
                            b.sS.wait(v, sb_t + 1)
                            b.sV.inc(v.tensor_scalar(RSC, PH[:], 0.25, C_MAGIC,
                                                     ALU.add, ALU.add))
                            b.sV.wait(v)
                            b.sV.inc(v.tensor_scalar(RSC, RSC, C_MAGIC, None,
                                                     ALU.subtract))
                            b.sV.wait(v)
                            b.sV.inc(v.scalar_tensor_tensor(RSC, RSC, -1.0, PH[:],
                                                            ALU.mult, ALU.add))

                        def _trig_s(s, vb0=vb0):
                            b.sV.wait(s, vb0 + 2)
                            b.sS.inc(s.activation(SSIN[:], RSC, AF.Sin, scale=2 * PI))
                            b.sV.wait(s, vb0 + 5)
                            b.sS.inc(s.activation(SCOS[:], RSC, AF.Sin, scale=2 * PI,
                                                  bias=C_HALFPI))
                        b.v(_trig_v)
                        b.s(_trig_s)

                        def _qc_v(v, wi=wi, hs=hs):
                            dst = QCB if wi == 0 else RCB
                            b.sS.wait(v)
                            for hh in range(2):
                                qsrc = QRH[64 * hh:64 * hh + 64, hs]
                                b.sV.inc(v.tensor_mul(dst[hh][0:64, hs], qsrc,
                                                      SCOS[64 * hh:64 * hh + 64, :]))
                                b.sV.inc(v.tensor_mul(dst[hh][64:128, hs], qsrc,
                                                      SSIN[64 * hh:64 * hh + 64, :]))
                        b.v(_qc_v)

                wm = wh_marks[hp]

                def h_mm(t, i, bank, wm=wm):
                    wm[0].wait(t, wm[1])
                    if i == 0:
                        b.sV.wait(t)
                    return mm_group(t, bank[:, :],
                                    [(stat_w(2, kt), ZLNt(kt)[:, i * QC:(i + 1) * QC])
                                     for kt in range(KT)])

                def h_ev(s, i, bank, bh_c=bh_c):
                    return s.activation(QRH[:, i * QC:(i + 1) * QC], bank[:, :],
                                        AF.Identity, bias=bh_c)
                mm_evac(NQ, h_mm, h_ev, [PSU[0], PSU[1]], "s")
                hm_marks[hp] = b.sT.n

                def htp_mm(t, i, bank):
                    if i < 2:
                        b.sS.wait(t)
                    return t.transpose(bank[:, 0:128], QRH[:, i * 128:(i + 1) * 128],
                                       EYEB[:])

                def htp_ev(v, i, bank):
                    return v.tensor_copy(HV[:, i * 128:(i + 1) * 128], bank[:, 0:128])
                mm_evac(16, htp_mm, htp_ev, [PSBK[0], PSBK[1]], "v")

                for hh in range(2):
                    qcb, rcb = QCB[hh], RCB[hh]

                    def rtp_mm(t, i, bank, rcb=rcb):
                        return t.transpose(bank[:, 0:128], rcb[:, i * 128:(i + 1) * 128], EYEB[:])

                    def rtp_ev(v, i, bank):
                        return v.tensor_copy(RCTM[:, i * 128:(i + 1) * 128], bank[:, 0:128])
                    mm_evac(16, rtp_mm, rtp_ev, [PSBK[0], PSBK[1]], "v")

                    def g_mm(t, i, bank):
                        return mm_group(t, bank[:, 0:128],
                                        [(RCTM[:, kt * 128:(kt + 1) * 128],
                                          RCTM[:, kt * 128:(kt + 1) * 128]) for kt in range(16)])

                    def g_ev(v, i, bank):
                        return v.tensor_copy(GG[:], bank[:, 0:128])
                    mm_evac(1, g_mm, g_ev, [PSU[0]], "v")

                    def qg_mm(t, i, bank, qcb=qcb):
                        if i == 0:
                            b.sV.wait(t)
                        return mm_group(t, bank[:, :], [(GG[:], qcb[:, i * QC:(i + 1) * QC])])

                    def qg_ev(v, i, bank):
                        return v.tensor_copy(QG[:, i * QC:(i + 1) * QC], bank[:, :])
                    mm_evac(NQ, qg_mm, qg_ev, [PSU[0], PSU[1]], "v")

                    PBB = RCTM[:, 0:T]

                    def _pb_v(v, qcb=qcb):
                        b.sV.wait(v)
                        b.sT.wait(v)
                        b.sV.inc(v.tensor_mul(PBB, QG[:, :], qcb[:]))
                    b.v(_pb_v)

                    def ns_mm(t, i, bank):
                        if i == 0:
                            b.sV.wait(t)
                        return mm_group(t, bank[0:1, :],
                                        [(ONECOL[:], PBB[:, i * QC:(i + 1) * QC])])

                    def ns_ev(s, i, bank):
                        return s.copy(ROW_A[0:1, i * QC:(i + 1) * QC], bank[0:1, :])
                    mm_evac(NQ, ns_mm, ns_ev, [PSU[0], PSU[1]], "s")

                    def rn_mm(t, i, bank):
                        return mm_group(t, bank[:, :],
                                        [(ONESF[:], ROW_A[0:1, i * QC:(i + 1) * QC])])

                    def rn_ev(s, i, bank):
                        return s.copy(RNREP[:, i * QC:(i + 1) * QC], bank[:, :])
                    mm_evac(NQ, rn_mm, rn_ev, [PSU[0], PSU[1]], "s")

                    prev_sc = None
                    for qc in range(NQ):
                        scm, avm, evm2 = {}, {}, {}
                        vcur, scur = b.sV.n, b.sS.n
                        kt2_v, kt2_s = prev_sc if prev_sc else (vcur, scur)
                        for kt in range(16):
                            if kt % 2 == 0:
                                evm2[kt] = (b.sV, vcur + kt // 2 + 1)
                            else:
                                evm2[kt] = (b.sS, scur + kt // 2 + 1)

                        def _sc_t(t, qc=qc, scm=scm, avm=avm, evm2=evm2, vcur=vcur,
                                  kt2_v=kt2_v, kt2_s=kt2_s, qcb=qcb, rcb=rcb, hh=hh):
                            for kt in range(16):
                                if kt < 3:
                                    b.sV.wait(t, kt2_v)
                                    b.sS.wait(t, kt2_s)
                                else:
                                    sm, cnt = evm2[kt - 3]
                                    sm.wait(t, cnt)
                                b.sT.inc(t.matmul((PSU[0], PSU[1], PSU[3])[kt % 3][:, :],
                                                  rcb[:, kt * 128:(kt + 1) * 128],
                                                  qcb[:, qc * QC:(qc + 1) * QC],
                                                  start=True, stop=True))
                                scm[kt] = b.sT.n
                                if kt >= 1:
                                    k2 = kt - 1
                                    if k2 == 0:
                                        b.sV.wait(t, vcur)
                                    sm, cnt = evm2[k2]
                                    sm.wait(t, cnt)
                                    b.sT.inc(t.matmul(
                                        PSU[2][0:64, :],
                                        HV[:, k2 * 128 + 64 * hh: k2 * 128 + 64 * hh + 64],
                                        SCBW[k2 % 2][:],
                                        start=(k2 == 0), stop=False))
                                    avm[k2] = b.sT.n
                            sm, cnt = evm2[15]
                            sm.wait(t, cnt)
                            b.sT.inc(t.matmul(PSU[2][0:64, :],
                                              HV[:, 15 * 128 + 64 * hh: 15 * 128 + 64 * hh + 64],
                                              SCBW[1][:], start=False, stop=True))
                            avm[15] = b.sT.n
                        b.t(_sc_t)

                        def _sc_v(v, scm=scm, avm=avm):
                            for kt in range(0, 16, 2):
                                b.sT.wait(v, scm[kt])
                                if kt >= 2:
                                    b.sT.wait(v, avm[kt - 2])
                                b.sV.inc(v.tensor_copy(SCBW[kt % 2][:],
                                                       (PSU[0], PSU[1], PSU[3])[kt % 3][:, :]))
                        b.v(_sc_v)
                        if hh == 1 and qc == NQ - 1:
                            pair_free[hp] = [(b.sV, b.sV.n)]

                        def _sc_s(s, scm=scm, avm=avm):
                            for kt in range(1, 16, 2):
                                b.sT.wait(s, scm[kt])
                                if kt >= 2:
                                    b.sT.wait(s, avm[kt - 2])
                                b.sS.inc(s.copy(SCBW[kt % 2][:],
                                                (PSU[0], PSU[1], PSU[3])[kt % 3][:, :]))
                        b.s(_sc_s)
                        prev_sc = (b.sV.n, b.sS.n)

                        def _rnq_s(s, qc=qc):
                            b.sS.wait(s)
                            b.sS.inc(s.activation(RNREP[:, qc * QC:(qc + 1) * QC],
                                                  RNREP[:, qc * QC:(qc + 1) * QC], AF.Sqrt))
                        b.s(_rnq_s)
                        rnq_m = b.sS.n

                        def _rnq_v(v, qc=qc, rnq_m=rnq_m):
                            b.sS.wait(v, rnq_m)
                            b.sV.inc(v.tensor_scalar(RNREP[:, qc * QC:(qc + 1) * QC],
                                                     RNREP[:, qc * QC:(qc + 1) * QC],
                                                     C_TINY, None, ALU.max))
                            b.sV.wait(v)
                            b.sV.inc(v.reciprocal(RNREP[:, qc * QC:(qc + 1) * QC],
                                                  RNREP[:, qc * QC:(qc + 1) * QC]))
                        b.v(_rnq_v)
                        if hh == 1 and qc == NQ - 1:
                            pair_free[hp].append((b.sS, b.sS.n))

                        def _av_ev(v, qc=qc, hh=hh, hp=hp):
                            b.sT.wait(v)
                            b.sS.wait(v)
                            b.sV.wait(v)
                            b.sV.inc(v.tensor_mul(
                                ATTO[64 * hh:64 * hh + 64,
                                     hp * T + qc * QC: hp * T + (qc + 1) * QC],
                                PSU[2][0:64, :],
                                RNREP[64 * hh:64 * hh + 64, qc * QC:(qc + 1) * QC]))
                        b.v(_av_ev)
            wsc_total = b.dZ.n
            assert wsc_total == 16 * len(wsc_jobs)
            wsc_ffn_total = wsc_total
            ph.close()
            tap("atto", [ATTOt(dt) for dt in range(KT)], KT * 128 * T, BF16)

            pwo = contextlib.ExitStack()
            WOB = [pwo.enter_context(nc.sbuf_tensor(f"WOB{i}", [128, KT * 128], BF16)) for i in range(2)]
            wo_marks = {}
            wo_tb = b.sT.n
            wo_tmarks = {et: wo_tb + (et + 1) * NQ for et in range(8)}

            wo_snap = b.snap()

            def _wo_g(g):
                for et in range(8):
                    osem = b.dPA if et % 2 == 0 else b.dPB
                    if et < 2:
                        b.wait_all(g, wo_snap)
                    else:
                        b.sT.wait(g, wo_tmarks[et - 2])
                    g.dma_start(out=WOB[et % 2][:],
                                in_=hbm_ap(W["L0_attn_Wo"], et * 128,
                                           [[D, 128], [128 * D, KT], [1, 128]])).then_inc(osem.h, 16)
                    osem.n += 16
                    wo_marks[et] = (osem, osem.n)
            b.g(_wo_g)

            def wo_mm(t, i, bank):
                et, qc = divmod(i, NQ)
                wo_marks[et][0].wait(t, wo_marks[et][1])
                return mm_group(t, bank[:, :],
                                [(WOB[et % 2][:, kt * 128:(kt + 1) * 128],
                                  ATTOt(kt)[:, qc * QC:(qc + 1) * QC]) for kt in range(KT)])

            def wo_ev(v, i, bank):
                et, qc = divmod(i, NQ)
                return v.scalar_tensor_tensor(Xt(et)[:, qc * QC:(qc + 1) * QC], bank[:, :],
                                              BIAS[:, 8 + et:9 + et],
                                              Xt(et)[:, qc * QC:(qc + 1) * QC],
                                              ALU.add, ALU.add)
            mm_evac(8 * NQ, wo_mm, wo_ev, [PSU[0], PSU[1]], "v")
            pwo.close()
            g_atto.__exit__(None, None, None)
            tap("x1", [Xt(dt) for dt in range(KT)], KT * 128 * T, F32)

            p1.close()

            p2 = contextlib.ExitStack()
            PSU = [p2.enter_context(nc.psum_tensor(f"P2U{i}", [128, 512], F32)) for i in range(4)]
            FB = p2.enter_context(nc.sbuf_tensor("FB", [128, 48], F32))
            B1C = p2.enter_context(nc.sbuf_tensor("B1C", [128, 64], F32))
            B2C = p2.enter_context(nc.sbuf_tensor("B2C", [128, 64], F32))
            YLN = p2.enter_context(nc.sbuf_tensor("YLN", [128, KT * T], BF16))
            SCR2 = p2.enter_context(nc.sbuf_tensor("SCR2", [128, T], F32))
            g_stb2 = [nc.sbuf_tensor(f"STB2{i}", [128, QC], BF16) for i in range(4)]
            STB2 = [g.__enter__() for g in g_stb2]

            def YLNt(dt):
                return YLN[:, dt * T:(dt + 1) * T]

            load(FB[:, 0:16], col_ap("L0_ffn_sb1", 2048), slow=True, pre=b.snap())
            load(FB[:, 16:32], col_ap("L0_ffn_sb3", 2048), slow=True)
            load(FB[:, 32:40], col_ap("L0_ffn_sb2", D), slow=True)
            for e in range(8):
                load(B1C[:, e * 8:(e + 1) * 8],
                     hbm_ap(W["L0_ffn_b1"], e * D, [[1, 128], [128, 8]]), slow=True)
                load(B2C[:, e * 8:(e + 1) * 8],
                     hbm_ap(W["L0_ffn_b2"], e * D, [[1, 128], [128, 8]]), slow=True)
            wld_fb = b.dW.n

            def b1col(e, kt):
                return B1C[:, e * 8 + kt: e * 8 + kt + 1]

            def b2col(e, kt):
                return B2C[:, e * 8 + kt: e * 8 + kt + 1]

            ln_stats_resident(Xt, PSU[0], PSU[1], STB2)
            ln_statmath_and_reps(PSU[2])
            ln_normalize(Xt, YLNt, SCR2)
            tap("yln", [YLNt(dt) for dt in range(KT)], KT * 128 * T, BF16)
            for g_ in reversed(g_stb2):
                g_.__exit__(None, None, None)

            psh = contextlib.ExitStack()
            SW1H = [psh.enter_context(nc.sbuf_tensor(f"SW1H{i}", [128, KT * 128], BF16)) for i in range(4)]
            SW3H = [psh.enter_context(nc.sbuf_tensor(f"SW3H{i}", [128, KT * 128], BF16)) for i in range(4)]
            SW2C = [psh.enter_context(nc.sbuf_tensor(f"SW2C{i}", [128, 16 * 128], BF16)) for i in range(4)]
            H1B = psh.enter_context(nc.sbuf_tensor("H1B", [128, QC], F32))
            H3B = psh.enter_context(nc.sbuf_tensor("H3B", [128, QC], F32))
            GATE = psh.enter_context(nc.sbuf_tensor("GATE", [128, 16 * QC], BF16))

            sh_snap = b.snap()
            NHT = 16
            for qc in range(NQ):
                tmk, smk, vmk, wmk = {}, {}, {}, {}
                tb3, sb3, vb3 = b.sT.n, b.sS.n, b.sV.n

                def _wA(g, qc=qc, wmk=wmk):
                    for ht in range(NHT):
                        if qc == 0 and ht == 0:
                            b.dZ.wait(g, wsc_ffn_total)
                        if qc == 0 and ht < 4:
                            b.wait_all(g, sh_snap)
                        else:
                            b.sT.wait(g, tb3 + 2 * (ht - 3) if ht >= 4 else prev_tmk_last)
                        wmk[ht] = [ydma(g, dst[ht % 4][:],
                                        hbm_ap(FSC, base + ht * 131072,
                                               [[1024, 128], [128, 8], [1, 128]]))
                                   for base, dst in ((FSC_S1, SW1H), (FSC_S3, SW3H))]
                b.y(_wA)

                def _A_t(t, qc=qc, tmk=tmk, smk=smk, vmk=vmk, wmk=wmk, sb3=sb3, vb3=vb3):
                    for ht in range(NHT):
                        wait_marks(t, wmk[ht])
                        if qc == 0 and ht == 0:
                            b.wait_all(t, sh_snap)
                        if ht >= 2:
                            b.sS.wait(t, sb3 + 2 * (ht - 1))
                        elif qc > 0 and ht == 0:
                            b.sV.wait(t, vb3)
                            b.sS.wait(t, sb3)
                        b.sT.inc(mm_group(t, PSU[ht % 2][:, :],
                                 [(SW1H[ht % 4][:, kt * 128:(kt + 1) * 128],
                                   YLNt(kt)[:, qc * QC:(qc + 1) * QC]) for kt in range(KT)]))
                        b.sT.inc(mm_group(t, PSU[2 + ht % 2][:, :],
                                 [(SW3H[ht % 4][:, kt * 128:(kt + 1) * 128],
                                   YLNt(kt)[:, qc * QC:(qc + 1) * QC]) for kt in range(KT)]))
                        tmk[ht] = b.sT.n

                def _A_s(s, qc=qc, tmk=tmk, smk=smk, vmk=vmk, vb3=vb3):
                    for ht in range(NHT):
                        b.sT.wait(s, tmk[ht])
                        if ht >= 1:
                            b.sV.wait(s, vb3 + ht)
                        b.sS.inc(s.activation(H1B[:], PSU[ht % 2][:, :], AF.Silu,
                                              bias=FB[:, ht:ht + 1]))
                        b.sS.inc(s.activation(H3B[:], PSU[2 + ht % 2][:, :], AF.Identity,
                                              bias=FB[:, 16 + ht:17 + ht]))
                        smk[ht] = b.sS.n

                def _A_v(v, qc=qc, smk=smk, vmk=vmk):
                    for ht in range(NHT):
                        b.sS.wait(v, smk[ht])
                        b.sV.inc(v.tensor_mul(GATE[:, ht * QC:(ht + 1) * QC], H1B[:], H3B[:]))
                        vmk[ht] = b.sV.n
                for ht in range(NHT):
                    smk[ht] = sb3 + 2 * (ht + 1)
                    vmk[ht] = vb3 + ht + 1
                b.t(_A_t)
                b.s(_A_s)
                b.v(_A_v)
                prev_tmk_last = tmk.get(NHT - 1, b.sT.n)

                wmk2 = {}
                tb4, vb4 = b.sT.n, b.sV.n

                def _wB(g, qc=qc, wmk2=wmk2, tb4=tb4):
                    for co in range(KT):
                        if qc == 0 and co < 4:
                            b.wait_all(g, sh_snap)
                        elif co >= 4:
                            b.sT.wait(g, tb4 + co - 3)
                        else:
                            b.sT.wait(g, tb4)
                        wmk2[co] = [ydma(g, SW2C[co % 4][:],
                                         hbm_ap(FSC, FSC_S2 + co * 262144,
                                                [[2048, 128], [128, 16], [1, 128]]))]
                b.y(_wB)

                def _B_t(t, qc=qc, wmk2=wmk2, vb4=vb4):
                    b.sV.wait(t, vb4)
                    for co in range(KT):
                        wait_marks(t, wmk2[co])
                        if co >= 2:
                            b.sV.wait(t, vb4 + 2 * (co - 1))
                        b.sT.inc(mm_group(t, PSU[co % 2][:, :],
                                 [(SW2C[co % 4][:, ht * 128:(ht + 1) * 128],
                                   GATE[:, ht * QC:(ht + 1) * QC]) for ht in range(NHT)]))

                def _B_v(v, qc=qc, tb4=tb4):
                    for co in range(KT):
                        b.sT.wait(v, tb4 + co + 1)
                        if co >= 1:
                            b.sV.wait(v)
                        b.sV.inc(v.tensor_scalar(SCR2[:, 0:QC], PSU[co % 2][:, :],
                                                 FB[:, 32 + co:33 + co], 0.5,
                                                 ALU.add, ALU.mult))
                        b.sV.wait(v)
                        b.sV.inc(v.tensor_add(Xt(co)[:, qc * QC:(qc + 1) * QC],
                                              SCR2[:, 0:QC],
                                              Xt(co)[:, qc * QC:(qc + 1) * QC]))
                b.t(_B_t)
                b.v(_B_v)
            psh.close()
            tap("x_sh", [Xt(dt) for dt in range(KT)], KT * 128 * T, F32)

            pmo = contextlib.ExitStack()
            W1E = [pmo.enter_context(nc.sbuf_tensor(f"W1E{i}", [128, KT * 1024], BF16)) for i in range(2)]
            W2E = pmo.enter_context(nc.sbuf_tensor("W2E", [128, KT * 1024], BF16))
            HE = pmo.enter_context(nc.sbuf_tensor("HE0", [128, KT * 512], BF16))
            moe_snap = b.snap()
            euse, huse = {}, {}

            for e in range(8):
                hsem = b.dPA if e % 2 == 0 else b.dPB

                def _we(g, e=e):
                    if e < 2:
                        b.wait_all(g, moe_snap)
                    else:
                        b.sT.wait(g, huse[e - 2])
                    m1 = [ydma(g, W1E[e % 2][:],
                               hbm_ap(FSC, FSC_W1 + e * 1048576,
                                      [[8192, 128], [1024, 8], [1, 1024]]))]
                    if e >= 1:
                        b.sT.wait(g, euse[e - 1])
                    m2 = [ydma(g, W2E[:],
                               hbm_ap(FSC, FSC_W2 + e * 1048576,
                                      [[8192, 128], [1024, 8], [1, 1024]]))]
                    _we.mark = (m1, m2)
                b.y(_we)
                wmark_w1, wmark_w2 = _we.mark

                g0 = (3 * e) % 8
                g1 = (3 * e + 3) % 8
                dg = g1 - g0

                def mvy(kt):
                    return bass.AP(YLN.tensor if hasattr(YLN, "tensor") else YLN,
                                   kt * T + g0, [[KT * T, 128], [dg, 2], [8, 256]])

                def h1_mm(t, i, bank, e=e, wmark=wmark_w1):
                    wait_marks(t, wmark)
                    return mm_group(t, bank[:, :],
                                    [(W1E[e % 2][:, kt * 1024 + i * 128:
                                                 kt * 1024 + (i + 1) * 128],
                                      mvy(kt)) for kt in range(KT)])

                def h1_ev(s, i, bank, e=e):
                    return s.activation(HE[:, i * 512:(i + 1) * 512], bank[:, :],
                                        AF.Gelu, bias=b1col(e, i))
                mm_evac(KT, h1_mm, h1_ev, [PSU[0], PSU[1]], "s", pre_engines=("s",))
                huse[e] = b.sT.n

                def y_mm(t, i, bank, e=e, wmark=wmark_w2):
                    if i == 0:
                        wait_marks(t, wmark)
                        b.sS.wait(t)
                    return mm_group(t, bank[:, :],
                                    [(W2E[:, kt * 1024 + i * 128:
                                          kt * 1024 + (i + 1) * 128],
                                      HE[:, kt * 512:(kt + 1) * 512]) for kt in range(KT)])

                def y_ev(v, i, bank, e=e, g0=g0, g1=g1):
                    for gb, grp in ((0, g0), (1, g1)):
                        if i >= 1 or gb == 1:
                            b.sV.wait(v)
                        b.sV.inc(v.tensor_scalar(SCR2[:, 0:256],
                                                 bank[:, gb * 256:(gb + 1) * 256],
                                                 b2col(e, i), 0.25, ALU.add, ALU.mult))
                        b.sV.wait(v)
                        inst = v.tensor_add(Xt(i)[:, grp::8], SCR2[:, 0:256],
                                            Xt(i)[:, grp::8])
                        if gb == 0:
                            b.sV.inc(inst)
                    return inst
                mm_evac(KT, y_mm, y_ev, [PSU[2], PSU[3]], "v", evac_incs=4, pre_engines=("v",))
                euse[e] = b.sT.n
            pmo.close()
            p2.close()
            tap("x2", [Xt(dt) for dt in range(KT)], KT * 128 * T, F32)

            p3 = contextlib.ExitStack()
            PSU = [p3.enter_context(nc.psum_tensor(f"P3U{i}", [128, 512], F32)) for i in range(8)]
            MLG = p3.enter_context(nc.sbuf_tensor("MLG", [128, 2 * KT], F32))
            ACOL = p3.enter_context(nc.sbuf_tensor("ACOL", [128, 128], F32))
            WCV = p3.enter_context(nc.sbuf_tensor("WCV", [128, 32], F32))
            MBC = p3.enter_context(nc.sbuf_tensor("MBC", [128, 3 * KT], F32))
            WDTB = p3.enter_context(nc.sbuf_tensor("WDTB", [32, 1024], BF16))
            WXB = p3.enter_context(nc.sbuf_tensor("WXB", [128, KT * 64], BF16))
            HLN = p3.enter_context(nc.sbuf_tensor("HLN", [128, KT * T], BF16))

            def HLNt(dt):
                return HLN[:, dt * T:(dt + 1) * T]

            pre3 = b.snap()
            load(MLG[:, 0:KT], col_ap("L1_attn_ln_g", D), slow=True, pre=pre3)
            load(MLG[:, KT:2 * KT], col_ap("L1_attn_ln_b", D), slow=True)
            wld_m = b.dW.n

            ONEROW = p3.enter_context(nc.sbuf_tensor("ONEROW", [1, 128], BF16))

            def _m_init(v):
                b.sV.inc(v.memset(CONST[:, 4:5], 1.0))
            b.v(_m_init)
            C_ONE = CONST[:, 4:5]

            def dir_param_load(dr, tg):
                def _or_v(v):
                    b.sS.wait(v)
                    b.sV.inc(v.memset(ONEROW[:], 1.0))
                b.v(_or_v)
                prd = b.snap()
                load(ACOL[:], hbm_ap(W[f"L1_attn_{tg}_Alog"], 0,
                                     [[16, 128], [128 * 16, KT], [1, 16]]),
                     slow=True, pre=prd)
                load(WCV[:],
                     hbm_ap(W[f"L1_attn_{tg}_Wconv"], 0, [[4, 128], [128 * 4, KT], [1, 4]]),
                     slow=True)
                load(MBC[:, 0:KT], col_ap(f"L1_attn_{tg}_bconv", D), slow=True)
                load(MBC[:, KT:2 * KT], col_ap(f"L1_attn_{tg}_bdt", D), slow=True)
                load(MBC[:, 2 * KT:3 * KT], col_ap(f"L1_attn_{tg}_Dp", D), slow=True)
                load(WDTB[:], hbm_ap(W[f"L1_attn_{tg}_Wdt"], 0,
                                     [[1024, 32], [1024 * 32, 1], [1, 1024]]))
                load(WXB[:], hbm_ap(W[f"L1_attn_{tg}_Wx"], 0,
                                    [[64, 128], [128 * 64, KT], [1, 64]]))
                mark = b.dW.n

                def _acol_s(s):
                    b.dW.wait(s, mark)
                    b.sS.inc(s.activation(ACOL[:], ACOL[:], AF.Exp))
                b.s(_acol_s)

                def _acol_v(v):
                    b.sS.wait(v)
                    b.sV.inc(v.tensor_scalar(ACOL[:], ACOL[:], -1.0, None, ALU.mult))
                b.v(_acol_v)
                return mark

            p3a = contextlib.ExitStack()
            SCR3 = p3a.enter_context(nc.sbuf_tensor("SCR3", [128, T], F32))
            STB3 = [p3a.enter_context(nc.sbuf_tensor(f"STB3{i}", [128, QC], BF16)) for i in range(4)]
            R4SV = p3a.enter_context(nc.sbuf_tensor("R4SV", [1, 2 * T], F32))
            USCR = [p3a.enter_context(nc.sbuf_tensor(f"USCR{i}", [128, QC], F32)) for i in range(2)]

            ln_stats_resident(Xt, PSU[0], PSU[1], STB3)
            ln_statmath_and_reps(PSU[2])

            def _sv_rows(s):
                b.sS.wait(s)
                b.sS.inc(s.copy(R4SV[0:1, 0:T], ROW_R0))
                b.sS.wait(s)
                b.sS.inc(s.copy(R4SV[0:1, T:2 * T], ROW_R1))
            b.s(_sv_rows)

            u_tb = b.sT.n
            u_vm, u_sm = {}, {}
            usnap = b.snap()

            def _u_v(v):
                b.wait_all(v, usnap)
                for j in range(NQ * KT):
                    q, dt = divmod(j, KT)
                    if j >= 2:
                        b.sT.wait(v, u_tb + 2 * (j - 1))
                    b.sV.inc(v.tensor_mul(USCR[j % 2][:], Xt(dt)[:, q * QC:(q + 1) * QC],
                                          REP[:, q * QC:(q + 1) * QC]))
                    b.sV.wait(v)
                    b.sV.inc(v.tensor_add(USCR[j % 2][:], USCR[j % 2][:],
                                          REP[:, T + q * QC: T + (q + 1) * QC]))
                    b.sV.wait(v)
                    b.sV.inc(v.tensor_copy(STB3[2 * (j % 2) + 1][:], USCR[j % 2][:]))
                    u_vm[j] = b.sV.n
            b.v(_u_v)

            def _u_s(s):
                for j in range(NQ * KT):
                    q, dt = divmod(j, KT)
                    b.sV.wait(s, u_vm[j])
                    b.sS.inc(s.activation(STB3[2 * (j % 2)][:], USCR[j % 2][:], AF.Square))
                    u_sm[j] = b.sS.n
                    if dt == KT - 1:
                        b.sT.wait(s, u_tb + 2 * (j + 1))
                        b.sS.inc(s.copy(ROW_A[0:1, q * QC:(q + 1) * QC], PSU[0][0:1, :]))
                        b.sS.inc(s.copy(ROW_B[0:1, q * QC:(q + 1) * QC], PSU[1][0:1, :]))
            b.s(_u_s)

            def _u_t(t):
                for j in range(NQ * KT):
                    q, dt = divmod(j, KT)
                    b.sV.wait(t, u_vm[j])
                    b.sS.wait(t, u_sm[j])
                    if dt == 0 and q >= 1:
                        b.sS.wait(t, u_sm[j])
                    b.sT.inc(t.matmul(PSU[0][0:1, :], ONECOL[:], STB3[2 * (j % 2) + 1][:],
                                      start=(dt == 0), stop=(dt == KT - 1)))
                    b.sT.inc(t.matmul(PSU[1][0:1, :], ONECOL[:], STB3[2 * (j % 2)][:],
                                      start=(dt == 0), stop=(dt == KT - 1)))
            b.t(_u_t)

            def _m5_s(s):
                b.sS.wait(s)
                b.sS.inc(s.activation(ROW_R0, ROW_A, AF.Identity, scale=1.0 / D))
                b.sS.wait(s)
                b.sS.inc(s.activation(ROW_R1, ROW_R0, AF.Square))
            b.s(_m5_s)

            def _m5_v(v):
                b.sS.wait(v)
                b.sV.inc(v.tensor_scalar(ROW_B, ROW_B, 1.0 / D, None, ALU.mult))
                b.sV.wait(v)
                b.sV.inc(v.tensor_sub(ROW_B, ROW_B, ROW_R1))
            b.v(_m5_v)

            def _m5_s2(s):
                b.sV.wait(s)
                b.sS.inc(s.activation(ROW_A, ROW_B, AF.Sqrt, bias=C_EPS[0:1, :]))
            b.s(_m5_s2)

            def _m5_v2(v):
                b.sS.wait(v)
                b.sV.inc(v.reciprocal(ROW_B, ROW_A))
                b.sV.wait(v)
                b.sV.inc(v.tensor_mul(ROW_A, ROW_B, R4SV[0:1, 0:T]))
                b.sV.inc(v.tensor_sub(ROW_R1, R4SV[0:1, T:2 * T], ROW_R0))
                b.sV.wait(v)
                b.sV.inc(v.tensor_mul(ROW_R1, ROW_R1, ROW_B))
            b.v(_m5_v2)

            def rep45_mm(t, i, bank):
                if i == 0:
                    b.sV.wait(t)
                k, q = divmod(i, NQ)
                row = ROW_A if k == 0 else ROW_R1
                return mm_group(t, bank[:, :], [(ONESF[:], row[0:1, q * QC:(q + 1) * QC])])

            def rep45_ev(s, i, bank):
                k, q = divmod(i, NQ)
                return s.copy(REP[:, k * T + q * QC: k * T + (q + 1) * QC], bank[:, :])
            mm_evac(2 * NQ, rep45_mm, rep45_ev, [PSU[0], PSU[1]], "s")

            ln_normalize(Xt, HLNt, SCR3,
                         gcol=lambda dt: MLG[:, dt:dt + 1],
                         bcol=lambda dt: MLG[:, KT + dt:KT + dt + 1])
            tap("hln", [HLNt(dt) for dt in range(KT)], KT * 128 * T, BF16)

            def rep4_mm(t, i, bank):
                if i == 0:
                    b.sV.wait(t)
                    b.sS.wait(t)
                k, q = divmod(i, NQ)
                row = R4SV[0:1, 0:T] if k == 0 else R4SV[0:1, T:2 * T]
                return mm_group(t, bank[:, :], [(ONESF[:], row[0:1, q * QC:(q + 1) * QC])])

            def rep4_ev(s, i, bank):
                k, q = divmod(i, NQ)
                return s.copy(REP[:, k * T + q * QC: k * T + (q + 1) * QC], bank[:, :])
            mm_evac(2 * NQ, rep4_mm, rep4_ev, [PSU[0], PSU[1]], "s")
            p3a.close()

            p3b = contextlib.ExitStack()
            WINS = [p3b.enter_context(nc.sbuf_tensor(f"WINS{i}", [128, KT * 128], BF16)) for i in range(3)]
            WINZB = p3b.enter_context(nc.sbuf_tensor("WINZ0", [128, 2 * KT * 128], BF16))
            WINZ = [WINZB[:, 0:KT * 128], WINZB[:, KT * 128:2 * KT * 128]]
            WOUTS = [p3b.enter_context(nc.sbuf_tensor(f"WOUTS{i}", [128, 512], BF16)) for i in range(2)]
            ZT0 = p3b.enter_context(nc.sbuf_tensor("ZT0", [128, QC], BF16))
            ZT = [ZT0, ZT0]
            XS_CH = p3b.enter_context(nc.sbuf_tensor("XS_CH", [128, KT * QC], BF16))
            DT_CH = p3b.enter_context(nc.sbuf_tensor("DT_CH", [128, KT * QC], BF16))
            U_CH = p3b.enter_context(nc.sbuf_tensor("U_CH", [128, KT * QC], BF16))
            DBCC = p3b.enter_context(nc.sbuf_tensor("DBCC", [64, QC], BF16))
            XZH = p3b.enter_context(nc.sbuf_tensor("XZH", [128, KT * 3], F32))
            XCC = p3b.enter_context(nc.sbuf_tensor("XCC", [128, QC], F32))
            BSTG_B = SROW[0:1, 0:2048].bitcast(BF16)
            BSTG_C = SROW[0:1, 2048:4096].bitcast(BF16)
            BREP = p3b.enter_context(nc.sbuf_tensor("BREP", [128, 8 * QC], BF16))
            CREP = p3b.enter_context(nc.sbuf_tensor("CREP", [128, 8 * QC], BF16))
            DAP = [p3b.enter_context(nc.sbuf_tensor(f"DAP{i}", [128, QC], BF16)) for i in range(4)]
            DBXP = [p3b.enter_context(nc.sbuf_tensor(f"DBXP{i}", [128, QC], BF16)) for i in range(2)]
            HBF = [p3b.enter_context(nc.sbuf_tensor("HBF0", [128, QC], BF16))] * 2
            CARRY = p3b.enter_context(nc.sbuf_tensor("CARRY", [128, 128], BF16))
            YSB = p3b.enter_context(nc.sbuf_tensor("YSB", [128, KT * QC], BF16))
            SCR4 = p3b.enter_context(nc.sbuf_tensor("SCR4", [128, QC], F32))

            def emit_dir(dr, tg):
                wdt_mark = dir_param_load(dr, tg)

                def mv_h(kt, c0, cn):
                    if dr == 0:
                        return HLNt(kt)[:, c0:c0 + cn]
                    return HLNt(kt)[:, ::-1][:, c0:c0 + cn]

                def _c_init(v):
                    b.sV.inc(v.memset(XZH[:], 0.0))
                    b.sV.inc(v.memset(CARRY[:], 0.0))
                b.v(_c_init)

                for ch in range(NQ):
                    c0 = ch * QC
                    st_tm, st_sm, st_vm, st_wm = {}, {}, {}, {}
                    s1snap = b.snap()
                    tb6, sb6, vb6 = b.sT.n, b.sS.n, b.sV.n

                    def _s1_g(g, ch=ch, dr=dr, st_wm=st_wm, tb6=tb6):
                        for dt in range(KT):
                            if dr == 0 and ch == 0 and dt == 0:
                                b.dZ.wait(g, wsc_total)
                            if dr == 0 and ch == 0 and dt < 2:
                                b.wait_all(g, s1snap)
                            elif dt >= 3:
                                b.sT.wait(g, tb6 + (dt - 3) + 1)
                            else:
                                b.sT.wait(g, s1_free[0])
                            st_wm[dt] = [ydma(g, WINS[dt % 3][:],
                                              hbm_ap(WSC[dr], dt * 131072,
                                                     [[1024, 128], [128, 8], [1, 128]]))]
                    b.y(_s1_g)

                    XZB = [PSU[0], PSU[1], PSU[3]]

                    def _s1_t(t, ch=ch, dr=dr, st_wm=st_wm, st_tm=st_tm, st_vm=st_vm, c0=c0):
                        for dt in range(KT):
                            wait_marks(t, st_wm[dt])
                            if dt >= 3:
                                b.sV.wait(t, st_vm[dt - 3])
                            elif ch > 0 or dr > 0:
                                b.sV.wait(t, vb6)
                                b.sS.wait(t, sb6)
                            b.sT.inc(mm_group(t, XZB[dt % 3][:, :],
                                              [(WINS[dt % 3][:, kt * 128:(kt + 1) * 128],
                                                mv_h(kt, c0, QC)) for kt in range(KT)]))
                            st_tm[dt] = b.sT.n

                    def _s1_t2(t, st_sm=st_sm):
                        for dt in range(KT):
                            b.sS.wait(t, st_sm[dt])
                            b.sT.inc(t.matmul(PSU[2][0:64, :],
                                              WXB[:, dt * 64:(dt + 1) * 64],
                                              XS_CH[:, dt * QC:(dt + 1) * QC],
                                              start=(dt == 0), stop=(dt == KT - 1)))
                    for dt in range(KT):
                        st_sm[dt] = sb6 + dt + 1
                        st_vm[dt] = vb6 + 8 * (dt + 1)
                    b.t(_s1_t)
                    b.t(_s1_t2)
                    s1_free[0] = b.sT.n
                    s1_vlast[0] = vb6 + 8 * KT

                    def _s1_v(v, ch=ch, dr=dr, st_tm=st_tm, st_vm=st_vm, c0=c0):
                        for dt in range(KT):
                            XC = XCC
                            b.sT.wait(v, st_tm[dt])
                            if dt == 0:
                                b.sV.wait(v)
                                b.sS.wait(v)
                            if dt >= 1:
                                b.sS.wait(v, sb6 + dt)
                            ps = (PSU[0], PSU[1], PSU[3])[dt % 3]
                            wc = lambda k: WCV[:, dt * 4 + k: dt * 4 + k + 1]
                            b.sV.inc(v.tensor_scalar(XC[:], ps[:, :], wc(3), None, ALU.mult))
                            for k in range(3):
                                sh = 3 - k
                                b.sV.wait(v)
                                b.sV.inc(v.scalar_tensor_tensor(XC[:, sh:QC], ps[:, 0:QC - sh],
                                                                wc(k), XC[:, sh:QC],
                                                                ALU.mult, ALU.add))
                                b.sV.wait(v)
                                b.sV.inc(v.scalar_tensor_tensor(XC[:, 0:sh],
                                                                XZH[:, dt * 3 + k: dt * 3 + k + sh],
                                                                wc(k), XC[:, 0:sh],
                                                                ALU.mult, ALU.add))
                            b.sV.wait(v)
                            b.sV.inc(v.tensor_copy(XZH[:, dt * 3: dt * 3 + 3], ps[:, QC - 3:QC]))
                            st_vm[dt] = b.sV.n
                    b.v(_s1_v)

                    def _s1_s(s, ch=ch, dr=dr, st_vm=st_vm):
                        for dt in range(KT):
                            b.sV.wait(s, st_vm[dt])
                            b.sS.inc(s.activation(XS_CH[:, dt * QC:(dt + 1) * QC], XCC[:],
                                                  AF.Silu,
                                                  bias=MBC[:, dt: dt + 1]))
                    b.s(_s1_s)

                    def _dbc_s(s):
                        b.sT.wait(s)
                        b.sS.inc(s.copy(DBCC[:], PSU[2][0:64, :]))
                    b.s(_dbc_s)
                    dbc_m = b.sS.n

                    dtmm_m, abs_m, sp_m, dtv1_m, dtv2_m = {}, {}, {}, {}, {}
                    da_m, tts_m, ym_m, ymm_m = {}, {}, {}, {}
                    yev_m, gate_m, gate3_m = {}, {}, {}

                    def emit_dtmm(j):
                        def _t(t, j=j):
                            if j < 2:
                                b.sS.wait(t)
                                b.dW.wait(t, wdt_mark)
                                b.sV.wait(t, s1_vlast[0])
                                if (6 + j) in dtv1_m:
                                    b.sV.wait(t, dtv1_m[6 + j])
                            else:
                                b.sV.wait(t, dtv1_m[j - 2])
                                b.sS.wait(t, abs_m[j - 2])
                            b.sT.inc(mm_group(t, PSU[2 + j % 2][:, :],
                                              [(WDTB[:, j * 128:(j + 1) * 128],
                                                DBCC[0:32, :])]))
                            dtmm_m[j] = b.sT.n
                        b.t(_t)

                    def emit_softplus(j):
                        def _s(s, j=j):
                            b.sT.wait(s, dtmm_m[j])
                            if j == 0:
                                b.sV.wait(s)
                            else:
                                b.sV.wait(s, dtv2_m[j - 1])
                            bcol = MBC[:, KT + j: KT + j + 1]
                            b.sS.inc(s.activation(SCR4[:], PSU[2 + j % 2][:, :], AF.Abs,
                                                  bias=bcol))
                            abs_m[j] = b.sS.n
                            b.sS.wait(s)
                            b.sS.inc(s.activation(SCR4[:], SCR4[:], AF.Exp, scale=-1.0))
                            b.sS.wait(s)
                            b.sS.inc(s.activation(SCR4[:], SCR4[:], AF.Ln, bias=C_ONE))
                            sp_m[j] = b.sS.n
                        b.s(_s)

                    def emit_dtphv(j):
                        def _v(v, j=j):
                            bcol = MBC[:, KT + j: KT + j + 1]
                            b.sS.wait(v, sp_m[j])
                            b.sT.wait(v, dtmm_m[j])
                            b.sV.inc(v.tensor_scalar(XCC[:], PSU[2 + j % 2][:, :], bcol, 0.0,
                                                     ALU.add, ALU.max))
                            assert dtv1_m.get(j, b.sV.n) == b.sV.n
                            dtv1_m[j] = b.sV.n
                            b.sV.wait(v)
                            b.sV.inc(v.tensor_add(DT_CH[:, j * QC:(j + 1) * QC], XCC[:], SCR4[:]))
                            dtv2_m[j] = b.sV.n
                            b.sV.wait(v)
                            b.sV.inc(v.tensor_mul(U_CH[:, j * QC:(j + 1) * QC],
                                                  DT_CH[:, j * QC:(j + 1) * QC],
                                                  XS_CH[:, j * QC:(j + 1) * QC]))
                        b.v(_v)

                    zw_m, zt_m, zsl_m = {}, {}, {}
                    for npass in range(2):
                        rep_vcap = b.sV.n
                        if npass == 1:
                            def _zw0(g, dr=dr):
                                b.sT.wait(g)
                                for j in (0, 1):
                                    zw_m[j] = [ydma(g, WINZ[j][:],
                                                    hbm_ap(WSC[dr], 1048576 + j * 131072,
                                                           [[1024, 128], [128, 8], [1, 128]]))]
                            b.y(_zw0)
                        if npass == 0:
                            emit_dtmm(0)
                            emit_dtmm(1)
                            emit_softplus(0)
                            emit_dtphv(0)
                            emit_softplus(1)


                        rp_gm = {}
                        tb8, sb8, vb8 = b.sT.n, b.sS.n, b.sV.n
                        stg_gate = stg_free_t[0]

                        def _rp_g(g, npass=npass, rp_gm=rp_gm, stg_gate=stg_gate):
                            b.sT.wait(g, stg_gate)
                            b.sS.wait(g, dbc_m)
                            g.dma_start(out=BSTG_B[:],
                                        in_=DBCC[32 + 8 * npass: 40 + 8 * npass, :])\
                                .then_inc(b.dPC.h, 16)
                            b.dPC.n += 16
                            g.dma_start(out=BSTG_C[:],
                                        in_=DBCC[48 + 8 * npass: 56 + 8 * npass, :])\
                                .then_inc(b.dPD.h, 16)
                            b.dPD.n += 16
                            rp_gm[0] = (b.dPC, b.dPC.n)
                            rp_gm[1] = (b.dPD, b.dPD.n)
                        b.g(_rp_g)

                        def _rp_t(t, npass=npass, rp_gm=rp_gm, sb8=sb8, vb8=vb8,
                                  rep_vcap=rep_vcap):
                            for u in range(16):
                                ni, bc = divmod(u, 2)
                                stg = BSTG_B if bc == 0 else BSTG_C
                                rp_gm[bc][0].wait(t, rp_gm[bc][1])
                                if u >= 2:
                                    if (u - 2) % 2 == 0:
                                        b.sS.wait(t, sb8 + (u - 2) // 2 + 1)
                                    else:
                                        b.sV.wait(t, vb8 + (u - 2) // 2 + 1)
                                elif npass == 0:
                                    b.sV.wait(t, rep_vcap)
                                else:
                                    b.sS.wait(t, rp_done[0])
                                    b.sV.wait(t, rp_done[1])
                                b.sT.inc(t.matmul(PSU[6 + u % 2][:, :], ONEROW[:],
                                                  stg[0:1, ni * QC:(ni + 1) * QC],
                                                  start=True, stop=True))
                        b.t(_rp_t)
                        stg_free_t[0] = b.sT.n

                        def _rp_s(s, tb8=tb8):
                            for u in range(0, 16, 2):
                                ni = u // 2
                                b.sT.wait(s, tb8 + u + 1)
                                if (7, ni) in ym_m:
                                    b.sV.wait(s, ym_m[(7, ni)])
                                b.sS.inc(s.copy(BREP[:, ni * QC:(ni + 1) * QC],
                                                PSU[6 + u % 2][:, :]))

                        def _rp_v(v, tb8=tb8):
                            for u in range(1, 16, 2):
                                ni = u // 2
                                b.sT.wait(v, tb8 + u + 1)
                                if (7, ni) in ym_m:
                                    b.sV.wait(v, ym_m[(7, ni)])
                                b.sV.inc(v.tensor_copy(CREP[:, ni * QC:(ni + 1) * QC],
                                                       PSU[6 + u % 2][:, :]))
                        b.s(_rp_s)
                        b.v(_rp_v)
                        rp_done = (b.sS.n, b.sV.n)
                        rp_sb = sb8

                        for dt in range(KT):
                            tb_cap, sb_cap, vb_cap = b.sT.n, b.sS.n, b.sV.n
                            if npass == 0 and dt + 2 < KT:
                                emit_dtmm(dt + 2)
                            tb_y = b.sT.n + (1 if npass == 1 else 0)
                            has_sp = (npass == 0 and dt + 2 < KT)
                            has_pv = (npass == 0 and dt + 1 < KT)
                            if has_pv:
                                dtv1_m[dt + 1] = vb_cap + 1
                                dtv2_m[dt + 1] = vb_cap + 2
                            voff = vb_cap + (3 if has_pv else 0)
                            soff = 0
                            if npass == 1:
                                zt_m[dt] = tb_cap + 1
                            for ni in range(8):
                                tts_m[(dt, ni)] = voff + 4 * ni + 2
                                ym_m[(dt, ni)] = voff + 4 * ni + 4
                                ymm_m[(dt, ni)] = tb_y + ni + 1
                                da_m[(dt, ni)] = (sb_cap + soff + ni + 1 if ni < 4
                                                  else sb_cap + soff +
                                                  (3 if has_sp else 0) + ni + 1)


                            def _da03(s, npass=npass, dt=dt, vb_cap=vb_cap):
                                for ni in range(4):
                                    n = npass * 8 + ni
                                    pdt = dt - 1 if dt >= 1 else (KT - 1)
                                    prev = tts_m.get((pdt, ni + 4))
                                    w = prev if prev is not None else vb_cap
                                    if npass == 0 and dt == 0:
                                        w = max(w, dtv2_m[0])
                                    b.sV.wait(s, w)
                                    acol = ACOL[:, dt * 16 + n: dt * 16 + n + 1]
                                    b.sS.inc(s.activation(DAP[ni % 4][:],
                                                          DT_CH[:, dt * QC:(dt + 1) * QC],
                                                          AF.Exp, scale=acol))
                            b.s(_da03)
                            if has_sp:
                                emit_softplus(dt + 2)

                            def _da47(s, npass=npass, dt=dt):
                                for ni in range(4, 8):
                                    n = npass * 8 + ni
                                    b.sV.wait(s, tts_m[(dt, ni - 4)])
                                    acol = ACOL[:, dt * 16 + n: dt * 16 + n + 1]
                                    b.sS.inc(s.activation(DAP[ni % 4][:],
                                                          DT_CH[:, dt * QC:(dt + 1) * QC],
                                                          AF.Exp, scale=acol))
                            b.s(_da47)
                            if npass == 1:
                                def _zsl(s, dt=dt, vb_cap=vb_cap):
                                    b.sT.wait(s, zt_m[dt])
                                    if dt >= 1:
                                        b.sV.wait(s, gate3_m[dt - 1])
                                    else:
                                        b.sV.wait(s, vb_cap)
                                    b.sS.inc(s.activation(ZT[dt % 2][:],
                                                          PSU[2 + dt % 2][:, :], AF.Silu))
                                    zsl_m[dt] = b.sS.n
                                b.s(_zsl)

                            def _blk_t(t, npass=npass, dt=dt, c0=c0, vb_cap=vb_cap):
                                if npass == 1:
                                    wait_marks(t, zw_m[dt])
                                    if dt >= 2:
                                        b.sS.wait(t, zsl_m[dt - 2])
                                    else:
                                        b.sV.wait(t, dtv1_m[6 + dt])
                                        b.sS.wait(t, abs_m[6 + dt])
                                    b.sT.inc(mm_group(t, PSU[2 + dt % 2][:, :],
                                             [(WINZ[dt % 2][:, kt * 128:(kt + 1) * 128],
                                               mv_h(kt, c0, QC)) for kt in range(KT)]))
                                    assert zt_m[dt] == b.sT.n
                                ybank = PSU[dt % 2]
                                for ni in range(8):
                                    if ni == 0:
                                        if npass == 0 and dt >= 2:
                                            b.sS.wait(t, yev_m[dt - 2])
                                        elif npass == 1 and dt < 2:
                                            b.sS.wait(t, yev_m[dt + 6])
                                    b.sV.wait(t, ym_m[(dt, ni)])
                                    b.sT.inc(t.matmul(ybank[:, :], EYEB[:], DBXP[ni % 2][:],
                                                      start=(ni == 0), stop=(ni == 7)))
                            b.t(_blk_t)

                            if npass == 1 and dt + 2 < KT:
                                def _zwn(g, dt=dt, dr=dr):
                                    b.sT.wait(g, zt_m[dt])
                                    zw_m[dt + 2] = [ydma(g, WINZ[dt % 2][:],
                                                         hbm_ap(WSC[dr],
                                                                1048576 + (dt + 2) * 131072,
                                                                [[1024, 128], [128, 8], [1, 128]]))]
                                b.y(_zwn)

                            if has_pv:
                                emit_dtphv(dt + 1)

                            def _blk_v(v, npass=npass, dt=dt, tb_cap=tb_cap, rp_sb=rp_sb):
                                for ni in range(8):
                                    n = npass * 8 + ni
                                    b.sS.wait(v, max(da_m[(dt, ni)], rp_sb + ni + 1))
                                    if ni < 2:
                                        b.sT.wait(v, tb_cap)
                                    else:
                                        b.sT.wait(v, ymm_m[(dt, ni - 2)])
                                    b.sV.inc(v.tensor_mul(DBXP[ni % 2][:],
                                                          U_CH[:, dt * QC:(dt + 1) * QC],
                                                          BREP[:, ni * QC:(ni + 1) * QC]))
                                    b.sV.wait(v)
                                    b.sV.inc(v.tensor_tensor_scan(HBF[ni % 2][:], DAP[ni % 4][:],
                                                                  DBXP[ni % 2][:],
                                                                  CARRY[:, dt * 16 + n: dt * 16 + n + 1],
                                                                  ALU.mult, ALU.add))
                                    b.sV.wait(v)
                                    b.sV.inc(v.tensor_copy(CARRY[:, dt * 16 + n: dt * 16 + n + 1],
                                                           HBF[ni % 2][:, QC - 1:QC]))
                                    b.sV.inc(v.tensor_mul(DBXP[ni % 2][:], HBF[ni % 2][:],
                                                          CREP[:, ni * QC:(ni + 1) * QC]))
                            b.v(_blk_v)
                            assert b.sV.n == voff + 32, (b.sV.n, voff)

                            if npass == 0:
                                def _yev_s(s, dt=dt):
                                    b.sT.wait(s)
                                    b.sS.inc(s.copy(YSB[:, dt * QC:(dt + 1) * QC],
                                                    PSU[dt % 2][:, :]))
                                    yev_m[dt] = b.sS.n
                                b.s(_yev_s)
                            else:
                                def _gz_g(g, dr=dr, dt=dt):
                                    b.sT.wait(g)
                                    _gz_g.m2 = [ydma(g, WOUTS[dt % 2][:],
                                                     hbm_ap(WSC[dr], 2097152 + dt * 65536,
                                                            [[512, 128], [1, 512]]))]
                                b.y(_gz_g)
                                owm = _gz_g.m2

                                def _g_v(v, dr=dr, dt=dt):
                                    ybank = PSU[dt % 2]
                                    b.sS.wait(v)
                                    b.sT.wait(v)
                                    dpcol = MBC[:, 2 * KT + dt: 2 * KT + dt + 1]
                                    b.sV.inc(v.scalar_tensor_tensor(
                                        SCR4[:], XS_CH[:, dt * QC:(dt + 1) * QC],
                                        dpcol, ybank[:, :], ALU.mult, ALU.add))
                                    gate_m[dt] = b.sV.n
                                    b.sV.wait(v)
                                    b.sV.inc(v.tensor_add(SCR4[:], SCR4[:],
                                                          YSB[:, dt * QC:(dt + 1) * QC]))
                                    b.sV.wait(v)
                                    b.sV.inc(v.tensor_mul(DBXP[0][:], SCR4[:],
                                                          ZT[dt % 2][:]))
                                    gate3_m[dt] = b.sV.n
                                b.v(_g_v)

                                def _wo_t(t, dt=dt, owm=owm):
                                    wait_marks(t, owm)
                                    b.sV.wait(t)
                                    if dt == 0:
                                        b.sS.wait(t)
                                    for ot in range(4):
                                        b.sT.inc(t.matmul(PSU[4 + ot][:, :],
                                                          WOUTS[dt % 2][:, ot * 128:(ot + 1) * 128],
                                                          DBXP[0][:], start=(dt == 0), stop=(dt == KT - 1)))
                                b.t(_wo_t)

                    for ot in range(4):
                        odt = dr * 4 + ot

                        def _ep_v(v, ot=ot, odt=odt, c0=c0):
                            b.sT.wait(v)
                            if ot >= 1:
                                b.sS.wait(v)
                            b.sV.inc(v.tensor_mul(SCR4[:], Xt(odt)[:, c0:c0 + QC],
                                                  REP[:, c0:c0 + QC]))
                            b.sV.wait(v)
                            b.sV.inc(v.tensor_add(SCR4[:], SCR4[:], REP[:, T + c0: T + c0 + QC]))
                            b.sV.wait(v)
                            b.sV.inc(v.tensor_add(XCC[:], SCR4[:], PSU[4 + ot][:, :]))
                        b.v(_ep_v)

                        def _ep_s(s):
                            b.sV.wait(s)
                            b.sS.inc(s.activation(XCC[:], XCC[:], AF.Gelu))
                        b.s(_ep_s)

                        def _ep_v2(v, ot=ot, odt=odt, c0=c0):
                            b.sS.wait(v)
                            b.sV.inc(v.tensor_add(Xt(odt)[:, c0:c0 + QC], XCC[:],
                                                  Xt(odt)[:, c0:c0 + QC]))
                        b.v(_ep_v2)
            stg_free_t = [b.sT.n]
            s1_free = [b.sT.n]
            s1_vlast = [0]
            emit_dir(0, "m1")
            emit_dir(1, "m2")
            p3b.close()
            p3.close()
            tap("x3", [Xt(dt) for dt in range(KT)], KT * 128 * T, F32)

            scr_hbm = nc.dram_tensor("scr_rows", [4096], F32)
            p4 = contextlib.ExitStack()
            PSU = [p4.enter_context(nc.psum_tensor(f"P4U{i}", [128, 512], F32)) for i in range(4)]
            PTF = [p4.enter_context(nc.psum_tensor(f"PTF{i}", [128, 128], F32)) for i in range(2)]
            QLFM = p4.enter_context(nc.sbuf_tensor("QLFM", [128, KT * T], BF16))
            QRFM = p4.enter_context(nc.sbuf_tensor("QRFM", [128, KT * 1024], BF16))
            XLSC = p4.enter_context(nc.sbuf_tensor("XLSC", [128, T], F32))
            XLSC2 = p4.enter_context(nc.sbuf_tensor("XLSC2", [128, T], F32))
            RHSK = p4.enter_context(nc.sbuf_tensor("RHSK", [128, 1024], F32))
            RTK = p4.enter_context(nc.sbuf_tensor("RTK", [128, 1024], F32))
            AMX = p4.enter_context(nc.sbuf_tensor("AMX", [128, 16], F32))
            AMT = p4.enter_context(nc.sbuf_tensor("AMT", [128, 1], F32))
            AMT2 = p4.enter_context(nc.sbuf_tensor("AMT2", [128, 1], F32))
            CMXJ = p4.enter_context(nc.sbuf_tensor("CMXJ", [128, 8], F32))
            RSJ = p4.enter_context(nc.sbuf_tensor("RSJ", [128, 8], F32))
            RLROW = p4.enter_context(nc.sbuf_tensor("RLROW", [1, 4096], F32))
            QREP = p4.enter_context(nc.sbuf_tensor("QREP", [128, T], F32))
            STB4 = [p4.enter_context(nc.sbuf_tensor(f"STB4{i}", [128, QC], BF16)) for i in range(4)]

            ln_stats_resident(Xt, PSU[0], PSU[1], STB4)
            ln_statmath_and_reps(PSU[2])



            rr_tm, rr_sm, rr_vm, rr_wm = {}, {}, {}, {}
            tb11, sb11, vb11 = b.sT.n, b.sS.n, b.sV.n

            rr_snap = b.snap()

            def _rr_g(g, rr_wm=rr_wm, tb11=tb11):
                for kt in range(KT):
                    hsem = b.dPA if kt % 2 == 0 else b.dPB
                    if kt == 0:
                        b.wait_all(g, rr_snap)
                    else:
                        b.sT.wait(g, tb11 + 8 * kt)
                    g.dma_start(out=RHSK[:], in_=hbm_ap(W["L1_ffn_rhs"], kt * 128 * 1024,
                                                        [[1024, 128], [1, 1], [1, 1024]]))\
                        .then_inc(hsem.h, 16)
                    hsem.n += 16
                    rr_wm[kt] = (hsem, hsem.n)

            def _rr_t(t, rr_wm=rr_wm, sb11=sb11):
                for kt in range(KT):
                    rr_wm[kt][0].wait(t, rr_wm[kt][1])
                    for jt in range(KT):
                        if kt * 8 + jt >= 2:
                            b.sS.wait(t, sb11 + kt * 8 + jt - 1)
                        b.sT.inc(t.transpose(PSU[2 + (jt % 2)][:, 0:128],
                                             RHSK[:, jt * 128:(jt + 1) * 128], EYEF[:]))

            def _rr_s(s, tb11=tb11, vb11=vb11):
                for kt in range(KT):
                    for jt in range(KT):
                        b.sT.wait(s, tb11 + kt * 8 + jt + 1)
                        if kt >= 1:
                            b.sV.wait(s, vb11 + 1 + 16 * (kt - 1) + 2 * (jt + 1))
                        b.sS.inc(s.copy(RTK[:, jt * 128:(jt + 1) * 128],
                                        PSU[2 + (jt % 2)][:, 0:128]))

            def _rr_v(v, sb11=sb11):
                b.sV.inc(v.memset(CMXJ[:], 0.0))
                for kt in range(KT):
                    for jt in range(KT):
                        b.sS.wait(v, sb11 + kt * 8 + jt + 1)
                        b.sV.wait(v)
                        b.sV.inc(v.reduce_max(AMT[:], RTK[:, jt * 128:(jt + 1) * 128],
                                              axis=AX, apply_absolute_value=True))
                        b.sV.wait(v)
                        b.sV.inc(v.tensor_max(CMXJ[:, jt:jt + 1], CMXJ[:, jt:jt + 1], AMT[:]))
            b.g(_rr_g)
            b.t(_rr_t)
            b.s(_rr_s)
            b.v(_rr_v)

            def _rs_v(v):
                b.sV.wait(v)
                b.sV.inc(v.tensor_scalar(RSJ[:], CMXJ[:], C_TINY, 1.0 / 127.0,
                                         ALU.max, ALU.mult))
                b.sV.wait(v)
                b.sV.inc(v.reciprocal(CMXJ[:], RSJ[:]))
            b.v(_rs_v)

            r2_tm, r2_sm, r2_vm, r2_wm = {}, {}, {}, {}
            tb12, sb12, vb12 = b.sT.n, b.sS.n, b.sV.n

            def _r2_g(g, r2_wm=r2_wm, tb12=tb12):
                for kt in range(KT):
                    hsem = b.dPC if kt % 2 == 0 else b.dPD
                    if kt == 0:
                        b.sT.wait(g)
                    else:
                        b.sT.wait(g, tb12 + 16 * (kt - 1) + 8)
                    g.dma_start(out=RHSK[:], in_=hbm_ap(W["L1_ffn_rhs"], kt * 128 * 1024,
                                                        [[1024, 128], [1, 1], [1, 1024]]))\
                        .then_inc(hsem.h, 16)
                    hsem.n += 16
                    r2_wm[kt] = (hsem, hsem.n)

            def _r2_t(t, r2_wm=r2_wm, sb12=sb12, vb12=vb12):
                for kt in range(KT):
                    r2_wm[kt][0].wait(t, r2_wm[kt][1])
                    for jt in range(KT):
                        if jt >= 2:
                            b.sS.wait(t, sb12 + 16 * kt + jt - 1)
                        elif kt >= 1:
                            b.sS.wait(t, sb12 + 16 * kt)
                        else:
                            b.sS.wait(t, sb12)
                        b.sT.inc(t.transpose(PSU[2 + (jt % 2)][:, 0:128],
                                             RHSK[:, jt * 128:(jt + 1) * 128], EYEF[:]))
                    b.sV.wait(t, vb12 + 10 * (kt + 1))
                    for jt in range(KT):
                        if jt >= 2:
                            b.sS.wait(t, sb12 + 16 * kt + 8 + jt - 1)
                        else:
                            b.sS.wait(t, sb12 + 16 * kt + 8)
                        b.sT.inc(t.transpose(PSU[2 + (jt % 2)][:, 0:128],
                                             RTK[:, jt * 128:(jt + 1) * 128], EYEF[:]))

            def _r2_s(s, tb12=tb12, vb12=vb12):
                for kt in range(KT):
                    for jt in range(KT):
                        b.sT.wait(s, tb12 + kt * 16 + jt + 1)
                        if kt == 0 and jt == 0:
                            b.sV.wait(s, vb12)
                        b.sS.inc(s.copy(RTK[:, jt * 128:(jt + 1) * 128],
                                        PSU[2 + (jt % 2)][:, 0:128]))
                    for jt in range(KT):
                        b.sT.wait(s, tb12 + kt * 16 + 8 + jt + 1)
                        b.sS.inc(s.copy(QRFM[:, kt * 1024 + jt * 128: kt * 1024 + (jt + 1) * 128],
                                        PSU[2 + (jt % 2)][:, 0:128]))

            def _r2_v(v, sb12=sb12):
                for kt in range(KT):
                    b.sS.wait(v, sb12 + 16 * kt + 8)
                    for jt in range(KT):
                        if kt > 0 or jt > 0:
                            b.sV.wait(v)
                        b.sV.inc(v.tensor_scalar(RTK[:, jt * 128:(jt + 1) * 128],
                                                 RTK[:, jt * 128:(jt + 1) * 128],
                                                 CMXJ[:, jt:jt + 1], None, ALU.mult))
                    b.sV.wait(v)
                    b.sV.inc(v.tensor_scalar(RTK[:], RTK[:], MAGIC, MAGIC, ALU.add, ALU.subtract))
                    b.sV.wait(v)
                    b.sV.inc(v.tensor_scalar(RTK[:], RTK[:], 127.0, -127.0, ALU.min, ALU.max))
            b.v(_r2_v)
            b.g(_r2_g)
            b.t(_r2_t)
            b.s(_r2_s)

            am_vm, am_tm, am_sm = {}, {}, {}
            am_snap = b.snap()
            tb10, sb10, vb10 = b.sT.n, b.sS.n, b.sV.n

            def _am_v(v, am_vm=am_vm):
                b.wait_all(v, am_snap)
                b.sV.inc(v.memset(AMX[:], 0.0))
                for dt in range(KT):
                    if dt >= 1:
                        b.sT.wait(v, tb10 + 16 * dt)
                    b.sV.inc(v.tensor_mul(XLSC[:], Xt(dt), REP[:, 0:T]))
                    b.sV.wait(v)
                    b.sV.inc(v.tensor_add(XLSC[:], XLSC[:], REP[:, T:2 * T]))
                    am_vm[dt] = b.sV.n
                    for tt in range(16):
                        b.sS.wait(v, sb10 + dt * 16 + tt + 1)
                        b.sV.wait(v)
                        b.sV.inc(v.reduce_max(AMT2[:], XLSC2[:, tt * 128:(tt + 1) * 128],
                                              axis=AX, apply_absolute_value=True))
                        b.sV.wait(v)
                        b.sV.inc(v.tensor_max(AMX[:, tt:tt + 1], AMX[:, tt:tt + 1], AMT2[:]))
            def _am_t(t, am_vm=am_vm, am_sm=am_sm):
                for dt in range(KT):
                    b.sV.wait(t, am_vm[dt])
                    for tt in range(16):
                        if dt * 16 + tt >= 2:
                            b.sS.wait(t, sb10 + dt * 16 + tt - 1)
                        b.sT.inc(t.transpose(PSU[tt % 2][:, 0:128],
                                             XLSC[:, tt * 128:(tt + 1) * 128], EYEF[:]))

            def _am_s(s, vb10=vb10):
                for dt in range(KT):
                    for tt in range(16):
                        b.sT.wait(s, tb10 + dt * 16 + tt + 1)
                        if dt >= 1:
                            b.sV.wait(s, vb10 + 1 + 34 * (dt - 1) + 2 + 2 * (tt + 1))
                        b.sS.inc(s.copy(XLSC2[:, tt * 128:(tt + 1) * 128],
                                        PSU[tt % 2][:, 0:128]))
            for dt in range(KT):
                for tt in range(16):
                    am_sm[dt * 16 + tt] = sb10 + dt * 16 + tt + 1
            b.v(_am_v)
            b.t(_am_t)
            b.s(_am_s)

            def _rl_v(v):
                b.sV.wait(v)
                b.sS.wait(v)
                b.sV.inc(v.tensor_scalar(XLSC[:, 0:16], AMX[:], C_TINY, 1.0 / 127.0,
                                         ALU.max, ALU.mult))
                b.sV.wait(v)
                b.sV.inc(v.reciprocal(XLSC[:, 16:32], XLSC[:, 0:16]))
            b.v(_rl_v)

            def _rl_y(y):
                b.sV.wait(y)
                y.dma_start(out=hbm_ap(scr_hbm, 0, [[1, 128], [128, 16]]),
                            in_=XLSC[:, 0:16], allow_slow_non_contiguous=True).then_inc(b.dX.h, 16)
                y.dma_start(out=hbm_ap(scr_hbm, 2048, [[1, 128], [128, 16]]),
                            in_=XLSC[:, 16:32], allow_slow_non_contiguous=True).then_inc(b.dX.h, 16)
                b.dX.n += 32
                b.dX.wait(y)
                y.dma_start(out=RLROW[:], in_=hbm_ap(scr_hbm, 0, [[4096, 1], [1, 4096]]))\
                    .then_inc(b.dX.h, 16)
                b.dX.n += 16
            b.y(_rl_y)

            def qrep_mm(t, i, bank):
                if i == 0:
                    b.dX.wait(t)
                    b.sS.wait(t)
                return mm_group(t, bank[:, :],
                                [(ONESF[:], RLROW[0:1, 2048 + i * QC: 2048 + (i + 1) * QC])])

            def qrep_ev(s, i, bank):
                return s.copy(QREP[:, i * QC:(i + 1) * QC], bank[:, :])
            mm_evac(NQ, qrep_mm, qrep_ev, [PSU[0], PSU[1]], "s")

            qv_m = {}

            def _ql_v(v, qv_m=qv_m):
                b.sS.wait(v)
                b.sT.wait(v)
                for dt in range(KT):
                    if dt >= 1:
                        b.sV.wait(v)
                    b.sV.inc(v.tensor_mul(XLSC[:], Xt(dt), REP[:, 0:T]))
                    b.sV.wait(v)
                    b.sV.inc(v.tensor_add(XLSC[:], XLSC[:], REP[:, T:2 * T]))
                    b.sV.wait(v)
                    b.sV.inc(v.tensor_mul(XLSC[:], XLSC[:], QREP[:]))
                    b.sV.wait(v)
                    b.sV.inc(v.tensor_scalar(XLSC[:], XLSC[:], MAGIC, MAGIC, ALU.add, ALU.subtract))
                    b.sV.wait(v)
                    b.sV.inc(v.tensor_scalar(QLFM[:, dt * T:(dt + 1) * T], XLSC[:],
                                             127.0, -127.0, ALU.min, ALU.max))
                    qv_m[dt] = b.sV.n
            b.v(_ql_v)
            tap("qlfm", [QLFM[:, dt * T:(dt + 1) * T] for dt in range(KT)], KT * 128 * T, BF16)

            lrep_done = None

            def lrep_mm(t, i, bank):
                if i == 0:
                    b.sV.wait(t)
                    b.sS.wait(t)
                return mm_group(t, bank[:, :],
                                [(ONESF[:], RLROW[0:1, i * QC:(i + 1) * QC])])

            def lrep_ev(s, i, bank):
                return s.copy(REP[:, i * QC:(i + 1) * QC], bank[:, :])
            mm_evac(NQ, lrep_mm, lrep_ev, [PSU[0], PSU[1]], "s")

            def ff_mm(t, i, bank):
                jt, qc = divmod(i, NQ)
                if i == 0:
                    b.sS.wait(t)
                    b.sV.wait(t)
                return mm_group(t, bank[:, :],
                                [(QRFM[:, kt * 1024 + jt * 128: kt * 1024 + (jt + 1) * 128],
                                  QLFM[:, kt * T + qc * QC: kt * T + (qc + 1) * QC])
                                 for kt in range(KT)])

            def ff_ev(v, i, bank):
                jt, qc = divmod(i, NQ)
                if i >= 1:
                    b.sV.wait(v)
                b.sV.inc(v.scalar_tensor_tensor(XLSC[:, 0:QC], bank[:, :],
                                                RSJ[:, jt:jt + 1],
                                                REP[:, qc * QC:(qc + 1) * QC],
                                                ALU.mult, ALU.mult))
                b.sV.wait(v)
                return v.tensor_add(Xt(jt)[:, qc * QC:(qc + 1) * QC], XLSC[:, 0:QC],
                                    Xt(jt)[:, qc * QC:(qc + 1) * QC])
            vb_ff = b.sV.n
            mm_evac(KT * NQ, ff_mm, ff_ev, [PSU[0], PSU[1]], "v", evac_incs=2)
            ffev_m = {jt: vb_ff + 2 * (4 * jt + 4) for jt in range(KT)}
            tap("x4", [Xt(dt) for dt in range(KT)], KT * 128 * T, F32)

            XTMO = QLFM[:, :].bitcast(F32)
            tbf, sbf = b.sT.n, b.sS.n
            dxf = b.dX.n

            def _pf_t(t):
                b.sS.wait(t, sbf)
                for gi in range(128):
                    half, r = divmod(gi, 64)
                    dt, u = divmod(r, 8)
                    tt = half * 8 + u
                    if u == 0:
                        b.sV.wait(t, ffev_m[dt])
                    if gi >= 2:
                        b.sS.wait(t, sbf + gi - 1)
                    b.sT.inc(t.transpose(PTF[gi % 2][:],
                                         X[:, dt * T + tt * 128: dt * T + (tt + 1) * 128], EYEF[:]))
            b.t(_pf_t)

            def _pf_s(s):
                for gi in range(128):
                    half, r = divmod(gi, 64)
                    dt, u = divmod(r, 8)
                    if gi == 64:
                        b.dX.wait(s, dxf + 16)
                    b.sT.wait(s, tbf + gi + 1)
                    b.sS.inc(s.copy(XTMO[:, u * D + dt * 128: u * D + (dt + 1) * 128],
                                    PTF[gi % 2][:]))
            b.s(_pf_s)

            def _pf_y(y):
                for half in range(2):
                    b.sS.wait(y, sbf + (half + 1) * 64)
                    y.dma_start(out=hbm_ap(out_hbm, half * 8 * 128 * D,
                                           [[D, 128], [128 * D, 8], [1, D]]),
                                in_=XTMO[:]).then_inc(b.dX.h, 16)
                    b.dX.n += 16
                b.dX.wait(y)
            b.y(_pf_y)
            p4.close()

    return nc, dbg_hbm


def kernel(**inputs):
    import ml_dtypes
    xq = np.ascontiguousarray(np.asarray(inputs["L0_attn_query"], dtype=np.float32))
    base = {k: np.ascontiguousarray(np.asarray(v, dtype=np.float32))
            for k, v in inputs.items()
            if k not in ("L0_attn_query", "L0_attn_key_in", "L0_attn_value", "L0_ffn_t_emb")}
    base["eye_f"] = np.eye(128, dtype=np.float32)
    base["eye_b"] = np.eye(128, dtype=ml_dtypes.bfloat16)
    nc, _ = build_program(DBG)
    in_maps = []
    for c in range(8):
        m = dict(base)
        m["x_in"] = xq[c]
        in_maps.append(m)
    res = run_bass_kernel_spmd(nc, in_maps, core_ids=list(range(8)))
    return np.stack([res.results[c]["out"] for c in range(8)])


# version 21 (final)
# speedup vs baseline: 1.0022x; 1.0022x over previous
import contextlib
import numpy as np
import concourse.bass as bass
import concourse.mybir as mybir
from concourse.bass_utils import run_bass_kernel_spmd

F32 = mybir.dt.float32
BF16 = mybir.dt.bfloat16
AF = mybir.ActivationFunctionType
ALU = mybir.AluOpType
AX = mybir.AxisListType.X

T = 2048
D = 1024
KT = D // 128
NQ = 4
QC = T // NQ
MAGIC = float(np.float32(1.5 * 2 ** 23))
PI = float(np.pi)

DBG = None


class Sem:
    def __init__(self, nc, name):
        self.h = nc.semaphore(name).__enter__()
        self.n = 0

    def inc(self, inst, k=1):
        inst.then_inc(self.h, k)
        self.n += k
        return inst

    def wait(self, eng, target=None):
        eng.wait_ge(self.h, self.n if target is None else target)


class B:
    def __init__(self, nc, block):
        self.nc = nc
        self.block = block
        self.sT = Sem(nc, "sT")
        self.sV = Sem(nc, "sV")
        self.sS = Sem(nc, "sS")
        self.sG = Sem(nc, "sG")
        self.dW = Sem(nc, "dW")
        self.dX = Sem(nc, "dX")
        self.dPA = Sem(nc, "dPA")
        self.dPB = Sem(nc, "dPB")
        self.dPC = Sem(nc, "dPC")
        self.dPD = Sem(nc, "dPD")
        self.dZ = Sem(nc, "dZ")
        self.dYA = Sem(nc, "dYA")
        self.dYB = Sem(nc, "dYB")
        self.dYC = Sem(nc, "dYC")
        self.dYD = Sem(nc, "dYD")
        self.all_sems = [self.sT, self.sV, self.sS, self.sG, self.dW, self.dX,
                         self.dPA, self.dPB, self.dPC, self.dPD, self.dZ,
                         self.dYA, self.dYB, self.dYC, self.dYD]

    def t(self, f): self.block.tensor(f)
    def v(self, f): self.block.vector(f)
    def s(self, f): self.block.scalar(f)
    def g(self, f): self.block.gpsimd(f)
    def y(self, f): self.block.sync(f)

    def engsem(self, name):
        return {"t": self.sT, "v": self.sV, "s": self.sS, "g": self.sG}[name]

    def eng_emit(self, name, f):
        getattr(self, name)(f)

    def snap(self):
        return [(sm, sm.n) for sm in self.all_sems]

    def wait_all(self, eng, snap):
        for sm, n in snap:
            if n:
                sm.wait(eng, n)


def hbm_ap(w, offset, dims):
    return bass.AP(w.tensor if hasattr(w, "tensor") else w, offset, dims)


def build_program(dbg=None):
    dbg = list(dbg or [])
    nc = bass.Bass("TRN2", target_bir_lowering=False)
    x_in = nc.dram_tensor("x_in", [T, D], F32, kind="ExternalInput")
    eye_f = nc.dram_tensor("eye_f", [128, 128], F32, kind="ExternalInput")
    eye_b = nc.dram_tensor("eye_b", [128, 128], BF16, kind="ExternalInput")
    W = {}
    wspecs = {
        "L0_attn_Wz": [D, D], "L0_attn_bz": [D], "L0_attn_ln_g": [D], "L0_attn_ln_b": [D],
        "L0_attn_Wq": [D, D], "L0_attn_bq": [D], "L0_attn_Wr": [D, D], "L0_attn_br": [D],
        "L0_attn_Wh": [D, D], "L0_attn_bh": [D], "L0_attn_Wphi": [64, 64], "L0_attn_bphi": [64],
        "L0_attn_Wo": [D, D], "L0_attn_bo": [D],
        "L0_ffn_sw1": [D, 2048], "L0_ffn_sb1": [2048], "L0_ffn_sw3": [D, 2048], "L0_ffn_sb3": [2048],
        "L0_ffn_sw2": [2048, D], "L0_ffn_sb2": [D],
        "L0_ffn_w1": [8, D, D], "L0_ffn_b1": [8, D], "L0_ffn_w2": [8, D, D], "L0_ffn_b2": [8, D],
        "L1_attn_ln_g": [D], "L1_attn_ln_b": [D],
        "L1_ffn_rhs": [D, D],
    }
    for tag in ("m1", "m2"):
        wspecs.update({
            f"L1_attn_{tag}_Win": [D, 2048], f"L1_attn_{tag}_Wconv": [D, 4],
            f"L1_attn_{tag}_bconv": [D], f"L1_attn_{tag}_Wx": [D, 64],
            f"L1_attn_{tag}_Wdt": [32, D], f"L1_attn_{tag}_bdt": [D],
            f"L1_attn_{tag}_Alog": [D, 16], f"L1_attn_{tag}_Dp": [D],
            f"L1_attn_{tag}_Wout": [D, 512],
        })
    for name, shp in wspecs.items():
        W[name] = nc.dram_tensor(name, shp, F32, kind="ExternalInput")
    out_hbm = nc.dram_tensor("out", [T, D], F32, kind="ExternalOutput")
    WSC = [nc.dram_tensor(f"wmsc{i}", [2621440], BF16) for i in range(2)]
    FSC = nc.dram_tensor("wfsc", [23068672], BF16)
    FSC_S1, FSC_S3, FSC_S2 = 0, 2097152, 4194304
    FSC_W1, FSC_W2 = 6291456, 14680064
    dbg_hbm = {}

    est = contextlib.ExitStack()
    with est:
        X = est.enter_context(nc.sbuf_tensor("X", [128, KT * T], F32))
        EYEF = est.enter_context(nc.sbuf_tensor("EYEF", [128, 128], F32))
        EYEB = est.enter_context(nc.sbuf_tensor("EYEB", [128, 128], BF16))
        ONESF = est.enter_context(nc.sbuf_tensor("ONESF", [1, 128], F32))
        ONECOL = est.enter_context(nc.sbuf_tensor("ONECOL", [128, 1], BF16))
        CONST = est.enter_context(nc.sbuf_tensor("CONST", [128, 8], F32))
        C_EPS, C_MAGIC, C_HALFPI, C_TINY = (CONST[:, i:i + 1] for i in range(4))
        C_NEGPI = CONST[:, 5:6]
        SROW = est.enter_context(nc.sbuf_tensor("SROW", [1, 2 * T], F32))
        REP = est.enter_context(nc.sbuf_tensor("REP", [128, 2 * T], F32))
        ROW_R0 = REP[0:1, 0:T]
        ROW_R1 = REP[0:1, T:2 * T]
        ROW_A = SROW[0:1, 0:T]
        ROW_B = SROW[0:1, T:2 * T]

        with nc.Block() as block:
            b = B(nc, block)

            def tap(name, ap_list, total_elems, dtype=F32):
                if name not in dbg:
                    return
                h = nc.dram_tensor("dbg_" + name, [total_elems], dtype, kind="ExternalOutput")
                dbg_hbm[name] = h
                snap = b.snap()
                state = {"off": 0}

                def _tap(y):
                    b.wait_all(y, snap)
                    for ap in ap_list:
                        npart = ap.shape[0]
                        nfree = int(np.prod(ap.shape[1:]))
                        y.dma_start(out=hbm_ap(h, state["off"], [[nfree, npart], [1, nfree]]),
                                    in_=ap).then_inc(b.dX.h, 16)
                        b.dX.n += 16
                        state["off"] += npart * nfree
                b.y(_tap)

            def load(dst_ap, src_ap, slow=False, pre=None):
                def _(g):
                    if pre is not None:
                        b.wait_all(g, pre)
                    g.dma_start(out=dst_ap, in_=src_ap,
                                allow_slow_non_contiguous=slow).then_inc(b.dW.h, 16)
                    b.dW.n += 16
                b.g(_)
                return b.dW.n

            ysems = [b.dYA, b.dYB, b.dYC, b.dYD]
            yrot = [0]

            def ydma(g, out, in_):
                sm = ysems[yrot[0] % 4]
                yrot[0] += 1
                sm.wait(g, sm.n)
                g.dma_start(out=out, in_=in_).then_inc(sm.h, 16)
                sm.n += 16
                return (sm, sm.n)

            def wait_marks(eng, marks):
                for sm, n in marks:
                    sm.wait(eng, n)

            def wload_ap(name, Kdim, M, base_off=0):
                w = W[name]
                if Kdim >= 128:
                    return hbm_ap(w, base_off, [[M, 128], [128 * M, Kdim // 128], [1, M]])
                return hbm_ap(w, base_off, [[M, Kdim], [M * Kdim, 1], [1, M]])

            def col_ap(name, n, base_off=0):
                w = W[name]
                if n >= 128:
                    return hbm_ap(w, base_off, [[1, 128], [128, n // 128]])
                return hbm_ap(w, base_off, [[1, n], [n, 1]])

            def mm_group(t, psum_ap, pairs):
                n = len(pairs)
                for i, (st, mv) in enumerate(pairs):
                    inst = t.matmul(psum_ap, st, mv, start=(i == 0), stop=(i == n - 1))
                return inst

            def mm_evac(n, mm_fn, evac_fn, banks, evac_eng="s", evac_incs=1,
                        pre_engines=("v", "s"), pre_marks=None):
                es = b.engsem(evac_eng)
                nb = len(banks)
                tb, eb = b.sT.n, es.n
                pre = (pre_marks if pre_marks is not None
                       else [(b.engsem(e), b.engsem(e).n) for e in pre_engines])

                def _t(t):
                    for i in range(n):
                        if i < nb:
                            b.wait_all(t, pre)
                        else:
                            es.wait(t, eb + (i - nb + 1) * evac_incs)
                        b.sT.inc(mm_fn(t, i, banks[i % nb]))

                def _e(eng):
                    for i in range(n):
                        b.sT.wait(eng, tb + i + 1)
                        es.inc(evac_fn(eng, i, banks[i % nb]))
                b.t(_t)
                b.eng_emit(evac_eng, _e)
                return tb, eb

            def Xt(dt):
                return X[:, dt * T:(dt + 1) * T]

            def _c_v(v):
                v.memset(ONESF[:], 1.0)
                v.memset(ONECOL[:], 1.0)
                v.memset(CONST[:, 0:1], 1e-5)
                v.memset(CONST[:, 1:2], MAGIC)
                v.memset(CONST[:, 2:3], PI / 2)
                v.memset(CONST[:, 5:6], -PI)
                b.sV.inc(v.memset(CONST[:, 3:4], 1e-12))
            b.v(_c_v)

            p0 = contextlib.ExitStack()
            XTMF = p0.enter_context(nc.sbuf_tensor("XTMF", [128, 8 * D], F32))
            PT = [p0.enter_context(nc.psum_tensor(f"PT{i}", [128, 128], F32)) for i in range(2)]
            tb0, sb0 = b.sT.n, b.sS.n

            p0_qm = {}

            def _p0_y(y):
                y.dma_start(out=EYEF[:], in_=eye_f[:]).then_inc(b.dX.h, 16)
                y.dma_start(out=EYEB[:], in_=eye_b[:]).then_inc(b.dX.h, 16)
                b.dX.n += 32
                for half in range(2):
                    for qq in range(4):
                        if half == 1:
                            b.sT.wait(y, tb0 + 16 * qq + 16)
                        p0_qm[(half, qq)] = [ydma(
                            y, XTMF[:, qq * 2 * D:(qq + 1) * 2 * D],
                            hbm_ap(x_in, (half * 8 + 2 * qq) * 128 * D,
                                   [[D, 128], [128 * D, 2], [1, D]]))]
            b.y(_p0_y)

            def _p0_t(t):
                b.dX.wait(t, 32)
                for half in range(2):
                    for i in range(64):
                        tt, dt = divmod(i, 8)
                        tt += half * 8
                        gi = half * 64 + i
                        if i % 16 == 0:
                            wait_marks(t, p0_qm[(half, i // 16)])
                        if gi >= 2:
                            b.sS.wait(t, sb0 + gi - 1)
                        b.sT.inc(t.transpose(PT[gi % 2][:],
                                             XTMF[:, (tt % 8) * D + dt * 128: (tt % 8) * D + (dt + 1) * 128],
                                             EYEF[:]))
            b.t(_p0_t)

            def _p0_s(s):
                for gi in range(128):
                    tt, dt = divmod(gi, 8)
                    b.sT.wait(s, tb0 + gi + 1)
                    b.sS.inc(s.copy(X[:, dt * T + tt * 128: dt * T + (tt + 1) * 128], PT[gi % 2][:]))
            b.s(_p0_s)
            p0.close()

            def ln_stats_from_chunks(chunk_fn, psA, psB, stb):
                sqm, xbm, evm = {}, {}, {}
                tb = b.sT.n
                lsnap = b.snap()

                def _s(s):
                    b.wait_all(s, lsnap)
                    for q in range(NQ):
                        for dt in range(KT):
                            j = q * KT + dt
                            src, waits = chunk_fn(j)
                            for sm, cnt in waits:
                                sm.wait(s, cnt)
                            if j >= 2:
                                b.sT.wait(s, tb + 2 * j - 2)
                            b.sS.inc(s.activation(stb[2 * (j % 2)][:], src, AF.Square))
                            sqm[j] = b.sS.n
                        b.sT.wait(s, tb + 16 * (q + 1))
                        b.sS.inc(s.copy(ROW_R0[0:1, q * QC:(q + 1) * QC], psA[0:1, :]))
                        b.sS.inc(s.copy(ROW_R1[0:1, q * QC:(q + 1) * QC], psB[0:1, :]))
                        evm[q] = b.sS.n
                b.s(_s)

                def _v(v):
                    b.wait_all(v, lsnap)
                    for q in range(NQ):
                        for dt in range(KT):
                            j = q * KT + dt
                            src, waits = chunk_fn(j)
                            for sm, cnt in waits:
                                sm.wait(v, cnt)
                            if j >= 2:
                                b.sT.wait(v, tb + 2 * j - 2)
                            b.sV.inc(v.tensor_copy(stb[2 * (j % 2) + 1][:], src))
                            xbm[j] = b.sV.n
                b.v(_v)

                def _t(t):
                    for q in range(NQ):
                        if q >= 1:
                            b.sS.wait(t, evm[q - 1])
                        for dt in range(KT):
                            j = q * KT + dt
                            b.sS.wait(t, sqm[j])
                            b.sV.wait(t, xbm[j])
                            b.sT.inc(t.matmul(psA[0:1, :], ONECOL[:], stb[2 * (j % 2) + 1][:],
                                              start=(dt == 0), stop=(dt == KT - 1)))
                            b.sT.inc(t.matmul(psB[0:1, :], ONECOL[:], stb[2 * (j % 2)][:],
                                              start=(dt == 0), stop=(dt == KT - 1)))
                b.t(_t)

            def ln_statmath_and_reps(psR, denom=D):
                def _sm_s(s):
                    b.sS.wait(s)
                    b.sS.inc(s.activation(ROW_A, ROW_R0, AF.Identity, scale=1.0 / denom))
                    b.sS.wait(s)
                    b.sS.inc(s.activation(ROW_B, ROW_A, AF.Square))
                b.s(_sm_s)

                def _sm_v(v):
                    b.sS.wait(v)
                    b.sV.inc(v.tensor_scalar(ROW_R1, ROW_R1, 1.0 / denom, None, ALU.mult))
                    b.sV.wait(v)
                    b.sV.inc(v.tensor_sub(ROW_B, ROW_R1, ROW_B))
                b.v(_sm_v)

                def _sm_s2(s):
                    b.sV.wait(s)
                    b.sS.inc(s.activation(ROW_R1, ROW_B, AF.Sqrt, bias=C_EPS[0:1, :]))
                b.s(_sm_s2)

                def _sm_v2(v):
                    b.sS.wait(v)
                    b.sV.inc(v.reciprocal(ROW_R0, ROW_R1))
                    b.sV.wait(v)
                    b.sV.inc(v.scalar_tensor_tensor(ROW_R1, ROW_A, -1.0, ROW_R0,
                                                    ALU.mult, ALU.mult))
                b.v(_sm_v2)

                def rep_mm(t, i, bank):
                    if i == 0:
                        b.sV.wait(t)
                    k, q = divmod(i, NQ)
                    row = ROW_R0 if k == 0 else ROW_R1
                    return mm_group(t, bank[:, :],
                                    [(ONESF[:], row[0:1, q * QC:(q + 1) * QC])])

                def rep_ev(s, i, bank):
                    k, q = divmod(i, NQ)
                    return s.copy(REP[:, k * T + q * QC: k * T + (q + 1) * QC], bank[:, :])
                mm_evac(2 * NQ, rep_mm, rep_ev, [psR], "s")

            def ln_stats_resident(src_tile, psA, psB, stb):
                def chunk_fn(j):
                    q, dt = divmod(j, KT)
                    return src_tile(dt)[:, q * QC:(q + 1) * QC], []
                ln_stats_from_chunks(chunk_fn, psA, psB, stb)

            def ln_normalize(src_tile, dst_tile, scratch, gcol=None, bcol=None):
                snap = b.snap()
                vb = b.sV.n
                afb = b.sS.n

                def _v(v):
                    b.wait_all(v, snap)
                    for dt in range(KT):
                        if gcol is None:
                            if dt >= 1:
                                b.sV.wait(v)
                            b.sV.inc(v.tensor_mul(scratch[:], src_tile(dt), REP[:, 0:T]))
                            b.sV.wait(v)
                            b.sV.inc(v.tensor_add(dst_tile(dt), scratch[:], REP[:, T:2 * T]))
                        else:
                            if dt >= 1:
                                b.sS.wait(v, afb + dt)
                            b.sV.inc(v.tensor_mul(scratch[:], src_tile(dt), REP[:, 0:T]))
                            b.sV.wait(v)
                            b.sV.inc(v.tensor_add(scratch[:], scratch[:], REP[:, T:2 * T]))
                b.v(_v)
                if gcol is not None:
                    def _s(s):
                        for dt in range(KT):
                            b.sV.wait(s, vb + 2 * (dt + 1))
                            b.sS.inc(s.activation(dst_tile(dt), scratch[:], AF.Identity,
                                                  bias=bcol(dt), scale=gcol(dt)))
                    b.s(_s)

            p1 = contextlib.ExitStack()
            PSU = [p1.enter_context(nc.psum_tensor(f"PSU{i}", [128, 512], F32)) for i in range(4)]
            PSBK = [p1.enter_context(nc.psum_tensor(f"PSBK{i}", [128, 1024], BF16)) for i in range(2)]
            BIAS = p1.enter_context(nc.sbuf_tensor("BIAS", [128, 32], F32))
            BIAS2 = p1.enter_context(nc.sbuf_tensor("BIAS2", [128, 26], F32))
            WPHI2 = p1.enter_context(nc.sbuf_tensor("WPHI2", [128, 128], BF16))
            ZLN = p1.enter_context(nc.sbuf_tensor("ZLN", [128, KT * T], BF16))

            def ZLNt(dt):
                return ZLN[:, dt * T:(dt + 1) * T]

            load(BIAS[:, 0:8], col_ap("L0_attn_bz", D), slow=True, pre=b.snap())
            load(BIAS[:, 8:16], col_ap("L0_attn_bo", D), slow=True)
            load(BIAS[:, 16:24], col_ap("L0_attn_ln_g", D), slow=True)
            load(BIAS[:, 24:32], col_ap("L0_attn_ln_b", D), slow=True)
            load(BIAS2[:, 0:8], hbm_ap(W["L0_attn_bq"], 0, [[1, 128], [128, 8]]), slow=True)
            load(BIAS2[:, 8:16], hbm_ap(W["L0_attn_br"], 0, [[1, 128], [128, 8]]), slow=True)
            load(BIAS2[:, 16:24], hbm_ap(W["L0_attn_bh"], 0, [[1, 128], [128, 8]]), slow=True)
            load(BIAS2[0:64, 24:25], hbm_ap(W["L0_attn_bphi"], 0, [[1, 64], [64, 1]]), slow=True)
            load(BIAS2[64:128, 24:25], hbm_ap(W["L0_attn_bphi"], 0, [[1, 64], [64, 1]]), slow=True)

            def _wp0_v(v):
                b.sT.wait(v)
                b.sV.inc(v.memset(WPHI2[:], 0.0))
            b.v(_wp0_v)
            load(WPHI2[0:64, 0:64],
                 hbm_ap(W["L0_attn_Wphi"], 0, [[64, 64], [64 * 64, 1], [1, 64]]), pre=b.snap())
            load(WPHI2[64:128, 64:128],
                 hbm_ap(W["L0_attn_Wphi"], 0, [[64, 64], [64 * 64, 1], [1, 64]]))
            wld_b2 = b.dW.n

            pz = contextlib.ExitStack()
            WZ = pz.enter_context(nc.sbuf_tensor("WZ", [128, KT * 1024], BF16))
            XLN = pz.enter_context(nc.sbuf_tensor("XLN", [128, KT * T], BF16))
            SCR = pz.enter_context(nc.sbuf_tensor("SCR", [128, T], F32))
            STB = [pz.enter_context(nc.sbuf_tensor(f"STB{i}", [128, QC], BF16)) for i in range(4)]
            ZCH = [pz.enter_context(nc.sbuf_tensor(f"ZCH{i}", [128, QC], F32)) for i in range(2)]

            def XLNt(dt):
                return XLN[:, dt * T:(dt + 1) * T]

            load(WZ[:], wload_ap("L0_attn_Wz", D, 1024))
            wld_wz = b.dW.n

            def _bphi_s(s):
                b.dW.wait(s, wld_wz)
                b.sS.inc(s.mul(BIAS2[:, 24:25], BIAS2[:, 24:25], 1.0 / (2 * PI)))
            b.s(_bphi_s)

            ln_stats_resident(Xt, PSU[0], PSU[1], STB)
            ln_statmath_and_reps(PSU[2])
            ln_normalize(Xt, XLNt, SCR)
            tap("xln1", [XLNt(dt) for dt in range(KT)], KT * 128 * T, BF16)

            NJ = NQ * KT
            zsnap = b.snap()
            z_tb, z_sb, z_vb = b.sT.n, b.sS.n, b.sV.n
            zmark, statmark = {}, {}

            def s_zevac_cnt(j):
                return z_sb + 18 * (j // KT) + 2 * (j % KT) + 1

            def s_sq_cnt(j):
                return z_sb + 18 * (j // KT) + 2 * (j % KT) + 2

            def s_q_cnt(q):
                return z_sb + 18 * (q + 1)

            def _za_t(t):
                b.wait_all(t, zsnap)
                b.dW.wait(t, wld_wz)

                def statmm(j):
                    q, et = divmod(j, KT)
                    if et == 0 and q >= 1:
                        b.sS.wait(t, s_q_cnt(q - 1))
                    b.sS.wait(t, s_sq_cnt(j))
                    b.sV.wait(t, z_vb + j + 1)
                    qj, ej = divmod(j, KT)
                    b.sT.inc(t.matmul(PSU[0][0:1, :], ONECOL[:],
                                      ZLN[:, ej * T + qj * QC: ej * T + (qj + 1) * QC],
                                      start=(et == 0), stop=(et == KT - 1)))
                    b.sT.inc(t.matmul(PSU[1][0:1, :], ONECOL[:], STB[2 * (j % 2)][:],
                                      start=(et == 0), stop=(et == KT - 1)))
                    statmark[j] = b.sT.n

                for j in range(NJ):
                    q, et = divmod(j, KT)
                    if j >= 2:
                        b.sS.wait(t, s_zevac_cnt(j - 2))
                    b.sT.inc(mm_group(t, PSU[2 + j % 2][:, :],
                                      [(WZ[:, kt * 1024 + et * 128: kt * 1024 + (et + 1) * 128],
                                        XLNt(kt)[:, q * QC:(q + 1) * QC]) for kt in range(KT)]))
                    zmark[j] = b.sT.n
                    if j >= 1:
                        statmm(j - 1)
                statmm(NJ - 1)
            b.t(_za_t)

            def _za_s(s):
                for j in range(NJ):
                    q, et = divmod(j, KT)
                    b.sT.wait(s, zmark[j])
                    if j >= 2:
                        b.sV.wait(s, z_vb + j - 1)
                    b.sS.inc(s.activation(ZCH[j % 2][:], PSU[2 + j % 2][:, :], AF.Identity,
                                          bias=BIAS[:, et:et + 1]))
                    b.sS.wait(s)
                    b.sS.inc(s.activation(STB[2 * (j % 2)][:], ZCH[j % 2][:], AF.Square))
                    if et == KT - 1:
                        b.sT.wait(s, statmark[j])
                        b.sS.inc(s.copy(ROW_R0[0:1, q * QC:(q + 1) * QC], PSU[0][0:1, :]))
                        b.sS.inc(s.copy(ROW_R1[0:1, q * QC:(q + 1) * QC], PSU[1][0:1, :]))
            b.s(_za_s)

            def _za_v(v):
                for j in range(NJ):
                    q, et = divmod(j, KT)
                    b.sS.wait(v, s_zevac_cnt(j))
                    b.sV.inc(v.tensor_copy(ZLNt(et)[:, q * QC:(q + 1) * QC], ZCH[j % 2][:]))
            b.v(_za_v)

            ln_statmath_and_reps(PSU[2], denom=D)

            z2sb, z2vb = b.sS.n, b.sV.n
            zb_vm = {j: z2vb + 2 * (j + 1) for j in range(NJ)}
            zb_sm = {j: z2sb + j + 1 for j in range(NJ)}

            def _zb_v(v):
                for j in range(NJ):
                    q, et = divmod(j, KT)
                    if j == 0:
                        b.sT.wait(v)
                        b.sS.wait(v)
                    if j >= 2:
                        b.sS.wait(v, zb_sm[j - 2])
                    b.sV.inc(v.tensor_mul(ZCH[j % 2][:],
                                          ZLNt(et)[:, q * QC:(q + 1) * QC],
                                          REP[:, q * QC:(q + 1) * QC]))
                    b.sV.wait(v)
                    b.sV.inc(v.tensor_add(ZCH[j % 2][:], ZCH[j % 2][:],
                                          REP[:, T + q * QC: T + (q + 1) * QC]))
            b.v(_zb_v)

            def _zb_s(s):
                for j in range(NJ):
                    q, et = divmod(j, KT)
                    b.sV.wait(s, zb_vm[j])
                    b.sS.inc(s.activation(ZLNt(et)[:, q * QC:(q + 1) * QC], ZCH[j % 2][:],
                                          AF.Identity, bias=BIAS[:, 24 + et:25 + et],
                                          scale=BIAS[:, 16 + et:17 + et]))
            b.s(_zb_s)
            tap("zln", [ZLNt(dt) for dt in range(KT)], KT * 128 * T, BF16)

            pz.close()

            g_atto = nc.sbuf_tensor("ATTO", [128, KT * T], BF16)
            ATTO = g_atto.__enter__()

            def ATTOt(dt):
                return ATTO[:, dt * T:(dt + 1) * T]

            NH, HD = 16, 64
            ph = contextlib.ExitStack()
            WHD = ph.enter_context(nc.sbuf_tensor("WHD", [128, 3 * KT * 128], BF16))
            QRH = ph.enter_context(nc.sbuf_tensor("QRH", [128, T], BF16))
            HV = ph.enter_context(nc.sbuf_tensor("HV", [128, 16 * 128], BF16))
            HC = T // 2
            PH = ph.enter_context(nc.sbuf_tensor("PH", [128, HC], F32))
            SSIN = ph.enter_context(nc.sbuf_tensor("SSIN", [128, HC], BF16))
            SCOS = ph.enter_context(nc.sbuf_tensor("SCOS", [128, HC], BF16))
            QCB = [ph.enter_context(nc.sbuf_tensor(f"QCB{i}", [128, T], BF16)) for i in range(2)]
            RCB = [ph.enter_context(nc.sbuf_tensor(f"RCB{i}", [128, T], BF16)) for i in range(2)]
            RCTM = ph.enter_context(nc.sbuf_tensor("RCTM", [128, 16 * 128], BF16))
            RSC = RCTM[:, :].bitcast(F32)
            GG = ph.enter_context(nc.sbuf_tensor("GG", [128, 128], BF16))
            SCBW = [ph.enter_context(nc.sbuf_tensor(f"SCBW{i}", [128, QC], BF16)) for i in range(2)]
            QG = REP[:, 0:T]
            RNREP = REP[:, T:2 * T]

            wsc_jobs = []
            for ht in range(16):
                wsc_jobs.append((
                    hbm_ap(FSC, FSC_S1 + ht * 131072, [[1024, 128], [128, 8], [1, 128]]),
                    hbm_ap(W["L0_ffn_sw1"], ht * 128, [[2048, 128], [128 * 2048, 8], [1, 128]])))
                wsc_jobs.append((
                    hbm_ap(FSC, FSC_S3 + ht * 131072, [[1024, 128], [128, 8], [1, 128]]),
                    hbm_ap(W["L0_ffn_sw3"], ht * 128, [[2048, 128], [128 * 2048, 8], [1, 128]])))
            for co in range(KT):
                wsc_jobs.append((
                    hbm_ap(FSC, FSC_S2 + co * 262144, [[2048, 128], [128, 16], [1, 128]]),
                    hbm_ap(W["L0_ffn_sw2"], co * 128, [[1024, 128], [128 * 1024, 16], [1, 128]])))
            for e in range(8):
                wsc_jobs.append((
                    hbm_ap(FSC, FSC_W1 + e * 1048576, [[8192, 128], [1024, 8], [1, 1024]]),
                    hbm_ap(W["L0_ffn_w1"], e * D * D, [[1024, 128], [131072, 8], [1, 1024]])))
                wsc_jobs.append((
                    hbm_ap(FSC, FSC_W2 + e * 1048576, [[8192, 128], [1024, 8], [1, 1024]]),
                    hbm_ap(W["L0_ffn_w2"], e * D * D, [[1024, 128], [131072, 8], [1, 1024]])))
            wsc_ffn_jobs = len(wsc_jobs)
            for dri, mtg in ((0, "m1"), (1, "m2")):
                for mdt in range(KT):
                    wsc_jobs.append((
                        hbm_ap(WSC[dri], mdt * 131072, [[1024, 128], [128, 8], [1, 128]]),
                        hbm_ap(W[f"L1_attn_{mtg}_Win"], mdt * 128,
                               [[2048, 128], [128 * 2048, 8], [1, 128]])))
                    wsc_jobs.append((
                        hbm_ap(WSC[dri], 1048576 + mdt * 131072,
                               [[1024, 128], [128, 8], [1, 128]]),
                        hbm_ap(W[f"L1_attn_{mtg}_Win"], 1024 + mdt * 128,
                               [[2048, 128], [128 * 2048, 8], [1, 128]])))
                wsc_jobs.append((
                    hbm_ap(WSC[dri], 2097152, [[512, 1024], [1, 512]]),
                    hbm_ap(W[f"L1_attn_{mtg}_Wout"], 0, [[512, 1024], [1, 512]])))

            wh_marks, hm_marks = {}, {}
            pair_free = {}
            head_snap = b.snap()
            for hp in range(8):
                bq_c = BIAS2[:, 0 + hp:1 + hp]
                br_c = BIAS2[:, 8 + hp:9 + hp]
                bh_c = BIAS2[:, 16 + hp:17 + hp]
                hsem = b.dPA if hp % 2 == 0 else b.dPB

                def _wg(g, hp=hp, hsem=hsem):
                    if hp == 0:
                        b.wait_all(g, head_snap)
                    else:
                        b.sT.wait(g, hm_marks[hp - 1])
                    for wi, nm in enumerate(("L0_attn_Wq", "L0_attn_Wr", "L0_attn_Wh")):
                        g.dma_start(out=WHD[:, wi * KT * 128:(wi + 1) * KT * 128],
                                    in_=hbm_ap(W[nm], hp * 128,
                                               [[D, 128], [128 * D, KT], [1, 128]])).then_inc(hsem.h, 16)
                        hsem.n += 16
                    wh_marks[hp] = (hsem, hsem.n)
                b.g(_wg)

                lo, hi = hp * 16, min((hp + 1) * 16, len(wsc_jobs))
                if lo < hi:
                    def _wsc_g(g, lo=lo, hi=hi):
                        for oa, ia in wsc_jobs[lo:hi]:
                            b.dZ.inc(g.dma_start(out=oa, in_=ia), 16)
                    b.g(_wsc_g)

                def stat_w(wi, kt):
                    return WHD[:, wi * KT * 128 + kt * 128: wi * KT * 128 + (kt + 1) * 128]

                for wi in range(2):
                    bias_c = bq_c if wi == 0 else br_c
                    wm = wh_marks[hp]

                    for half in range(2):
                        hs = slice(half * HC, (half + 1) * HC)

                        def p_mm(t, i, bank, wi=wi, wm=wm, half=half):
                            wm[0].wait(t, wm[1])
                            j = half * 2 + i
                            return mm_group(t, bank[:, :],
                                            [(stat_w(wi, kt), ZLNt(kt)[:, j * QC:(j + 1) * QC])
                                             for kt in range(KT)])

                        def p_ev(s, i, bank, bias_c=bias_c, half=half):
                            j = half * 2 + i
                            return s.activation(QRH[:, j * QC:(j + 1) * QC], bank[:, :],
                                                AF.Identity, bias=bias_c)
                        if wi == 0 and half == 0:
                            mm_evac(2, p_mm, p_ev, [PSU[0], PSU[1]], "s",
                                    pre_marks=pair_free.get(hp - 1))
                        elif wi == 1 and half == 0:
                            mm_evac(2, p_mm, p_ev, [PSU[0], PSU[1]], "s")
                        else:
                            mm_evac(2, p_mm, p_ev, [PSU[0], PSU[1]], "s",
                                    pre_engines=("s",))

                        def phi_mm(t, i, bank, half=half):
                            return mm_group(t, bank[:, :],
                                            [(WPHI2[:],
                                              QRH[:, half * HC + i * QC: half * HC + (i + 1) * QC])])

                        def phi_ev(s, i, bank):
                            return s.activation(PH[:, i * QC:(i + 1) * QC], bank[:, :],
                                                AF.Identity, scale=1.0 / (2 * PI),
                                                bias=BIAS2[:, 24:25])
                        if half == 0:
                            mm_evac(2, phi_mm, phi_ev, [PSU[0], PSU[1]], "s",
                                    pre_engines=("s",))
                        else:
                            mm_evac(2, phi_mm, phi_ev, [PSU[0], PSU[1]], "s")

                        vb0 = b.sV.n
                        sb_t = b.sS.n

                        def _trig_v(v, sb_t=sb_t):
                            b.sS.wait(v)
                            b.sV.inc(v.tensor_scalar(RSC, PH[:], C_MAGIC,
                                                     C_MAGIC, ALU.add, ALU.subtract))
                            b.sV.wait(v)
                            b.sV.inc(v.scalar_tensor_tensor(RSC, RSC, -1.0, PH[:],
                                                            ALU.mult, ALU.add))
                            b.sS.wait(v, sb_t + 1)
                            b.sV.inc(v.tensor_scalar(RSC, PH[:], 0.25, C_MAGIC,
                                                     ALU.add, ALU.add))
                            b.sV.wait(v)
                            b.sV.inc(v.tensor_scalar(RSC, RSC, C_MAGIC, None,
                                                     ALU.subtract))
                            b.sV.wait(v)
                            b.sV.inc(v.scalar_tensor_tensor(RSC, RSC, -1.0, PH[:],
                                                            ALU.mult, ALU.add))

                        def _trig_s(s, vb0=vb0):
                            b.sV.wait(s, vb0 + 2)
                            b.sS.inc(s.activation(SSIN[:], RSC, AF.Sin, scale=2 * PI))
                            b.sV.wait(s, vb0 + 5)
                            b.sS.inc(s.activation(SCOS[:], RSC, AF.Sin, scale=2 * PI,
                                                  bias=C_HALFPI))
                        b.v(_trig_v)
                        b.s(_trig_s)

                        def _qc_v(v, wi=wi, hs=hs):
                            dst = QCB if wi == 0 else RCB
                            b.sS.wait(v)
                            for hh in range(2):
                                qsrc = QRH[64 * hh:64 * hh + 64, hs]
                                b.sV.inc(v.tensor_mul(dst[hh][0:64, hs], qsrc,
                                                      SCOS[64 * hh:64 * hh + 64, :]))
                                b.sV.inc(v.tensor_mul(dst[hh][64:128, hs], qsrc,
                                                      SSIN[64 * hh:64 * hh + 64, :]))
                        b.v(_qc_v)

                wm = wh_marks[hp]

                def h_mm(t, i, bank, wm=wm):
                    wm[0].wait(t, wm[1])
                    if i == 0:
                        b.sV.wait(t)
                    return mm_group(t, bank[:, :],
                                    [(stat_w(2, kt), ZLNt(kt)[:, i * QC:(i + 1) * QC])
                                     for kt in range(KT)])

                def h_ev(s, i, bank, bh_c=bh_c):
                    return s.activation(QRH[:, i * QC:(i + 1) * QC], bank[:, :],
                                        AF.Identity, bias=bh_c)
                mm_evac(NQ, h_mm, h_ev, [PSU[0], PSU[1]], "s")
                hm_marks[hp] = b.sT.n

                def htp_mm(t, i, bank):
                    if i < 2:
                        b.sS.wait(t)
                    return t.transpose(bank[:, 0:128], QRH[:, i * 128:(i + 1) * 128],
                                       EYEB[:])

                def htp_ev(v, i, bank):
                    return v.tensor_copy(HV[:, i * 128:(i + 1) * 128], bank[:, 0:128])
                mm_evac(16, htp_mm, htp_ev, [PSBK[0], PSBK[1]], "v")

                for hh in range(2):
                    qcb, rcb = QCB[hh], RCB[hh]

                    def rtp_mm(t, i, bank, rcb=rcb):
                        return t.transpose(bank[:, 0:128], rcb[:, i * 128:(i + 1) * 128], EYEB[:])

                    def rtp_ev(v, i, bank):
                        return v.tensor_copy(RCTM[:, i * 128:(i + 1) * 128], bank[:, 0:128])
                    mm_evac(16, rtp_mm, rtp_ev, [PSBK[0], PSBK[1]], "v")

                    def g_mm(t, i, bank):
                        return mm_group(t, bank[:, 0:128],
                                        [(RCTM[:, kt * 128:(kt + 1) * 128],
                                          RCTM[:, kt * 128:(kt + 1) * 128]) for kt in range(16)])

                    def g_ev(v, i, bank):
                        return v.tensor_copy(GG[:], bank[:, 0:128])
                    mm_evac(1, g_mm, g_ev, [PSU[0]], "v")

                    def qg_mm(t, i, bank, qcb=qcb):
                        if i == 0:
                            b.sV.wait(t)
                        return mm_group(t, bank[:, :], [(GG[:], qcb[:, i * QC:(i + 1) * QC])])

                    def qg_ev(v, i, bank):
                        return v.tensor_copy(QG[:, i * QC:(i + 1) * QC], bank[:, :])
                    mm_evac(NQ, qg_mm, qg_ev, [PSU[0], PSU[1]], "v")

                    PBB = RCTM[:, 0:T]

                    def _pb_v(v, qcb=qcb):
                        b.sV.wait(v)
                        b.sT.wait(v)
                        b.sV.inc(v.tensor_mul(PBB, QG[:, :], qcb[:]))
                    b.v(_pb_v)

                    def ns_mm(t, i, bank):
                        if i == 0:
                            b.sV.wait(t)
                        return mm_group(t, bank[0:1, :],
                                        [(ONECOL[:], PBB[:, i * QC:(i + 1) * QC])])

                    def ns_ev(s, i, bank):
                        return s.copy(ROW_A[0:1, i * QC:(i + 1) * QC], bank[0:1, :])
                    mm_evac(NQ, ns_mm, ns_ev, [PSU[0], PSU[1]], "s")

                    def rn_mm(t, i, bank):
                        return mm_group(t, bank[:, :],
                                        [(ONESF[:], ROW_A[0:1, i * QC:(i + 1) * QC])])

                    def rn_ev(s, i, bank):
                        return s.copy(RNREP[:, i * QC:(i + 1) * QC], bank[:, :])
                    mm_evac(NQ, rn_mm, rn_ev, [PSU[0], PSU[1]], "s")

                    prev_sc = None
                    for qc in range(NQ):
                        scm, avm, evm2 = {}, {}, {}
                        vcur, scur = b.sV.n, b.sS.n
                        kt2_v, kt2_s = prev_sc if prev_sc else (vcur, scur)
                        for kt in range(16):
                            if kt % 2 == 0:
                                evm2[kt] = (b.sV, vcur + kt // 2 + 1)
                            else:
                                evm2[kt] = (b.sS, scur + kt // 2 + 1)

                        def _sc_t(t, qc=qc, scm=scm, avm=avm, evm2=evm2, vcur=vcur,
                                  kt2_v=kt2_v, kt2_s=kt2_s, qcb=qcb, rcb=rcb, hh=hh):
                            for kt in range(16):
                                if kt < 3:
                                    b.sV.wait(t, kt2_v)
                                    b.sS.wait(t, kt2_s)
                                else:
                                    sm, cnt = evm2[kt - 3]
                                    sm.wait(t, cnt)
                                b.sT.inc(t.matmul((PSU[0], PSU[1], PSU[3])[kt % 3][:, :],
                                                  rcb[:, kt * 128:(kt + 1) * 128],
                                                  qcb[:, qc * QC:(qc + 1) * QC],
                                                  start=True, stop=True))
                                scm[kt] = b.sT.n
                                if kt >= 1:
                                    k2 = kt - 1
                                    if k2 == 0:
                                        b.sV.wait(t, vcur)
                                    sm, cnt = evm2[k2]
                                    sm.wait(t, cnt)
                                    b.sT.inc(t.matmul(
                                        PSU[2][0:64, :],
                                        HV[:, k2 * 128 + 64 * hh: k2 * 128 + 64 * hh + 64],
                                        SCBW[k2 % 2][:],
                                        start=(k2 == 0), stop=False))
                                    avm[k2] = b.sT.n
                            sm, cnt = evm2[15]
                            sm.wait(t, cnt)
                            b.sT.inc(t.matmul(PSU[2][0:64, :],
                                              HV[:, 15 * 128 + 64 * hh: 15 * 128 + 64 * hh + 64],
                                              SCBW[1][:], start=False, stop=True))
                            avm[15] = b.sT.n
                        b.t(_sc_t)

                        def _sc_v(v, scm=scm, avm=avm):
                            for kt in range(0, 16, 2):
                                b.sT.wait(v, scm[kt])
                                if kt >= 2:
                                    b.sT.wait(v, avm[kt - 2])
                                b.sV.inc(v.tensor_copy(SCBW[kt % 2][:],
                                                       (PSU[0], PSU[1], PSU[3])[kt % 3][:, :]))
                        b.v(_sc_v)
                        if hh == 1 and qc == NQ - 1:
                            pair_free[hp] = [(b.sV, b.sV.n)]

                        def _sc_s(s, scm=scm, avm=avm):
                            for kt in range(1, 16, 2):
                                b.sT.wait(s, scm[kt])
                                if kt >= 2:
                                    b.sT.wait(s, avm[kt - 2])
                                b.sS.inc(s.copy(SCBW[kt % 2][:],
                                                (PSU[0], PSU[1], PSU[3])[kt % 3][:, :]))
                        b.s(_sc_s)
                        prev_sc = (b.sV.n, b.sS.n)

                        def _rnq_s(s, qc=qc):
                            b.sS.wait(s)
                            b.sS.inc(s.activation(RNREP[:, qc * QC:(qc + 1) * QC],
                                                  RNREP[:, qc * QC:(qc + 1) * QC], AF.Sqrt))
                        b.s(_rnq_s)
                        rnq_m = b.sS.n

                        def _rnq_v(v, qc=qc, rnq_m=rnq_m):
                            b.sS.wait(v, rnq_m)
                            b.sV.inc(v.tensor_scalar(RNREP[:, qc * QC:(qc + 1) * QC],
                                                     RNREP[:, qc * QC:(qc + 1) * QC],
                                                     C_TINY, None, ALU.max))
                            b.sV.wait(v)
                            b.sV.inc(v.reciprocal(RNREP[:, qc * QC:(qc + 1) * QC],
                                                  RNREP[:, qc * QC:(qc + 1) * QC]))
                        b.v(_rnq_v)
                        if hh == 1 and qc == NQ - 1:
                            pair_free[hp].append((b.sS, b.sS.n))

                        def _av_ev(v, qc=qc, hh=hh, hp=hp):
                            b.sT.wait(v)
                            b.sS.wait(v)
                            b.sV.wait(v)
                            b.sV.inc(v.tensor_mul(
                                ATTO[64 * hh:64 * hh + 64,
                                     hp * T + qc * QC: hp * T + (qc + 1) * QC],
                                PSU[2][0:64, :],
                                RNREP[64 * hh:64 * hh + 64, qc * QC:(qc + 1) * QC]))
                        b.v(_av_ev)
            wsc_total = b.dZ.n
            assert wsc_total == 16 * len(wsc_jobs)
            wsc_ffn_total = wsc_total
            ph.close()
            tap("atto", [ATTOt(dt) for dt in range(KT)], KT * 128 * T, BF16)

            pwo = contextlib.ExitStack()
            WOB = [pwo.enter_context(nc.sbuf_tensor(f"WOB{i}", [128, KT * 128], BF16)) for i in range(2)]
            wo_marks = {}
            wo_tb = b.sT.n
            wo_tmarks = {et: wo_tb + (et + 1) * NQ for et in range(8)}

            wo_snap = b.snap()

            def _wo_g(g):
                for et in range(8):
                    osem = b.dPA if et % 2 == 0 else b.dPB
                    if et < 2:
                        b.wait_all(g, wo_snap)
                    else:
                        b.sT.wait(g, wo_tmarks[et - 2])
                    g.dma_start(out=WOB[et % 2][:],
                                in_=hbm_ap(W["L0_attn_Wo"], et * 128,
                                           [[D, 128], [128 * D, KT], [1, 128]])).then_inc(osem.h, 16)
                    osem.n += 16
                    wo_marks[et] = (osem, osem.n)
            b.g(_wo_g)

            def wo_mm(t, i, bank):
                et, qc = divmod(i, NQ)
                wo_marks[et][0].wait(t, wo_marks[et][1])
                return mm_group(t, bank[:, :],
                                [(WOB[et % 2][:, kt * 128:(kt + 1) * 128],
                                  ATTOt(kt)[:, qc * QC:(qc + 1) * QC]) for kt in range(KT)])

            def wo_ev(v, i, bank):
                et, qc = divmod(i, NQ)
                return v.scalar_tensor_tensor(Xt(et)[:, qc * QC:(qc + 1) * QC], bank[:, :],
                                              BIAS[:, 8 + et:9 + et],
                                              Xt(et)[:, qc * QC:(qc + 1) * QC],
                                              ALU.add, ALU.add)
            mm_evac(8 * NQ, wo_mm, wo_ev, [PSU[0], PSU[1]], "v")
            pwo.close()
            g_atto.__exit__(None, None, None)
            tap("x1", [Xt(dt) for dt in range(KT)], KT * 128 * T, F32)

            p1.close()

            p2 = contextlib.ExitStack()
            PSU = [p2.enter_context(nc.psum_tensor(f"P2U{i}", [128, 512], F32)) for i in range(4)]
            FB = p2.enter_context(nc.sbuf_tensor("FB", [128, 48], F32))
            B1C = p2.enter_context(nc.sbuf_tensor("B1C", [128, 64], F32))
            B2C = p2.enter_context(nc.sbuf_tensor("B2C", [128, 64], F32))
            YLN = p2.enter_context(nc.sbuf_tensor("YLN", [128, KT * T], BF16))
            SCR2 = p2.enter_context(nc.sbuf_tensor("SCR2", [128, T], F32))
            g_stb2 = [nc.sbuf_tensor(f"STB2{i}", [128, QC], BF16) for i in range(4)]
            STB2 = [g.__enter__() for g in g_stb2]

            def YLNt(dt):
                return YLN[:, dt * T:(dt + 1) * T]

            load(FB[:, 0:16], col_ap("L0_ffn_sb1", 2048), slow=True, pre=b.snap())
            load(FB[:, 16:32], col_ap("L0_ffn_sb3", 2048), slow=True)
            load(FB[:, 32:40], col_ap("L0_ffn_sb2", D), slow=True)
            for e in range(8):
                load(B1C[:, e * 8:(e + 1) * 8],
                     hbm_ap(W["L0_ffn_b1"], e * D, [[1, 128], [128, 8]]), slow=True)
                load(B2C[:, e * 8:(e + 1) * 8],
                     hbm_ap(W["L0_ffn_b2"], e * D, [[1, 128], [128, 8]]), slow=True)
            wld_fb = b.dW.n

            def b1col(e, kt):
                return B1C[:, e * 8 + kt: e * 8 + kt + 1]

            def b2col(e, kt):
                return B2C[:, e * 8 + kt: e * 8 + kt + 1]

            ln_stats_resident(Xt, PSU[0], PSU[1], STB2)
            ln_statmath_and_reps(PSU[2])
            ln_normalize(Xt, YLNt, SCR2)
            tap("yln", [YLNt(dt) for dt in range(KT)], KT * 128 * T, BF16)
            for g_ in reversed(g_stb2):
                g_.__exit__(None, None, None)

            psh = contextlib.ExitStack()
            SW1H = [psh.enter_context(nc.sbuf_tensor(f"SW1H{i}", [128, KT * 128], BF16)) for i in range(4)]
            SW3H = [psh.enter_context(nc.sbuf_tensor(f"SW3H{i}", [128, KT * 128], BF16)) for i in range(4)]
            SW2C = [psh.enter_context(nc.sbuf_tensor(f"SW2C{i}", [128, 16 * 128], BF16)) for i in range(4)]
            H1B = psh.enter_context(nc.sbuf_tensor("H1B", [128, QC], F32))
            H3B = psh.enter_context(nc.sbuf_tensor("H3B", [128, QC], F32))
            GATE = psh.enter_context(nc.sbuf_tensor("GATE", [128, 16 * QC], BF16))

            sh_snap = b.snap()
            NHT = 16
            for qc in range(NQ):
                tmk, smk, vmk, wmk = {}, {}, {}, {}
                tb3, sb3, vb3 = b.sT.n, b.sS.n, b.sV.n

                def _wA(g, qc=qc, wmk=wmk):
                    for ht in range(NHT):
                        if qc == 0 and ht == 0:
                            b.dZ.wait(g, wsc_ffn_total)
                        if qc == 0 and ht < 4:
                            b.wait_all(g, sh_snap)
                        else:
                            b.sT.wait(g, tb3 + 2 * (ht - 3) if ht >= 4 else prev_tmk_last)
                        wmk[ht] = [ydma(g, dst[ht % 4][:],
                                        hbm_ap(FSC, base + ht * 131072,
                                               [[1024, 128], [128, 8], [1, 128]]))
                                   for base, dst in ((FSC_S1, SW1H), (FSC_S3, SW3H))]
                b.y(_wA)

                def _A_t(t, qc=qc, tmk=tmk, smk=smk, vmk=vmk, wmk=wmk, sb3=sb3, vb3=vb3):
                    for ht in range(NHT):
                        wait_marks(t, wmk[ht])
                        if qc == 0 and ht == 0:
                            b.wait_all(t, sh_snap)
                        if ht >= 2:
                            b.sS.wait(t, sb3 + 2 * (ht - 1))
                        elif qc > 0 and ht == 0:
                            b.sV.wait(t, vb3)
                            b.sS.wait(t, sb3)
                        b.sT.inc(mm_group(t, PSU[ht % 2][:, :],
                                 [(SW1H[ht % 4][:, kt * 128:(kt + 1) * 128],
                                   YLNt(kt)[:, qc * QC:(qc + 1) * QC]) for kt in range(KT)]))
                        b.sT.inc(mm_group(t, PSU[2 + ht % 2][:, :],
                                 [(SW3H[ht % 4][:, kt * 128:(kt + 1) * 128],
                                   YLNt(kt)[:, qc * QC:(qc + 1) * QC]) for kt in range(KT)]))
                        tmk[ht] = b.sT.n

                def _A_s(s, qc=qc, tmk=tmk, smk=smk, vmk=vmk, vb3=vb3):
                    for ht in range(NHT):
                        b.sT.wait(s, tmk[ht])
                        if ht >= 1:
                            b.sV.wait(s, vb3 + ht)
                        b.sS.inc(s.activation(H1B[:], PSU[ht % 2][:, :], AF.Silu,
                                              bias=FB[:, ht:ht + 1]))
                        b.sS.inc(s.activation(H3B[:], PSU[2 + ht % 2][:, :], AF.Identity,
                                              bias=FB[:, 16 + ht:17 + ht]))
                        smk[ht] = b.sS.n

                def _A_v(v, qc=qc, smk=smk, vmk=vmk):
                    for ht in range(NHT):
                        b.sS.wait(v, smk[ht])
                        b.sV.inc(v.tensor_mul(GATE[:, ht * QC:(ht + 1) * QC], H1B[:], H3B[:]))
                        vmk[ht] = b.sV.n
                for ht in range(NHT):
                    smk[ht] = sb3 + 2 * (ht + 1)
                    vmk[ht] = vb3 + ht + 1
                b.t(_A_t)
                b.s(_A_s)
                b.v(_A_v)
                prev_tmk_last = tmk.get(NHT - 1, b.sT.n)

                wmk2 = {}
                tb4, vb4 = b.sT.n, b.sV.n

                def _wB(g, qc=qc, wmk2=wmk2, tb4=tb4):
                    for co in range(KT):
                        if qc == 0 and co < 4:
                            b.wait_all(g, sh_snap)
                        elif co >= 4:
                            b.sT.wait(g, tb4 + co - 3)
                        else:
                            b.sT.wait(g, tb4)
                        wmk2[co] = [ydma(g, SW2C[co % 4][:],
                                         hbm_ap(FSC, FSC_S2 + co * 262144,
                                                [[2048, 128], [128, 16], [1, 128]]))]
                b.y(_wB)

                def _B_t(t, qc=qc, wmk2=wmk2, vb4=vb4):
                    b.sV.wait(t, vb4)
                    for co in range(KT):
                        wait_marks(t, wmk2[co])
                        if co >= 2:
                            b.sV.wait(t, vb4 + 2 * (co - 1))
                        b.sT.inc(mm_group(t, PSU[co % 2][:, :],
                                 [(SW2C[co % 4][:, ht * 128:(ht + 1) * 128],
                                   GATE[:, ht * QC:(ht + 1) * QC]) for ht in range(NHT)]))

                def _B_v(v, qc=qc, tb4=tb4):
                    for co in range(KT):
                        b.sT.wait(v, tb4 + co + 1)
                        if co >= 1:
                            b.sV.wait(v)
                        b.sV.inc(v.tensor_scalar(SCR2[:, 0:QC], PSU[co % 2][:, :],
                                                 FB[:, 32 + co:33 + co], 0.5,
                                                 ALU.add, ALU.mult))
                        b.sV.wait(v)
                        b.sV.inc(v.tensor_add(Xt(co)[:, qc * QC:(qc + 1) * QC],
                                              SCR2[:, 0:QC],
                                              Xt(co)[:, qc * QC:(qc + 1) * QC]))
                b.t(_B_t)
                b.v(_B_v)
            psh.close()
            tap("x_sh", [Xt(dt) for dt in range(KT)], KT * 128 * T, F32)

            pmo = contextlib.ExitStack()
            W1E = [pmo.enter_context(nc.sbuf_tensor(f"W1E{i}", [128, KT * 1024], BF16)) for i in range(2)]
            W2E = pmo.enter_context(nc.sbuf_tensor("W2E", [128, KT * 1024], BF16))
            HE = pmo.enter_context(nc.sbuf_tensor("HE0", [128, KT * 512], BF16))
            moe_snap = b.snap()
            euse, huse = {}, {}

            for e in range(8):
                hsem = b.dPA if e % 2 == 0 else b.dPB

                def _we(g, e=e):
                    if e < 2:
                        b.wait_all(g, moe_snap)
                    else:
                        b.sT.wait(g, huse[e - 2])
                    m1 = [ydma(g, W1E[e % 2][:],
                               hbm_ap(FSC, FSC_W1 + e * 1048576,
                                      [[8192, 128], [1024, 8], [1, 1024]]))]
                    if e >= 1:
                        b.sT.wait(g, euse[e - 1])
                    m2 = [ydma(g, W2E[:],
                               hbm_ap(FSC, FSC_W2 + e * 1048576,
                                      [[8192, 128], [1024, 8], [1, 1024]]))]
                    _we.mark = (m1, m2)
                b.y(_we)
                wmark_w1, wmark_w2 = _we.mark

                g0 = (3 * e) % 8
                g1 = (3 * e + 3) % 8
                dg = g1 - g0

                def mvy(kt):
                    return bass.AP(YLN.tensor if hasattr(YLN, "tensor") else YLN,
                                   kt * T + g0, [[KT * T, 128], [dg, 2], [8, 256]])

                def h1_mm(t, i, bank, e=e, wmark=wmark_w1):
                    wait_marks(t, wmark)
                    return mm_group(t, bank[:, :],
                                    [(W1E[e % 2][:, kt * 1024 + i * 128:
                                                 kt * 1024 + (i + 1) * 128],
                                      mvy(kt)) for kt in range(KT)])

                def h1_ev(s, i, bank, e=e):
                    return s.activation(HE[:, i * 512:(i + 1) * 512], bank[:, :],
                                        AF.Gelu, bias=b1col(e, i))
                mm_evac(KT, h1_mm, h1_ev, [PSU[0], PSU[1]], "s", pre_engines=("s",))
                huse[e] = b.sT.n

                def y_mm(t, i, bank, e=e, wmark=wmark_w2):
                    if i == 0:
                        wait_marks(t, wmark)
                        b.sS.wait(t)
                    return mm_group(t, bank[:, :],
                                    [(W2E[:, kt * 1024 + i * 128:
                                          kt * 1024 + (i + 1) * 128],
                                      HE[:, kt * 512:(kt + 1) * 512]) for kt in range(KT)])

                def y_ev(v, i, bank, e=e, g0=g0, g1=g1):
                    for gb, grp in ((0, g0), (1, g1)):
                        if i >= 1 or gb == 1:
                            b.sV.wait(v)
                        b.sV.inc(v.tensor_scalar(SCR2[:, 0:256],
                                                 bank[:, gb * 256:(gb + 1) * 256],
                                                 b2col(e, i), 0.25, ALU.add, ALU.mult))
                        b.sV.wait(v)
                        inst = v.tensor_add(Xt(i)[:, grp::8], SCR2[:, 0:256],
                                            Xt(i)[:, grp::8])
                        if gb == 0:
                            b.sV.inc(inst)
                    return inst
                mm_evac(KT, y_mm, y_ev, [PSU[2], PSU[3]], "v", evac_incs=4, pre_engines=("v",))
                euse[e] = b.sT.n
            pmo.close()
            p2.close()
            tap("x2", [Xt(dt) for dt in range(KT)], KT * 128 * T, F32)

            p3 = contextlib.ExitStack()
            PSU = [p3.enter_context(nc.psum_tensor(f"P3U{i}", [128, 512], F32)) for i in range(8)]
            MLG = p3.enter_context(nc.sbuf_tensor("MLG", [128, 2 * KT], F32))
            ACOL = p3.enter_context(nc.sbuf_tensor("ACOL", [128, 128], F32))
            WCV = p3.enter_context(nc.sbuf_tensor("WCV", [128, 32], F32))
            MBC = p3.enter_context(nc.sbuf_tensor("MBC", [128, 3 * KT], F32))
            WDTB = p3.enter_context(nc.sbuf_tensor("WDTB", [32, 1024], BF16))
            WXB = p3.enter_context(nc.sbuf_tensor("WXB", [128, KT * 64], BF16))
            HLN = p3.enter_context(nc.sbuf_tensor("HLN", [128, KT * T], BF16))

            def HLNt(dt):
                return HLN[:, dt * T:(dt + 1) * T]

            pre3 = b.snap()
            load(MLG[:, 0:KT], col_ap("L1_attn_ln_g", D), slow=True, pre=pre3)
            load(MLG[:, KT:2 * KT], col_ap("L1_attn_ln_b", D), slow=True)
            wld_m = b.dW.n

            ONEROW = p3.enter_context(nc.sbuf_tensor("ONEROW", [1, 128], BF16))

            def _m_init(v):
                b.sV.inc(v.memset(CONST[:, 4:5], 1.0))
            b.v(_m_init)
            C_ONE = CONST[:, 4:5]

            def dir_param_load(dr, tg):
                def _or_v(v):
                    b.sS.wait(v)
                    b.sV.inc(v.memset(ONEROW[:], 1.0))
                b.v(_or_v)
                prd = b.snap()
                load(ACOL[:], hbm_ap(W[f"L1_attn_{tg}_Alog"], 0,
                                     [[16, 128], [128 * 16, KT], [1, 16]]),
                     slow=True, pre=prd)
                load(WCV[:],
                     hbm_ap(W[f"L1_attn_{tg}_Wconv"], 0, [[4, 128], [128 * 4, KT], [1, 4]]),
                     slow=True)
                load(MBC[:, 0:KT], col_ap(f"L1_attn_{tg}_bconv", D), slow=True)
                load(MBC[:, KT:2 * KT], col_ap(f"L1_attn_{tg}_bdt", D), slow=True)
                load(MBC[:, 2 * KT:3 * KT], col_ap(f"L1_attn_{tg}_Dp", D), slow=True)
                load(WDTB[:], hbm_ap(W[f"L1_attn_{tg}_Wdt"], 0,
                                     [[1024, 32], [1024 * 32, 1], [1, 1024]]))
                load(WXB[:], hbm_ap(W[f"L1_attn_{tg}_Wx"], 0,
                                    [[64, 128], [128 * 64, KT], [1, 64]]))
                mark = b.dW.n

                def _acol_s(s):
                    b.dW.wait(s, mark)
                    b.sS.inc(s.activation(ACOL[:], ACOL[:], AF.Exp))
                b.s(_acol_s)

                def _acol_v(v):
                    b.sS.wait(v)
                    b.sV.inc(v.tensor_scalar(ACOL[:], ACOL[:], -1.0, None, ALU.mult))
                b.v(_acol_v)
                return mark

            p3a = contextlib.ExitStack()
            SCR3 = p3a.enter_context(nc.sbuf_tensor("SCR3", [128, T], F32))
            STB3 = [p3a.enter_context(nc.sbuf_tensor(f"STB3{i}", [128, QC], BF16)) for i in range(4)]
            R4SV = p3a.enter_context(nc.sbuf_tensor("R4SV", [1, 2 * T], F32))
            USCR = [p3a.enter_context(nc.sbuf_tensor(f"USCR{i}", [128, QC], F32)) for i in range(2)]

            ln_stats_resident(Xt, PSU[0], PSU[1], STB3)
            ln_statmath_and_reps(PSU[2])

            def _sv_rows(s):
                b.sS.wait(s)
                b.sS.inc(s.copy(R4SV[0:1, 0:T], ROW_R0))
                b.sS.wait(s)
                b.sS.inc(s.copy(R4SV[0:1, T:2 * T], ROW_R1))
            b.s(_sv_rows)

            u_tb = b.sT.n
            u_vm, u_sm = {}, {}
            usnap = b.snap()

            def _u_v(v):
                b.wait_all(v, usnap)
                for j in range(NQ * KT):
                    q, dt = divmod(j, KT)
                    if j >= 2:
                        b.sT.wait(v, u_tb + 2 * (j - 1))
                    b.sV.inc(v.tensor_mul(USCR[j % 2][:], Xt(dt)[:, q * QC:(q + 1) * QC],
                                          REP[:, q * QC:(q + 1) * QC]))
                    b.sV.wait(v)
                    b.sV.inc(v.tensor_add(USCR[j % 2][:], USCR[j % 2][:],
                                          REP[:, T + q * QC: T + (q + 1) * QC]))
                    b.sV.wait(v)
                    b.sV.inc(v.tensor_copy(STB3[2 * (j % 2) + 1][:], USCR[j % 2][:]))
                    u_vm[j] = b.sV.n
            b.v(_u_v)

            def _u_s(s):
                for j in range(NQ * KT):
                    q, dt = divmod(j, KT)
                    b.sV.wait(s, u_vm[j])
                    b.sS.inc(s.activation(STB3[2 * (j % 2)][:], USCR[j % 2][:], AF.Square))
                    u_sm[j] = b.sS.n
                    if dt == KT - 1:
                        b.sT.wait(s, u_tb + 2 * (j + 1))
                        b.sS.inc(s.copy(ROW_A[0:1, q * QC:(q + 1) * QC], PSU[0][0:1, :]))
                        b.sS.inc(s.copy(ROW_B[0:1, q * QC:(q + 1) * QC], PSU[1][0:1, :]))
            b.s(_u_s)

            def _u_t(t):
                for j in range(NQ * KT):
                    q, dt = divmod(j, KT)
                    b.sV.wait(t, u_vm[j])
                    b.sS.wait(t, u_sm[j])
                    if dt == 0 and q >= 1:
                        b.sS.wait(t, u_sm[j])
                    b.sT.inc(t.matmul(PSU[0][0:1, :], ONECOL[:], STB3[2 * (j % 2) + 1][:],
                                      start=(dt == 0), stop=(dt == KT - 1)))
                    b.sT.inc(t.matmul(PSU[1][0:1, :], ONECOL[:], STB3[2 * (j % 2)][:],
                                      start=(dt == 0), stop=(dt == KT - 1)))
            b.t(_u_t)

            def _m5_s(s):
                b.sS.wait(s)
                b.sS.inc(s.activation(ROW_R0, ROW_A, AF.Identity, scale=1.0 / D))
                b.sS.wait(s)
                b.sS.inc(s.activation(ROW_R1, ROW_R0, AF.Square))
            b.s(_m5_s)

            def _m5_v(v):
                b.sS.wait(v)
                b.sV.inc(v.tensor_scalar(ROW_B, ROW_B, 1.0 / D, None, ALU.mult))
                b.sV.wait(v)
                b.sV.inc(v.tensor_sub(ROW_B, ROW_B, ROW_R1))
            b.v(_m5_v)

            def _m5_s2(s):
                b.sV.wait(s)
                b.sS.inc(s.activation(ROW_A, ROW_B, AF.Sqrt, bias=C_EPS[0:1, :]))
            b.s(_m5_s2)

            def _m5_v2(v):
                b.sS.wait(v)
                b.sV.inc(v.reciprocal(ROW_B, ROW_A))
                b.sV.wait(v)
                b.sV.inc(v.tensor_mul(ROW_A, ROW_B, R4SV[0:1, 0:T]))
                b.sV.inc(v.tensor_sub(ROW_R1, R4SV[0:1, T:2 * T], ROW_R0))
                b.sV.wait(v)
                b.sV.inc(v.tensor_mul(ROW_R1, ROW_R1, ROW_B))
            b.v(_m5_v2)

            def rep45_mm(t, i, bank):
                if i == 0:
                    b.sV.wait(t)
                k, q = divmod(i, NQ)
                row = ROW_A if k == 0 else ROW_R1
                return mm_group(t, bank[:, :], [(ONESF[:], row[0:1, q * QC:(q + 1) * QC])])

            def rep45_ev(s, i, bank):
                k, q = divmod(i, NQ)
                return s.copy(REP[:, k * T + q * QC: k * T + (q + 1) * QC], bank[:, :])
            mm_evac(2 * NQ, rep45_mm, rep45_ev, [PSU[0], PSU[1]], "s")

            ln_normalize(Xt, HLNt, SCR3,
                         gcol=lambda dt: MLG[:, dt:dt + 1],
                         bcol=lambda dt: MLG[:, KT + dt:KT + dt + 1])
            tap("hln", [HLNt(dt) for dt in range(KT)], KT * 128 * T, BF16)

            def rep4_mm(t, i, bank):
                if i == 0:
                    b.sV.wait(t)
                    b.sS.wait(t)
                k, q = divmod(i, NQ)
                row = R4SV[0:1, 0:T] if k == 0 else R4SV[0:1, T:2 * T]
                return mm_group(t, bank[:, :], [(ONESF[:], row[0:1, q * QC:(q + 1) * QC])])

            def rep4_ev(s, i, bank):
                k, q = divmod(i, NQ)
                return s.copy(REP[:, k * T + q * QC: k * T + (q + 1) * QC], bank[:, :])
            mm_evac(2 * NQ, rep4_mm, rep4_ev, [PSU[0], PSU[1]], "s")
            p3a.close()

            p3b = contextlib.ExitStack()
            WINS = [p3b.enter_context(nc.sbuf_tensor(f"WINS{i}", [128, KT * 128], BF16)) for i in range(3)]
            WINZB = p3b.enter_context(nc.sbuf_tensor("WINZ0", [128, 2 * KT * 128], BF16))
            WINZ = [WINZB[:, 0:KT * 128], WINZB[:, KT * 128:2 * KT * 128]]
            WOUTS = [p3b.enter_context(nc.sbuf_tensor(f"WOUTS{i}", [128, 512], BF16)) for i in range(2)]
            ZT0 = p3b.enter_context(nc.sbuf_tensor("ZT0", [128, QC], BF16))
            ZT = [ZT0, ZT0]
            XS_CH = p3b.enter_context(nc.sbuf_tensor("XS_CH", [128, KT * QC], BF16))
            DT_CH = p3b.enter_context(nc.sbuf_tensor("DT_CH", [128, KT * QC], BF16))
            U_CH = p3b.enter_context(nc.sbuf_tensor("U_CH", [128, KT * QC], BF16))
            DBCC = p3b.enter_context(nc.sbuf_tensor("DBCC", [64, QC], BF16))
            XZH = p3b.enter_context(nc.sbuf_tensor("XZH", [128, KT * 3], F32))
            XCC = p3b.enter_context(nc.sbuf_tensor("XCC", [128, QC], F32))
            BSTG_B = SROW[0:1, 0:2048].bitcast(BF16)
            BSTG_C = SROW[0:1, 2048:4096].bitcast(BF16)
            BREP = p3b.enter_context(nc.sbuf_tensor("BREP", [128, 8 * QC], BF16))
            CREP = p3b.enter_context(nc.sbuf_tensor("CREP", [128, 8 * QC], BF16))
            DAP = [p3b.enter_context(nc.sbuf_tensor(f"DAP{i}", [128, QC], BF16)) for i in range(4)]
            DBXP = [p3b.enter_context(nc.sbuf_tensor(f"DBXP{i}", [128, QC], BF16)) for i in range(2)]
            HBF = [p3b.enter_context(nc.sbuf_tensor("HBF0", [128, QC], BF16))] * 2
            CARRY = p3b.enter_context(nc.sbuf_tensor("CARRY", [128, 128], BF16))
            YSB = p3b.enter_context(nc.sbuf_tensor("YSB", [128, KT * QC], BF16))
            SCR4 = p3b.enter_context(nc.sbuf_tensor("SCR4", [128, QC], F32))

            def emit_dir(dr, tg):
                wdt_mark = dir_param_load(dr, tg)

                def mv_h(kt, c0, cn):
                    if dr == 0:
                        return HLNt(kt)[:, c0:c0 + cn]
                    return HLNt(kt)[:, ::-1][:, c0:c0 + cn]

                def _c_init(v):
                    b.sV.inc(v.memset(XZH[:], 0.0))
                    b.sV.inc(v.memset(CARRY[:], 0.0))
                b.v(_c_init)

                for ch in range(NQ):
                    c0 = ch * QC
                    st_tm, st_sm, st_vm, st_wm = {}, {}, {}, {}
                    s1snap = b.snap()
                    tb6, sb6, vb6 = b.sT.n, b.sS.n, b.sV.n

                    def _s1_g(g, ch=ch, dr=dr, st_wm=st_wm, tb6=tb6):
                        for dt in range(KT):
                            if dr == 0 and ch == 0 and dt == 0:
                                b.dZ.wait(g, wsc_total)
                            if dr == 0 and ch == 0 and dt < 2:
                                b.wait_all(g, s1snap)
                            elif dt >= 3:
                                b.sT.wait(g, tb6 + (dt - 3) + 1)
                            else:
                                b.sT.wait(g, s1_free[0])
                            st_wm[dt] = [ydma(g, WINS[dt % 3][:],
                                              hbm_ap(WSC[dr], dt * 131072,
                                                     [[1024, 128], [128, 8], [1, 128]]))]
                    b.y(_s1_g)

                    XZB = [PSU[0], PSU[1], PSU[3]]

                    def _s1_t(t, ch=ch, dr=dr, st_wm=st_wm, st_tm=st_tm, st_vm=st_vm, c0=c0):
                        for dt in range(KT):
                            wait_marks(t, st_wm[dt])
                            if dt >= 3:
                                b.sV.wait(t, st_vm[dt - 3])
                            elif ch > 0 or dr > 0:
                                b.sV.wait(t, vb6)
                                b.sS.wait(t, sb6)
                            b.sT.inc(mm_group(t, XZB[dt % 3][:, :],
                                              [(WINS[dt % 3][:, kt * 128:(kt + 1) * 128],
                                                mv_h(kt, c0, QC)) for kt in range(KT)]))
                            st_tm[dt] = b.sT.n

                    def _s1_t2(t, st_sm=st_sm):
                        for dt in range(KT):
                            b.sS.wait(t, st_sm[dt])
                            b.sT.inc(t.matmul(PSU[2][0:64, :],
                                              WXB[:, dt * 64:(dt + 1) * 64],
                                              XS_CH[:, dt * QC:(dt + 1) * QC],
                                              start=(dt == 0), stop=(dt == KT - 1)))
                    for dt in range(KT):
                        st_sm[dt] = sb6 + dt + 1
                        st_vm[dt] = vb6 + 8 * (dt + 1)
                    b.t(_s1_t)
                    b.t(_s1_t2)
                    s1_free[0] = b.sT.n
                    s1_vlast[0] = vb6 + 8 * KT

                    def _s1_v(v, ch=ch, dr=dr, st_tm=st_tm, st_vm=st_vm, c0=c0):
                        for dt in range(KT):
                            XC = XCC
                            b.sT.wait(v, st_tm[dt])
                            if dt == 0:
                                b.sV.wait(v)
                                b.sS.wait(v)
                            if dt >= 1:
                                b.sS.wait(v, sb6 + dt)
                            ps = (PSU[0], PSU[1], PSU[3])[dt % 3]
                            wc = lambda k: WCV[:, dt * 4 + k: dt * 4 + k + 1]
                            b.sV.inc(v.tensor_scalar(XC[:], ps[:, :], wc(3), None, ALU.mult))
                            for k in range(3):
                                sh = 3 - k
                                b.sV.wait(v)
                                b.sV.inc(v.scalar_tensor_tensor(XC[:, sh:QC], ps[:, 0:QC - sh],
                                                                wc(k), XC[:, sh:QC],
                                                                ALU.mult, ALU.add))
                                b.sV.wait(v)
                                b.sV.inc(v.scalar_tensor_tensor(XC[:, 0:sh],
                                                                XZH[:, dt * 3 + k: dt * 3 + k + sh],
                                                                wc(k), XC[:, 0:sh],
                                                                ALU.mult, ALU.add))
                            b.sV.wait(v)
                            b.sV.inc(v.tensor_copy(XZH[:, dt * 3: dt * 3 + 3], ps[:, QC - 3:QC]))
                            st_vm[dt] = b.sV.n
                    b.v(_s1_v)

                    def _s1_s(s, ch=ch, dr=dr, st_vm=st_vm):
                        for dt in range(KT):
                            b.sV.wait(s, st_vm[dt])
                            b.sS.inc(s.activation(XS_CH[:, dt * QC:(dt + 1) * QC], XCC[:],
                                                  AF.Silu,
                                                  bias=MBC[:, dt: dt + 1]))
                    b.s(_s1_s)

                    def _dbc_s(s):
                        b.sT.wait(s)
                        b.sS.inc(s.copy(DBCC[:], PSU[2][0:64, :]))
                    b.s(_dbc_s)
                    dbc_m = b.sS.n

                    dtmm_m, abs_m, sp_m, dtv1_m, dtv2_m = {}, {}, {}, {}, {}
                    da_m, tts_m, ym_m, ymm_m = {}, {}, {}, {}
                    yev_m, gate_m, gate3_m = {}, {}, {}

                    def emit_dtmm(j):
                        def _t(t, j=j):
                            if j < 2:
                                b.sS.wait(t)
                                b.dW.wait(t, wdt_mark)
                                b.sV.wait(t, s1_vlast[0])
                                if (6 + j) in dtv1_m:
                                    b.sV.wait(t, dtv1_m[6 + j])
                            else:
                                b.sV.wait(t, dtv1_m[j - 2])
                                b.sS.wait(t, abs_m[j - 2])
                            b.sT.inc(mm_group(t, PSU[2 + j % 2][:, :],
                                              [(WDTB[:, j * 128:(j + 1) * 128],
                                                DBCC[0:32, :])]))
                            dtmm_m[j] = b.sT.n
                        b.t(_t)

                    def emit_softplus(j):
                        def _s(s, j=j):
                            b.sT.wait(s, dtmm_m[j])
                            if j == 0:
                                b.sV.wait(s)
                            else:
                                b.sV.wait(s, dtv2_m[j - 1])
                            bcol = MBC[:, KT + j: KT + j + 1]
                            b.sS.inc(s.activation(SCR4[:], PSU[2 + j % 2][:, :], AF.Abs,
                                                  bias=bcol))
                            abs_m[j] = b.sS.n
                            b.sS.wait(s)
                            b.sS.inc(s.activation(SCR4[:], SCR4[:], AF.Exp, scale=-1.0))
                            b.sS.wait(s)
                            b.sS.inc(s.activation(SCR4[:], SCR4[:], AF.Ln, bias=C_ONE))
                            sp_m[j] = b.sS.n
                        b.s(_s)

                    def emit_dtphv(j):
                        def _v(v, j=j):
                            bcol = MBC[:, KT + j: KT + j + 1]
                            b.sS.wait(v, sp_m[j])
                            b.sT.wait(v, dtmm_m[j])
                            b.sV.inc(v.tensor_scalar(XCC[:], PSU[2 + j % 2][:, :], bcol, 0.0,
                                                     ALU.add, ALU.max))
                            assert dtv1_m.get(j, b.sV.n) == b.sV.n
                            dtv1_m[j] = b.sV.n
                            b.sV.wait(v)
                            b.sV.inc(v.tensor_add(DT_CH[:, j * QC:(j + 1) * QC], XCC[:], SCR4[:]))
                            dtv2_m[j] = b.sV.n
                            b.sV.wait(v)
                            b.sV.inc(v.tensor_mul(U_CH[:, j * QC:(j + 1) * QC],
                                                  DT_CH[:, j * QC:(j + 1) * QC],
                                                  XS_CH[:, j * QC:(j + 1) * QC]))
                        b.v(_v)

                    zw_m, zt_m, zsl_m = {}, {}, {}
                    for npass in range(2):
                        rep_vcap = b.sV.n
                        if npass == 1:
                            def _zw0(g, dr=dr):
                                b.sT.wait(g)
                                for j in (0, 1):
                                    zw_m[j] = [ydma(g, WINZ[j][:],
                                                    hbm_ap(WSC[dr], 1048576 + j * 131072,
                                                           [[1024, 128], [128, 8], [1, 128]]))]
                            b.y(_zw0)
                        if npass == 0:
                            emit_dtmm(0)
                            emit_dtmm(1)
                            emit_softplus(0)
                            emit_dtphv(0)
                            emit_softplus(1)


                        rp_gm = {}
                        tb8, sb8, vb8 = b.sT.n, b.sS.n, b.sV.n
                        stg_gate = stg_free_t[0]

                        def _rp_g(g, npass=npass, rp_gm=rp_gm, stg_gate=stg_gate):
                            b.sT.wait(g, stg_gate)
                            b.sS.wait(g, dbc_m)
                            g.dma_start(out=BSTG_B[:],
                                        in_=DBCC[32 + 8 * npass: 40 + 8 * npass, :])\
                                .then_inc(b.dPC.h, 16)
                            b.dPC.n += 16
                            g.dma_start(out=BSTG_C[:],
                                        in_=DBCC[48 + 8 * npass: 56 + 8 * npass, :])\
                                .then_inc(b.dPD.h, 16)
                            b.dPD.n += 16
                            rp_gm[0] = (b.dPC, b.dPC.n)
                            rp_gm[1] = (b.dPD, b.dPD.n)
                        b.g(_rp_g)

                        def _rp_t(t, npass=npass, rp_gm=rp_gm, sb8=sb8, vb8=vb8,
                                  rep_vcap=rep_vcap):
                            for u in range(16):
                                ni, bc = divmod(u, 2)
                                stg = BSTG_B if bc == 0 else BSTG_C
                                rp_gm[bc][0].wait(t, rp_gm[bc][1])
                                if u >= 2:
                                    if (u - 2) % 2 == 0:
                                        b.sS.wait(t, sb8 + (u - 2) // 2 + 1)
                                    else:
                                        b.sV.wait(t, vb8 + (u - 2) // 2 + 1)
                                elif npass == 0:
                                    b.sV.wait(t, rep_vcap)
                                else:
                                    b.sS.wait(t, rp_done[0])
                                    b.sV.wait(t, rp_done[1])
                                b.sT.inc(t.matmul(PSU[6 + u % 2][:, :], ONEROW[:],
                                                  stg[0:1, ni * QC:(ni + 1) * QC],
                                                  start=True, stop=True))
                        b.t(_rp_t)
                        stg_free_t[0] = b.sT.n

                        def _rp_s(s, tb8=tb8):
                            for u in range(0, 16, 2):
                                ni = u // 2
                                b.sT.wait(s, tb8 + u + 1)
                                if (7, ni) in ym_m:
                                    b.sV.wait(s, ym_m[(7, ni)])
                                b.sS.inc(s.copy(BREP[:, ni * QC:(ni + 1) * QC],
                                                PSU[6 + u % 2][:, :]))

                        def _rp_v(v, tb8=tb8):
                            for u in range(1, 16, 2):
                                ni = u // 2
                                b.sT.wait(v, tb8 + u + 1)
                                if (7, ni) in ym_m:
                                    b.sV.wait(v, ym_m[(7, ni)])
                                b.sV.inc(v.tensor_copy(CREP[:, ni * QC:(ni + 1) * QC],
                                                       PSU[6 + u % 2][:, :]))
                        b.s(_rp_s)
                        b.v(_rp_v)
                        rp_done = (b.sS.n, b.sV.n)
                        rp_sb = sb8

                        for dt in range(KT):
                            tb_cap, sb_cap, vb_cap = b.sT.n, b.sS.n, b.sV.n
                            if npass == 0 and dt + 2 < KT:
                                emit_dtmm(dt + 2)
                            tb_y = b.sT.n + (1 if npass == 1 else 0)
                            has_sp = (npass == 0 and dt + 2 < KT)
                            has_pv = (npass == 0 and dt + 1 < KT)
                            if has_pv:
                                dtv1_m[dt + 1] = vb_cap + 1
                                dtv2_m[dt + 1] = vb_cap + 2
                            voff = vb_cap + (3 if has_pv else 0)
                            soff = 0
                            if npass == 1:
                                zt_m[dt] = tb_cap + 1
                            for ni in range(8):
                                tts_m[(dt, ni)] = voff + 4 * ni + 2
                                ym_m[(dt, ni)] = voff + 4 * ni + 4
                                ymm_m[(dt, ni)] = tb_y + ni + 1
                                da_m[(dt, ni)] = (sb_cap + soff + ni + 1 if ni < 4
                                                  else sb_cap + soff +
                                                  (3 if has_sp else 0) + ni + 1)


                            def _da03(s, npass=npass, dt=dt, vb_cap=vb_cap):
                                for ni in range(4):
                                    n = npass * 8 + ni
                                    pdt = dt - 1 if dt >= 1 else (KT - 1)
                                    prev = tts_m.get((pdt, ni + 4))
                                    w = prev if prev is not None else vb_cap
                                    if npass == 0 and dt == 0:
                                        w = max(w, dtv2_m[0])
                                    b.sV.wait(s, w)
                                    acol = ACOL[:, dt * 16 + n: dt * 16 + n + 1]
                                    b.sS.inc(s.activation(DAP[ni % 4][:],
                                                          DT_CH[:, dt * QC:(dt + 1) * QC],
                                                          AF.Exp, scale=acol))
                            b.s(_da03)
                            if has_sp:
                                emit_softplus(dt + 2)

                            def _da47(s, npass=npass, dt=dt):
                                for ni in range(4, 8):
                                    n = npass * 8 + ni
                                    b.sV.wait(s, tts_m[(dt, ni - 4)])
                                    acol = ACOL[:, dt * 16 + n: dt * 16 + n + 1]
                                    b.sS.inc(s.activation(DAP[ni % 4][:],
                                                          DT_CH[:, dt * QC:(dt + 1) * QC],
                                                          AF.Exp, scale=acol))
                            b.s(_da47)
                            if npass == 1:
                                def _zsl(s, dt=dt, vb_cap=vb_cap):
                                    b.sT.wait(s, zt_m[dt])
                                    if dt >= 1:
                                        b.sV.wait(s, gate3_m[dt - 1])
                                    else:
                                        b.sV.wait(s, vb_cap)
                                    b.sS.inc(s.activation(ZT[dt % 2][:],
                                                          PSU[2 + dt % 2][:, :], AF.Silu))
                                    zsl_m[dt] = b.sS.n
                                b.s(_zsl)

                            def _blk_t(t, npass=npass, dt=dt, c0=c0, vb_cap=vb_cap):
                                if npass == 1:
                                    wait_marks(t, zw_m[dt])
                                    if dt >= 2:
                                        b.sS.wait(t, zsl_m[dt - 2])
                                    else:
                                        b.sV.wait(t, dtv1_m[6 + dt])
                                        b.sS.wait(t, abs_m[6 + dt])
                                    b.sT.inc(mm_group(t, PSU[2 + dt % 2][:, :],
                                             [(WINZ[dt % 2][:, kt * 128:(kt + 1) * 128],
                                               mv_h(kt, c0, QC)) for kt in range(KT)]))
                                    assert zt_m[dt] == b.sT.n
                                ybank = PSU[dt % 2]
                                for ni in range(8):
                                    if ni == 0:
                                        if npass == 0 and dt >= 2:
                                            b.sS.wait(t, yev_m[dt - 2])
                                        elif npass == 1 and dt < 2:
                                            b.sS.wait(t, yev_m[dt + 6])
                                    b.sV.wait(t, ym_m[(dt, ni)])
                                    b.sT.inc(t.matmul(ybank[:, :], EYEB[:], DBXP[ni % 2][:],
                                                      start=(ni == 0), stop=(ni == 7)))
                            b.t(_blk_t)

                            if npass == 1 and dt + 2 < KT:
                                def _zwn(g, dt=dt, dr=dr):
                                    b.sT.wait(g, zt_m[dt])
                                    zw_m[dt + 2] = [ydma(g, WINZ[dt % 2][:],
                                                         hbm_ap(WSC[dr],
                                                                1048576 + (dt + 2) * 131072,
                                                                [[1024, 128], [128, 8], [1, 128]]))]
                                b.y(_zwn)

                            if has_pv:
                                emit_dtphv(dt + 1)

                            def _blk_v(v, npass=npass, dt=dt, tb_cap=tb_cap, rp_sb=rp_sb):
                                for ni in range(8):
                                    n = npass * 8 + ni
                                    b.sS.wait(v, max(da_m[(dt, ni)], rp_sb + ni + 1))
                                    if ni < 2:
                                        b.sT.wait(v, tb_cap)
                                    else:
                                        b.sT.wait(v, ymm_m[(dt, ni - 2)])
                                    b.sV.inc(v.tensor_mul(DBXP[ni % 2][:],
                                                          U_CH[:, dt * QC:(dt + 1) * QC],
                                                          BREP[:, ni * QC:(ni + 1) * QC]))
                                    b.sV.wait(v)
                                    b.sV.inc(v.tensor_tensor_scan(HBF[ni % 2][:], DAP[ni % 4][:],
                                                                  DBXP[ni % 2][:],
                                                                  CARRY[:, dt * 16 + n: dt * 16 + n + 1],
                                                                  ALU.mult, ALU.add))
                                    b.sV.wait(v)
                                    b.sV.inc(v.tensor_copy(CARRY[:, dt * 16 + n: dt * 16 + n + 1],
                                                           HBF[ni % 2][:, QC - 1:QC]))
                                    b.sV.inc(v.tensor_mul(DBXP[ni % 2][:], HBF[ni % 2][:],
                                                          CREP[:, ni * QC:(ni + 1) * QC]))
                            b.v(_blk_v)
                            assert b.sV.n == voff + 32, (b.sV.n, voff)

                            if npass == 0:
                                def _yev_s(s, dt=dt):
                                    b.sT.wait(s)
                                    b.sS.inc(s.copy(YSB[:, dt * QC:(dt + 1) * QC],
                                                    PSU[dt % 2][:, :]))
                                    yev_m[dt] = b.sS.n
                                b.s(_yev_s)
                            else:
                                def _gz_g(g, dr=dr, dt=dt):
                                    b.sT.wait(g)
                                    _gz_g.m2 = [ydma(g, WOUTS[dt % 2][:],
                                                     hbm_ap(WSC[dr], 2097152 + dt * 65536,
                                                            [[512, 128], [1, 512]]))]
                                b.y(_gz_g)
                                owm = _gz_g.m2

                                def _g_v(v, dr=dr, dt=dt):
                                    ybank = PSU[dt % 2]
                                    b.sS.wait(v)
                                    b.sT.wait(v)
                                    dpcol = MBC[:, 2 * KT + dt: 2 * KT + dt + 1]
                                    b.sV.inc(v.scalar_tensor_tensor(
                                        SCR4[:], XS_CH[:, dt * QC:(dt + 1) * QC],
                                        dpcol, ybank[:, :], ALU.mult, ALU.add))
                                    gate_m[dt] = b.sV.n
                                    b.sV.wait(v)
                                    b.sV.inc(v.tensor_add(SCR4[:], SCR4[:],
                                                          YSB[:, dt * QC:(dt + 1) * QC]))
                                    b.sV.wait(v)
                                    b.sV.inc(v.tensor_mul(DBXP[0][:], SCR4[:],
                                                          ZT[dt % 2][:]))
                                    gate3_m[dt] = b.sV.n
                                b.v(_g_v)

                                def _wo_t(t, dt=dt, owm=owm):
                                    wait_marks(t, owm)
                                    b.sV.wait(t)
                                    if dt == 0:
                                        b.sS.wait(t)
                                    for ot in range(4):
                                        b.sT.inc(t.matmul(PSU[4 + ot][:, :],
                                                          WOUTS[dt % 2][:, ot * 128:(ot + 1) * 128],
                                                          DBXP[0][:], start=(dt == 0), stop=(dt == KT - 1)))
                                b.t(_wo_t)

                    for ot in range(4):
                        odt = dr * 4 + ot

                        def _ep_v(v, ot=ot, odt=odt, c0=c0):
                            b.sT.wait(v)
                            if ot >= 1:
                                b.sS.wait(v)
                            b.sV.inc(v.tensor_mul(SCR4[:], Xt(odt)[:, c0:c0 + QC],
                                                  REP[:, c0:c0 + QC]))
                            b.sV.wait(v)
                            b.sV.inc(v.tensor_add(SCR4[:], SCR4[:], REP[:, T + c0: T + c0 + QC]))
                            b.sV.wait(v)
                            b.sV.inc(v.tensor_add(XCC[:], SCR4[:], PSU[4 + ot][:, :]))
                        b.v(_ep_v)

                        def _ep_s(s):
                            b.sV.wait(s)
                            b.sS.inc(s.activation(XCC[:], XCC[:], AF.Gelu))
                        b.s(_ep_s)

                        def _ep_v2(v, ot=ot, odt=odt, c0=c0):
                            b.sS.wait(v)
                            b.sV.inc(v.tensor_add(Xt(odt)[:, c0:c0 + QC], XCC[:],
                                                  Xt(odt)[:, c0:c0 + QC]))
                        b.v(_ep_v2)
            stg_free_t = [b.sT.n]
            s1_free = [b.sT.n]
            s1_vlast = [0]
            emit_dir(0, "m1")
            emit_dir(1, "m2")
            p3b.close()
            p3.close()
            tap("x3", [Xt(dt) for dt in range(KT)], KT * 128 * T, F32)

            scr_hbm = nc.dram_tensor("scr_rows", [4096], F32)
            p4 = contextlib.ExitStack()
            PSU = [p4.enter_context(nc.psum_tensor(f"P4U{i}", [128, 512], F32)) for i in range(4)]
            PTF = [p4.enter_context(nc.psum_tensor(f"PTF{i}", [128, 128], F32)) for i in range(2)]
            QLFM = p4.enter_context(nc.sbuf_tensor("QLFM", [128, KT * T], BF16))
            QRFM = p4.enter_context(nc.sbuf_tensor("QRFM", [128, KT * 1024], BF16))
            XLSC = p4.enter_context(nc.sbuf_tensor("XLSC", [128, T], F32))
            XLSC2 = p4.enter_context(nc.sbuf_tensor("XLSC2", [128, T], F32))
            RHSK = p4.enter_context(nc.sbuf_tensor("RHSK", [128, 1024], F32))
            RTK = p4.enter_context(nc.sbuf_tensor("RTK", [128, 1024], F32))
            AMX = p4.enter_context(nc.sbuf_tensor("AMX", [128, 16], F32))
            AMT = p4.enter_context(nc.sbuf_tensor("AMT", [128, 1], F32))
            AMT2 = p4.enter_context(nc.sbuf_tensor("AMT2", [128, 1], F32))
            CMXJ = p4.enter_context(nc.sbuf_tensor("CMXJ", [128, 8], F32))
            RSJ = p4.enter_context(nc.sbuf_tensor("RSJ", [128, 8], F32))
            RLROW = p4.enter_context(nc.sbuf_tensor("RLROW", [1, 4096], F32))
            QREP = p4.enter_context(nc.sbuf_tensor("QREP", [128, T], F32))
            STB4 = [p4.enter_context(nc.sbuf_tensor(f"STB4{i}", [128, QC], BF16)) for i in range(4)]

            ln_stats_resident(Xt, PSU[0], PSU[1], STB4)
            ln_statmath_and_reps(PSU[2])



            rr_tm, rr_sm, rr_vm, rr_wm = {}, {}, {}, {}
            tb11, sb11, vb11 = b.sT.n, b.sS.n, b.sV.n

            rr_snap = b.snap()

            def _rr_g(g, rr_wm=rr_wm, tb11=tb11):
                for kt in range(KT):
                    hsem = b.dPA if kt % 2 == 0 else b.dPB
                    if kt == 0:
                        b.wait_all(g, rr_snap)
                    else:
                        b.sT.wait(g, tb11 + 8 * kt)
                    g.dma_start(out=RHSK[:], in_=hbm_ap(W["L1_ffn_rhs"], kt * 128 * 1024,
                                                        [[1024, 128], [1, 1], [1, 1024]]))\
                        .then_inc(hsem.h, 16)
                    hsem.n += 16
                    rr_wm[kt] = (hsem, hsem.n)

            def _rr_t(t, rr_wm=rr_wm, sb11=sb11):
                for kt in range(KT):
                    rr_wm[kt][0].wait(t, rr_wm[kt][1])
                    for jt in range(KT):
                        if kt * 8 + jt >= 2:
                            b.sS.wait(t, sb11 + kt * 8 + jt - 1)
                        b.sT.inc(t.transpose(PSU[2 + (jt % 2)][:, 0:128],
                                             RHSK[:, jt * 128:(jt + 1) * 128], EYEF[:]))

            def _rr_s(s, tb11=tb11, vb11=vb11):
                for kt in range(KT):
                    for jt in range(KT):
                        b.sT.wait(s, tb11 + kt * 8 + jt + 1)
                        if kt >= 1:
                            b.sV.wait(s, vb11 + 1 + 16 * (kt - 1) + 2 * (jt + 1))
                        b.sS.inc(s.copy(RTK[:, jt * 128:(jt + 1) * 128],
                                        PSU[2 + (jt % 2)][:, 0:128]))

            def _rr_v(v, sb11=sb11):
                b.sV.inc(v.memset(CMXJ[:], 0.0))
                for kt in range(KT):
                    for jt in range(KT):
                        b.sS.wait(v, sb11 + kt * 8 + jt + 1)
                        b.sV.wait(v)
                        b.sV.inc(v.reduce_max(AMT[:], RTK[:, jt * 128:(jt + 1) * 128],
                                              axis=AX, apply_absolute_value=True))
                        b.sV.wait(v)
                        b.sV.inc(v.tensor_max(CMXJ[:, jt:jt + 1], CMXJ[:, jt:jt + 1], AMT[:]))
            b.g(_rr_g)
            b.t(_rr_t)
            b.s(_rr_s)
            b.v(_rr_v)

            def _rs_v(v):
                b.sV.wait(v)
                b.sV.inc(v.tensor_scalar(RSJ[:], CMXJ[:], C_TINY, 1.0 / 127.0,
                                         ALU.max, ALU.mult))
                b.sV.wait(v)
                b.sV.inc(v.reciprocal(CMXJ[:], RSJ[:]))
            b.v(_rs_v)

            r2_tm, r2_sm, r2_vm, r2_wm = {}, {}, {}, {}
            tb12, sb12, vb12 = b.sT.n, b.sS.n, b.sV.n

            def _r2_g(g, r2_wm=r2_wm, tb12=tb12):
                for kt in range(KT):
                    hsem = b.dPC if kt % 2 == 0 else b.dPD
                    if kt == 0:
                        b.sT.wait(g)
                    else:
                        b.sT.wait(g, tb12 + 16 * (kt - 1) + 8)
                    g.dma_start(out=RHSK[:], in_=hbm_ap(W["L1_ffn_rhs"], kt * 128 * 1024,
                                                        [[1024, 128], [1, 1], [1, 1024]]))\
                        .then_inc(hsem.h, 16)
                    hsem.n += 16
                    r2_wm[kt] = (hsem, hsem.n)

            def _r2_t(t, r2_wm=r2_wm, sb12=sb12, vb12=vb12):
                for kt in range(KT):
                    r2_wm[kt][0].wait(t, r2_wm[kt][1])
                    for jt in range(KT):
                        if jt >= 2:
                            b.sS.wait(t, sb12 + 16 * kt + jt - 1)
                        elif kt >= 1:
                            b.sS.wait(t, sb12 + 16 * kt)
                        else:
                            b.sS.wait(t, sb12)
                        b.sT.inc(t.transpose(PSU[2 + (jt % 2)][:, 0:128],
                                             RHSK[:, jt * 128:(jt + 1) * 128], EYEF[:]))
                    b.sV.wait(t, vb12 + 10 * (kt + 1))
                    for jt in range(KT):
                        if jt >= 2:
                            b.sS.wait(t, sb12 + 16 * kt + 8 + jt - 1)
                        else:
                            b.sS.wait(t, sb12 + 16 * kt + 8)
                        b.sT.inc(t.transpose(PSU[2 + (jt % 2)][:, 0:128],
                                             RTK[:, jt * 128:(jt + 1) * 128], EYEF[:]))

            def _r2_s(s, tb12=tb12, vb12=vb12):
                for kt in range(KT):
                    for jt in range(KT):
                        b.sT.wait(s, tb12 + kt * 16 + jt + 1)
                        if kt == 0 and jt == 0:
                            b.sV.wait(s, vb12)
                        b.sS.inc(s.copy(RTK[:, jt * 128:(jt + 1) * 128],
                                        PSU[2 + (jt % 2)][:, 0:128]))
                    for jt in range(KT):
                        b.sT.wait(s, tb12 + kt * 16 + 8 + jt + 1)
                        b.sS.inc(s.copy(QRFM[:, kt * 1024 + jt * 128: kt * 1024 + (jt + 1) * 128],
                                        PSU[2 + (jt % 2)][:, 0:128]))

            def _r2_v(v, sb12=sb12):
                for kt in range(KT):
                    b.sS.wait(v, sb12 + 16 * kt + 8)
                    for jt in range(KT):
                        if kt > 0 or jt > 0:
                            b.sV.wait(v)
                        b.sV.inc(v.tensor_scalar(RTK[:, jt * 128:(jt + 1) * 128],
                                                 RTK[:, jt * 128:(jt + 1) * 128],
                                                 CMXJ[:, jt:jt + 1], None, ALU.mult))
                    b.sV.wait(v)
                    b.sV.inc(v.tensor_scalar(RTK[:], RTK[:], MAGIC, MAGIC, ALU.add, ALU.subtract))
                    b.sV.wait(v)
                    b.sV.inc(v.tensor_scalar(RTK[:], RTK[:], 127.0, -127.0, ALU.min, ALU.max))
            b.v(_r2_v)
            b.g(_r2_g)
            b.t(_r2_t)
            b.s(_r2_s)

            am_vm, am_tm, am_sm = {}, {}, {}
            am_snap = b.snap()
            tb10, sb10, vb10 = b.sT.n, b.sS.n, b.sV.n

            def _am_v(v, am_vm=am_vm):
                b.wait_all(v, am_snap)
                b.sV.inc(v.memset(AMX[:], 0.0))
                for dt in range(KT):
                    if dt >= 1:
                        b.sT.wait(v, tb10 + 16 * dt)
                    b.sV.inc(v.tensor_mul(XLSC[:], Xt(dt), REP[:, 0:T]))
                    b.sV.wait(v)
                    b.sV.inc(v.tensor_add(XLSC[:], XLSC[:], REP[:, T:2 * T]))
                    am_vm[dt] = b.sV.n
                    for tt in range(16):
                        b.sS.wait(v, sb10 + dt * 16 + tt + 1)
                        b.sV.wait(v)
                        b.sV.inc(v.reduce_max(AMT2[:], XLSC2[:, tt * 128:(tt + 1) * 128],
                                              axis=AX, apply_absolute_value=True))
                        b.sV.wait(v)
                        b.sV.inc(v.tensor_max(AMX[:, tt:tt + 1], AMX[:, tt:tt + 1], AMT2[:]))
            def _am_t(t, am_vm=am_vm, am_sm=am_sm):
                for dt in range(KT):
                    b.sV.wait(t, am_vm[dt])
                    for tt in range(16):
                        if dt * 16 + tt >= 2:
                            b.sS.wait(t, sb10 + dt * 16 + tt - 1)
                        b.sT.inc(t.transpose(PSU[tt % 2][:, 0:128],
                                             XLSC[:, tt * 128:(tt + 1) * 128], EYEF[:]))

            def _am_s(s, vb10=vb10):
                for dt in range(KT):
                    for tt in range(16):
                        b.sT.wait(s, tb10 + dt * 16 + tt + 1)
                        if dt >= 1:
                            b.sV.wait(s, vb10 + 1 + 34 * (dt - 1) + 2 + 2 * (tt + 1))
                        b.sS.inc(s.copy(XLSC2[:, tt * 128:(tt + 1) * 128],
                                        PSU[tt % 2][:, 0:128]))
            for dt in range(KT):
                for tt in range(16):
                    am_sm[dt * 16 + tt] = sb10 + dt * 16 + tt + 1
            b.v(_am_v)
            b.t(_am_t)
            b.s(_am_s)

            def _rl_v(v):
                b.sV.wait(v)
                b.sS.wait(v)
                b.sV.inc(v.tensor_scalar(XLSC[:, 0:16], AMX[:], C_TINY, 1.0 / 127.0,
                                         ALU.max, ALU.mult))
                b.sV.wait(v)
                b.sV.inc(v.reciprocal(XLSC[:, 16:32], XLSC[:, 0:16]))
            b.v(_rl_v)

            def _rl_y(y):
                b.sV.wait(y)
                y.dma_start(out=hbm_ap(scr_hbm, 0, [[1, 128], [128, 16]]),
                            in_=XLSC[:, 0:16], allow_slow_non_contiguous=True).then_inc(b.dX.h, 16)
                y.dma_start(out=hbm_ap(scr_hbm, 2048, [[1, 128], [128, 16]]),
                            in_=XLSC[:, 16:32], allow_slow_non_contiguous=True).then_inc(b.dX.h, 16)
                b.dX.n += 32
                b.dX.wait(y)
                y.dma_start(out=RLROW[:], in_=hbm_ap(scr_hbm, 0, [[4096, 1], [1, 4096]]))\
                    .then_inc(b.dX.h, 16)
                b.dX.n += 16
            b.y(_rl_y)

            def qrep_mm(t, i, bank):
                if i == 0:
                    b.dX.wait(t)
                    b.sS.wait(t)
                return mm_group(t, bank[:, :],
                                [(ONESF[:], RLROW[0:1, 2048 + i * QC: 2048 + (i + 1) * QC])])

            def qrep_ev(s, i, bank):
                return s.copy(QREP[:, i * QC:(i + 1) * QC], bank[:, :])
            mm_evac(NQ, qrep_mm, qrep_ev, [PSU[0], PSU[1]], "s")

            qv_m = {}

            def _ql_v(v, qv_m=qv_m):
                b.sS.wait(v)
                b.sT.wait(v)
                for dt in range(KT):
                    if dt >= 1:
                        b.sV.wait(v)
                    b.sV.inc(v.tensor_mul(XLSC[:], Xt(dt), REP[:, 0:T]))
                    b.sV.wait(v)
                    b.sV.inc(v.tensor_add(XLSC[:], XLSC[:], REP[:, T:2 * T]))
                    b.sV.wait(v)
                    b.sV.inc(v.tensor_mul(XLSC[:], XLSC[:], QREP[:]))
                    b.sV.wait(v)
                    b.sV.inc(v.tensor_scalar(XLSC[:], XLSC[:], MAGIC, MAGIC, ALU.add, ALU.subtract))
                    b.sV.wait(v)
                    b.sV.inc(v.tensor_scalar(QLFM[:, dt * T:(dt + 1) * T], XLSC[:],
                                             127.0, -127.0, ALU.min, ALU.max))
                    qv_m[dt] = b.sV.n
            b.v(_ql_v)
            tap("qlfm", [QLFM[:, dt * T:(dt + 1) * T] for dt in range(KT)], KT * 128 * T, BF16)

            lrep_done = None

            def lrep_mm(t, i, bank):
                if i == 0:
                    b.sV.wait(t)
                    b.sS.wait(t)
                return mm_group(t, bank[:, :],
                                [(ONESF[:], RLROW[0:1, i * QC:(i + 1) * QC])])

            def lrep_ev(s, i, bank):
                return s.copy(REP[:, i * QC:(i + 1) * QC], bank[:, :])
            mm_evac(NQ, lrep_mm, lrep_ev, [PSU[0], PSU[1]], "s")

            def ff_mm(t, i, bank):
                jt, qc = divmod(i, NQ)
                if i == 0:
                    b.sS.wait(t)
                    b.sV.wait(t)
                return mm_group(t, bank[:, :],
                                [(QRFM[:, kt * 1024 + jt * 128: kt * 1024 + (jt + 1) * 128],
                                  QLFM[:, kt * T + qc * QC: kt * T + (qc + 1) * QC])
                                 for kt in range(KT)])

            def ff_ev(v, i, bank):
                jt, qc = divmod(i, NQ)
                if i >= 1:
                    b.sV.wait(v)
                b.sV.inc(v.scalar_tensor_tensor(XLSC[:, 0:QC], bank[:, :],
                                                RSJ[:, jt:jt + 1],
                                                REP[:, qc * QC:(qc + 1) * QC],
                                                ALU.mult, ALU.mult))
                b.sV.wait(v)
                return v.tensor_add(Xt(jt)[:, qc * QC:(qc + 1) * QC], XLSC[:, 0:QC],
                                    Xt(jt)[:, qc * QC:(qc + 1) * QC])
            vb_ff = b.sV.n
            mm_evac(KT * NQ, ff_mm, ff_ev, [PSU[0], PSU[1]], "v", evac_incs=2)
            ffev_m = {jt: vb_ff + 2 * (4 * jt + 4) for jt in range(KT)}
            tap("x4", [Xt(dt) for dt in range(KT)], KT * 128 * T, F32)

            XTMO = QLFM[:, :].bitcast(F32)
            tbf, sbf = b.sT.n, b.sS.n
            dxf = b.dX.n

            def _pf_t(t):
                b.sS.wait(t, sbf)
                for gi in range(128):
                    half, r = divmod(gi, 64)
                    dt, u = divmod(r, 8)
                    tt = half * 8 + u
                    if u == 0:
                        b.sV.wait(t, ffev_m[dt])
                    if gi >= 2:
                        b.sS.wait(t, sbf + gi - 1)
                    b.sT.inc(t.transpose(PTF[gi % 2][:],
                                         X[:, dt * T + tt * 128: dt * T + (tt + 1) * 128], EYEF[:]))
            b.t(_pf_t)

            def _pf_s(s):
                for gi in range(128):
                    half, r = divmod(gi, 64)
                    dt, u = divmod(r, 8)
                    if gi == 64:
                        b.dX.wait(s, dxf + 16)
                    b.sT.wait(s, tbf + gi + 1)
                    b.sS.inc(s.copy(XTMO[:, u * D + dt * 128: u * D + (dt + 1) * 128],
                                    PTF[gi % 2][:]))
            b.s(_pf_s)

            def _pf_y(y):
                for half in range(2):
                    b.sS.wait(y, sbf + (half + 1) * 64)
                    y.dma_start(out=hbm_ap(out_hbm, half * 8 * 128 * D,
                                           [[D, 128], [128 * D, 8], [1, D]]),
                                in_=XTMO[:]).then_inc(b.dX.h, 16)
                    b.dX.n += 16
                b.dX.wait(y)
            b.y(_pf_y)
            p4.close()

    return nc, dbg_hbm


def kernel(**inputs):
    import ml_dtypes
    xq = np.ascontiguousarray(np.asarray(inputs["L0_attn_query"], dtype=np.float32))
    base = {k: np.ascontiguousarray(np.asarray(v, dtype=np.float32))
            for k, v in inputs.items()
            if k not in ("L0_attn_query", "L0_attn_key_in", "L0_attn_value", "L0_ffn_t_emb")}
    base["eye_f"] = np.eye(128, dtype=np.float32)
    base["eye_b"] = np.eye(128, dtype=ml_dtypes.bfloat16)
    nc, _ = build_program(DBG)
    in_maps = []
    for c in range(8):
        m = dict(base)
        m["x_in"] = xq[c]
        in_maps.append(m)
    res = run_bass_kernel_spmd(nc, in_maps, core_ids=list(range(8)))
    return np.stack([res.results[c]["out"] for c in range(8)])
